# version 10
# baseline (speedup 1.0000x reference)
import os
import numpy as np
import ml_dtypes
from contextlib import ExitStack

import concourse.bass as bass
import concourse.mybir as mybir
from concourse.bass_utils import run_bass_kernel_spmd

F32 = mybir.dt.float32
BF16 = mybir.dt.bfloat16
I32 = mybir.dt.int32
AF = mybir.ActivationFunctionType
ALU = mybir.AluOpType
AX = mybir.AxisListType

D = 1024
F = 4096
NCORES = 8
EPS = 1e-5
C_GRIF = 8.0

ENGINES = ("sync", "tensor", "vector", "scalar", "gpsimd")


def sview(t, off, stride, count, rowlen, parts=128):
    return bass.AP(t, off, [[rowlen, parts], [stride, count]])


class Sched:

    def __init__(self, nc, semnames):
        self.nc = nc
        self.sems = {}
        self.count = {n: 0 for n in semnames}
        self.semnames = semnames
        self.q = {e: [] for e in ENGINES}

    def make_sems(self, stack):
        for n in self.semnames:
            self.sems[n] = stack.enter_context(self.nc.semaphore(n))

    OWN = {"vector": "sV", "scalar": "sS", "gpsimd": "sG"}

    def emit(self, eng, fn, inc=None, waits=()):
        waits = [w for w in waits if w is not None]
        for s, v in waits:
            if v > self.count[s]:
                raise AssertionError(f"wait on future {s}: {v} > {self.count[s]}")
        own = self.OWN.get(eng)
        if own is not None:
            waits = [(own, self.count[own])] + waits
        self.q[eng].append((tuple(waits), fn, inc))
        if inc is not None:
            self.count[inc[0]] += inc[1]
        return (inc[0], self.count[inc[0]]) if inc is not None else None

    def tok(self, sem):
        return (sem, self.count[sem])

    def copy(self, eng, dst, src, waits=()):
        if eng == "scalar":
            return self.emit("scalar", lambda s, dst=dst, src=src: s.copy(dst, src),
                             inc=("sS", 1), waits=waits)
        return self.emit("vector", lambda v, dst=dst, src=src: v.tensor_copy(dst, src),
                         inc=("sV", 1), waits=waits)

    def flush(self, block):
        for eng in ENGINES:
            ops = self.q[eng]
            if not ops:
                continue

            def body(engine, ops=ops):
                lastw = {}
                for waits, fn, inc in ops:
                    for s, v in waits:
                        if lastw.get(s, -1) >= v or v <= 0:
                            continue
                        engine.wait_ge(self.sems[s], v)
                        lastw[s] = v
                    if fn is not None:
                        ins = fn(engine)
                        if inc is not None:
                            ins.then_inc(self.sems[inc[0]], inc[1])

            getattr(block, eng)(body)
        self.q = {e: [] for e in ENGINES}


def build_program(B_core=2, S_seq=4096):
    NTOK = B_core * S_seq
    T1 = 256
    NST1 = NTOK // T1
    NST2_SEQ = S_seq // 512
    GPERSEQ = S_seq // 128

    nc = bass.Bass("TRN2", target_bir_lowering=False)

    x_in = nc.dram_tensor("L0_attn_x", [B_core, S_seq, D], F32, kind="ExternalInput").ap()
    wa_in = nc.dram_tensor("L0_attn_Wa", [D, D], F32, kind="ExternalInput").ap()
    wx_in = nc.dram_tensor("L0_attn_Wx", [D, D], F32, kind="ExternalInput").ap()
    ba_in = nc.dram_tensor("L0_attn_ba", [D], F32, kind="ExternalInput").ap()
    bx_in = nc.dram_tensor("L0_attn_bx", [D], F32, kind="ExternalInput").ap()
    lam_in = nc.dram_tensor("L0_attn_Lam", [D], F32, kind="ExternalInput").ap()
    w1_in = nc.dram_tensor("L0_ffn_w1", [F, D], F32, kind="ExternalInput").ap()
    b1_in = nc.dram_tensor("L0_ffn_b1", [F], F32, kind="ExternalInput").ap()
    w2_in = nc.dram_tensor("L0_ffn_w2", [D, F], F32, kind="ExternalInput").ap()
    b2_in = nc.dram_tensor("L0_ffn_b2", [D], F32, kind="ExternalInput").ap()
    id_f_in = nc.dram_tensor("ident_f32", [128, 128], F32, kind="ExternalInput").ap()
    id_b_in = nc.dram_tensor("ident_bf16", [128, 128], BF16, kind="ExternalInput").ap()
    y_out = nc.dram_tensor("y", [B_core, S_seq, D], F32, kind="ExternalOutput").ap()

    wq1T = nc.dram_tensor("wq1T", [D, F], BF16, kind="Internal").ap()
    wq2T = nc.dram_tensor("wq2T", [F, D], BF16, kind="Internal").ap()
    wqaT = nc.dram_tensor("wqaT", [D, D], BF16, kind="Internal").ap()
    wqxT = nc.dram_tensor("wqxT", [D, D], BF16, kind="Internal").ap()
    xs1 = nc.dram_tensor("xs1", [B_core, S_seq, D], F32, kind="Internal").ap()
    xs2 = nc.dram_tensor("xs2", [B_core, S_seq, D], F32, kind="Internal").ap()

    sch = Sched(nc, ["sC", "sW", "sLA", "sLB", "sOA", "sOB", "sO", "sT", "sV", "sS", "sG"])
    E = sch.emit

    def gtile(g):
        return g // GPERSEQ, (g % GPERSEQ) * 128

    with ExitStack() as top:
        sch.make_sems(top)
        sb = lambda name, shape, dt=F32: top.enter_context(nc.sbuf_tensor(name, shape, dt))

        ident_f = sb("ident_f", [128, 128], F32)
        ident_b = sb("ident_b", [128, 128], BF16)
        ones_row = sb("ones_row", [1, 128], F32)
        onescol = sb("onescol", [128, 1], F32)
        c256 = sb("c256", [128, 1], F32)
        c16 = sb("c16", [128, 1], F32)
        coef_fm = sb("coef_fm", [128, 8], F32)
        ba_fm = sb("ba_fm", [128, 8], F32)
        bx_fm = sb("bx_fm", [128, 8], F32)
        carry = sb("carry", [128, 8], F32)
        b1c = sb("b1c", [128, F], BF16)
        b2h = sb("b2h", [128, D], BF16)
        wM1 = sb("wM1", [128, 1], F32)
        wM2h = sb("wM2h", [128, 1], F32)
        sw1b = sb("sw1b", [128, 1], F32)
        sw2b = sb("sw2b", [128, 1], F32)
        NC_ = 21
        colp = sb("colp", [128, 3 * NC_], F32)

        with ExitStack() as ph:
            psb = lambda name, shape, dt=F32: ph.enter_context(nc.sbuf_tensor(name, shape, dt))
            wld = [psb("p0_wld0", [128, F]), psb("p0_wld1", [128, F])]
            zw = psb("p0_zw", [128, F])
            qw = psb("p0_qw", [128, F], I32)
            tern = psb("p0_tern", [128, F], BF16)
            twbf = psb("p0_twbf", [128, F], BF16)
            acc32 = psb("p0_acc", [128, 64])
            acc8 = psb("p0_acc8", [128, 32])
            corrw = psb("p0_corrw", [128, F], BF16)
            corrw2 = psb("p0_corrw2", [128, F], BF16)
            stg = [psb("p0_stg0", [128, 128], BF16), psb("p0_stg1", [128, 128], BF16)]
            rowbuf = psb("p0_row", [1, F])
            vec8 = psb("p0_vec8", [8, 128])
            cell = psb("p0_cell", [1, 4])
            pst = [ph.enter_context(nc.psum_tensor("p0_pst0", [128, 1024], BF16)),
                   ph.enter_context(nc.psum_tensor("p0_pst1", [128, 1024], BF16))]
            ps_bc = ph.enter_context(nc.psum_tensor("p0_bc", [128, 512], F32))

            with nc.Block("phase0") as blk:
                E("vector", lambda v: v.memset(ones_row[:], 1.0), inc=("sV", 1))
                E("vector", lambda v: v.memset(onescol[:], 1.0), inc=("sV", 1))
                E("vector", lambda v: v.memset(c256[:], 256.0), inc=("sV", 1))
                E("vector", lambda v: v.memset(c16[:], 16.0), inc=("sV", 1))
                E("vector", lambda v: v.memset(carry[:], 0.0), inc=("sV", 1))
                tconst = sch.tok("sV")
                E("sync", lambda q: q.dma_start(ident_f[:], id_f_in), inc=("sC", 16))
                E("sync", lambda q: q.dma_start(ident_b[:], id_b_in), inc=("sC", 16))
                tid = sch.tok("sC")
                for vin, vdst in ((lam_in, coef_fm), (ba_in, ba_fm), (bx_in, bx_fm)):
                    E("sync", lambda q, vin=vin: q.dma_start(
                        vec8[:], vin.rearrange("(c p) -> c p", p=128)),
                      waits=[sch.tok("sT")], inc=("sC", 16))
                    E("tensor", lambda t: t.transpose(ps_bc[:, 0:8], vec8[:],
                                                      ident_f[0:8, 0:8]),
                      waits=[sch.tok("sC"), sch.tok("sV")], inc=("sT", 1))
                    E("vector", lambda v, vdst=vdst: v.tensor_copy(vdst[:], ps_bc[:, 0:8]),
                      waits=[sch.tok("sT")], inc=("sV", 1))
                E("sync", lambda q: q.dma_start(rowbuf[0:1, 0:F], b1_in[None, :]),
                  waits=[sch.tok("sT")], inc=("sC", 16))
                tq6 = sch.tok("sC")
                E("scalar", lambda s: s.activation(coef_fm[:], coef_fm[:], AF.Exp, scale=-1.0),
                  waits=[tq6, sch.tok("sV")], inc=("sS", 1))
                E("scalar", lambda s: s.activation(coef_fm[:], coef_fm[:], AF.Ln, bias=1.0),
                  inc=("sS", 1))
                E("vector", lambda v: v.tensor_single_scalar(coef_fm[:], coef_fm[:], -C_GRIF,
                                                             op=ALU.mult),
                  waits=[sch.tok("sS")], inc=("sV", 1))

                for s5 in range(F // 512):
                    sl = slice(s5 * 512, (s5 + 1) * 512)
                    E("tensor", lambda t, sl=sl: t.matmul(ps_bc[:], ones_row[:],
                                                          rowbuf[0:1, sl], start=True, stop=True),
                      waits=[tq6, tconst, sch.tok("sV")], inc=("sT", 1))
                    E("vector", lambda v, sl=sl: v.tensor_copy(b1c[:, sl], ps_bc[:]),
                      waits=[sch.tok("sT")], inc=("sV", 1))
                E("sync", lambda q: q.dma_start(rowbuf[0:1, 0:D], b2_in[None, :]),
                  waits=[sch.tok("sV")], inc=("sC", 16))
                tq7 = sch.tok("sC")
                for s5 in range(D // 512):
                    sl = slice(s5 * 512, (s5 + 1) * 512)
                    E("tensor", lambda t, sl=sl: t.matmul(ps_bc[:], ones_row[:],
                                                          rowbuf[0:1, sl], start=True, stop=True),
                      waits=[tq7, sch.tok("sS")], inc=("sT", 1))
                    E("scalar", lambda s, sl=sl: s.activation(b2h[:, sl], ps_bc[:],
                                                              AF.Copy, scale=0.5),
                      waits=[sch.tok("sT")], inc=("sS", 1))

                def mean_abs(w_ap, nrows, ncols, acc_base, out_sw, out_m, deq):
                    nch = nrows // 128
                    nsb = ncols // 128
                    for ci in range(nch):
                        wld_t = wld[ci % 2][:, 0:ncols]
                        sl_ = "sLA" if ci % 2 == 0 else "sLB"
                        w = [sch.tok("sS"), sch.tok("sV")]
                        E("sync", lambda q, ci=ci, wld_t=wld_t, w_ap=w_ap: q.dma_start(
                            wld_t, w_ap[ci * 128:(ci + 1) * 128, :]), waits=w, inc=(sl_, 16))
                        tq = sch.tok(sl_)
                        E("scalar", lambda s, wld_t=wld_t: s.activation(
                            zw[:, 0:ncols], wld_t, AF.Abs), waits=[tq], inc=("sS", 1))
                        tz = sch.tok("sS")
                        for sb in range(nsb):
                            E("vector", lambda v, sb=sb: v.reduce_sum(
                                acc8[:, sb:sb + 1], zw[:, sb * 128:(sb + 1) * 128],
                                axis=AX.X), waits=[tz] if sb == 0 else [], inc=("sV", 1))
                        E("vector", lambda v, ci=ci: v.reduce_sum(
                            acc32[:, acc_base + ci:acc_base + ci + 1], acc8[:, 0:nsb],
                            axis=AX.X), inc=("sV", 1))
                    E("vector", lambda v: v.reduce_sum(
                        colp[:, 0:1], acc32[:, acc_base:acc_base + nch], axis=AX.X),
                      waits=[sch.tok("sS")], inc=("sV", 1))
                    E("tensor", lambda t: t.matmul(ps_bc[0:1, 0:1], colp[:, 0:1], onescol[:],
                                                   start=True, stop=True),
                      waits=[sch.tok("sV")], inc=("sT", 1))
                    E("vector", lambda v: v.tensor_copy(cell[0:1, 0:1], ps_bc[0:1, 0:1]),
                      waits=[sch.tok("sT")], inc=("sV", 1))
                    E("vector", lambda v: v.tensor_single_scalar(
                        cell[0:1, 1:2], cell[0:1, 0:1], 1.0 / (nrows * ncols), op=ALU.mult),
                      inc=("sV", 1))
                    E("vector", lambda v: v.tensor_single_scalar(
                        cell[0:1, 1:2], cell[0:1, 1:2], EPS, op=ALU.max), inc=("sV", 1))
                    E("vector", lambda v: v.reciprocal(cell[0:1, 2:3], cell[0:1, 1:2]),
                      inc=("sV", 1))
                    tv2 = sch.tok("sV")
                    E("tensor", lambda t: t.matmul(ps_bc[:, 0:1], ones_row[:], cell[0:1, 2:3],
                                                   start=True, stop=True),
                      waits=[tv2], inc=("sT", 1))
                    E("vector", lambda v, out_sw=out_sw: v.tensor_copy(out_sw, ps_bc[:, 0:1]),
                      waits=[sch.tok("sT")], inc=("sV", 1))
                    E("tensor", lambda t: t.matmul(ps_bc[:, 0:1], ones_row[:], cell[0:1, 1:2],
                                                   start=True, stop=True),
                      waits=[sch.tok("sV")], inc=("sT", 1))
                    E("scalar", lambda s, out_m=out_m, deq=deq: s.activation(
                        out_m, ps_bc[:, 0:1], AF.Copy, scale=deq),
                      waits=[sch.tok("sT")], inc=("sS", 1))

                mean_abs(w1_in, F, D, 0, sw1b[:], wM1[:], 1.0 / 127.0)
                mean_abs(w2_in, D, F, 40, sw2b[:], wM2h[:], 0.5 / 127.0)

                def quant_transpose(w_ap, nrows, ncols, swcol, out_T, do_quant, pctr=[0]):
                    nch = nrows // 128
                    npc = ncols // 128
                    for ci in range(nch):
                        wld_t = wld[ci % 2][:, 0:ncols]
                        sl_ = "sLA" if ci % 2 == 0 else "sLB"
                        w = [sch.tok("sT"), sch.tok("sS"), sch.tok("sV")]
                        E("sync", lambda q, ci=ci, wld_t=wld_t, w_ap=w_ap: q.dma_start(
                            wld_t, w_ap[ci * 128:(ci + 1) * 128, :]), waits=w, inc=(sl_, 16))
                        tq = sch.tok(sl_)
                        if do_quant:
                            E("scalar", lambda s, wld_t=wld_t, swcol=swcol: s.activation(
                                zw[:, 0:ncols], wld_t, AF.Identity, scale=swcol),
                              waits=[tq, sch.tok("sV")], inc=("sS", 1))
                            E("scalar", lambda s: s.copy(qw[:, 0:ncols], zw[:, 0:ncols]),
                              inc=("sS", 1))
                            tqq = sch.tok("sS")
                            E("vector", lambda v: v.scalar_tensor_tensor(
                                corrw[:, 0:ncols], zw[:, 0:ncols], -0.5, qw[:, 0:ncols],
                                op0=ALU.add, op1=ALU.is_ge), waits=[tqq], inc=("sV", 1))
                            E("vector", lambda v: v.scalar_tensor_tensor(
                                corrw2[:, 0:ncols], zw[:, 0:ncols], 0.5, qw[:, 0:ncols],
                                op0=ALU.add, op1=ALU.is_le), inc=("sV", 1))
                            E("vector", lambda v: v.scalar_tensor_tensor(
                                zw[:, 0:ncols], corrw[:, 0:ncols], 1.0, qw[:, 0:ncols],
                                op0=ALU.mult, op1=ALU.add), inc=("sV", 1))
                            E("vector", lambda v: v.tensor_sub(
                                zw[:, 0:ncols], zw[:, 0:ncols], corrw2[:, 0:ncols]),
                              inc=("sV", 1))
                            E("vector", lambda v: v.tensor_scalar(
                                tern[:, 0:ncols], zw[:, 0:ncols], 1.0, -1.0,
                                op0=ALU.min, op1=ALU.max),
                              waits=[sch.tok("sT")], inc=("sV", 1))
                            src = tern
                        else:
                            E("vector", lambda v, wld_t=wld_t: v.tensor_copy(
                                twbf[:, 0:ncols], wld_t),
                              waits=[tq, sch.tok("sT")], inc=("sV", 1))
                            src = twbf
                        tsrc = sch.tok("sV")
                        for pi in range(npc):
                            gp = pctr[0]; pctr[0] += 1
                            pr = gp % 2
                            so_ = "sOA" if pr == 0 else "sOB"
                            wtr = [tsrc]
                            if pi >= 2:
                                wtr.append(sch.tok("sV"))
                            E("tensor", lambda t, pi=pi, pr=pr, src=src: t.transpose(
                                pst[pr][:, 0:128], src[:, pi * 128:(pi + 1) * 128], ident_b[:]),
                              waits=wtr, inc=("sT", 1))
                            tt = sch.tok("sT")
                            wev = [tt]
                            if gp >= 2:
                                wev.append(sch.tok(so_))
                            E("vector", lambda v, pr=pr: v.tensor_copy(stg[pr][:],
                                                                       pst[pr][:, 0:128]),
                              waits=wev, inc=("sV", 1))
                            tv = sch.tok("sV")
                            E("sync", lambda q, ci=ci, pi=pi, pr=pr, out_T=out_T: q.dma_start(
                                out_T[pi * 128:(pi + 1) * 128, ci * 128:(ci + 1) * 128],
                                stg[pr][:]),
                              waits=[tv], inc=(so_, 16))

                quant_transpose(w1_in, F, D, sw1b[:], wq1T, True)
                quant_transpose(w2_in, D, F, sw2b[:], wq2T, True)
                quant_transpose(wa_in, D, D, None, wqaT, False)
                quant_transpose(wx_in, D, D, None, wqxT, False)
                sch.flush(blk)

        tok_p0a = sch.tok("sOA")
        tok_p0b = sch.tok("sOB")

        def ffn_phase(name, src_ap, dst_ap, src_tok_fn, store_toks_out, entry):
            NT = NTOK // 128
            with ExitStack() as ph:
                psb = lambda n, shape, dt=F32: ph.enter_context(
                    nc.sbuf_tensor(name + n, shape, dt))
                w1T = psb("_w1T", [128, 8 * F], BF16)
                w2T = psb("_w2T", [128, 32 * D], BF16)
                xb = [psb(f"_xb{i}", [128, D]) for i in range(3)]
                rb1 = psb("_rb1", [128, D], BF16)
                r1T = psb("_r1T", [128, 2048], BF16)
                hbuf = psb("_h", [128, F])
                zq = [psb(f"_zq{i}", [128, 1024]) for i in range(2)]
                qi = [psb(f"_qi{i}", [128, 1024], mybir.dt.int16) for i in range(2)]
                gbuf = [psb(f"_g{i}", [128, F], BF16) for i in range(2)]
                gstg2_ = psb("_gs2", [128, 1024], BF16)
                gstg2 = [gstg2_[:, 0:512], gstg2_[:, 512:1024]]
                bnh = psb("_bnh", [128, 48])
                bnb = bnh
                sqm2 = psb("_sqm2", [128, 16])
                sqm = sqm2
                pm = [ph.enter_context(nc.psum_tensor(f"{name}_pm{i}", [128, 512], F32))
                      for i in range(2)]
                po = [ph.enter_context(nc.psum_tensor(f"{name}_po{i}", [128, 512], F32))
                      for i in range(2)]
                ptr2 = ph.enter_context(nc.psum_tensor(name + "_pt2", [128, 4096], BF16))

                with nc.Block(name) as blk:
                    for k in range(8):
                        E("sync", lambda q, k=k: q.dma_start(
                            w1T[:, k * F:(k + 1) * F], wq1T[k * 128:(k + 1) * 128, :]),
                          waits=[tok_p0a, tok_p0b] + entry, inc=("sW", 16))
                    for fc in range(32):
                        E("sync", lambda q, fc=fc: q.dma_start(
                            w2T[:, fc * D:(fc + 1) * D], wq2T[fc * 128:(fc + 1) * 128, :]),
                          waits=[tok_p0a, tok_p0b] + entry, inc=("sW", 16))
                    tok_w = sch.tok("sW")

                    tiles = {}
                    st = dict(pm={}, po_ev=None, ptr={}, gstg={})

                    def emit_load(j):
                        if j >= NT:
                            return
                        sl_ = "sLA" if j % 2 == 0 else "sLB"
                        waits = [tok_w, src_tok_fn(j)]
                        if j >= 3:
                            waits.append(tiles[j - 3]["stored"])
                        if j >= 2:
                            waits.append(tiles[j - 2]["load"])
                        b, t = gtile(j)
                        E("sync", lambda q, j=j, b=b, t=t: q.dma_start(
                            xb[j % 3][:], src_ap[b, t:t + 128, :]),
                          waits=waits, inc=(sl_, 16))
                        tiles.setdefault(j, {})["load"] = sch.tok(sl_)

                    def stA(j):
                        if j >= NT:
                            return
                        tj = tiles.setdefault(j, {})
                        cb = NC_ * (j % 3)
                        col = lambda i: colp[:, cb + i:cb + i + 1]
                        tj["col"] = col
                        t2 = tiles.get(j - 2, {})
                        t3 = tiles.get(j - 3, {})
                        xt = xb[j % 3][:]
                        E("vector", lambda v: v.reduce_sum(col(0), xt, axis=AX.X),
                          waits=[tj["load"], t3.get("colfree")], inc=("sV", 1))
                        E("vector", lambda v: v.tensor_single_scalar(
                            col(3), col(0), 1.0 / 1024.0, op=ALU.mult), inc=("sV", 1))
                        E("vector", lambda v: v.reduce_max(col(12), xt, axis=AX.X),
                          inc=("sV", 1))
                        E("vector", lambda v: v.tensor_reduce(col(13), xt, axis=AX.X,
                                                              op=ALU.min), inc=("sV", 1))
                        E("vector", lambda v: v.tensor_sub(col(14), col(12), col(3)),
                          inc=("sV", 1))
                        E("vector", lambda v: v.tensor_sub(col(15), col(3), col(13)),
                          inc=("sV", 1))
                        E("vector", lambda v: v.tensor_max(col(14), col(14), col(15)),
                          inc=("sV", 1))
                        E("vector", lambda v: v.tensor_single_scalar(
                            col(15), col(14), 1e-30, op=ALU.max), inc=("sV", 1))
                        E("vector", lambda v: v.reciprocal(col(15), col(15)), inc=("sV", 1))
                        E("vector", lambda v: v.tensor_single_scalar(
                            col(18), col(15), 127.0, op=ALU.mult), inc=("sV", 1))
                        E("vector", lambda v: v.scalar_tensor_tensor(
                            col(19), col(3), -1.0, col(18), op0=ALU.mult, op1=ALU.mult),
                          inc=("sV", 1))
                        E("vector", lambda v: v.tensor_single_scalar(col(19), col(19), 256.0,
                                                                     op=ALU.add), inc=("sV", 1))
                        tcols = sch.tok("sV")
                        tj["tcols"] = tcols
                        E("vector", lambda v: v.tensor_scalar(
                            zq[0][:], xt, col(18), col(19), op0=ALU.mult, op1=ALU.add),
                          waits=[tcols, t2.get("zq0free"),
                                 tiles.get(j - 1, {}).get("r1")], inc=("sV", 1))
                        tz1 = sch.tok("sV")
                        E("scalar", lambda s: s.copy(qi[0][:], zq[0][:]), waits=[tz1],
                          inc=("sS", 1))
                        tq1 = sch.tok("sS")
                        E("gpsimd", lambda g: g.tensor_tensor(
                            zq[0][:], zq[0][:], qi[0][:], op=ALU.subtract), waits=[tq1],
                          inc=("sG", 1))
                        E("gpsimd", lambda g: g.tensor_scalar(
                            zq[0][:], zq[0][:], -0.5, 0.0, op0=ALU.add, op1=ALU.is_ge),
                          inc=("sG", 1))
                        E("gpsimd", lambda g: g.tensor_single_scalar(
                            zq[0][:], zq[0][:], -256.0, op=ALU.add), inc=("sG", 1))
                        E("gpsimd", lambda g: g.tensor_tensor(
                            rb1[:], zq[0][:], qi[0][:], op=ALU.add),
                          waits=[tiles.get(j - 1, {}).get("tr1_done")], inc=("sG", 1))
                        tj["r1"] = sch.tok("sG")
                        E("vector", lambda v: v.bn_stats(bnb[:, 0:6], xt[:, 0:512]),
                          waits=[t2.get("sqmS")], inc=("sV", 1))
                        E("vector", lambda v: v.bn_stats(bnb[:, 6:12], xt[:, 512:1024]),
                          inc=("sV", 1))
                        tbn = sch.tok("sV")
                        E("vector", lambda v: v.tensor_tensor(
                            sqm[:, 0:4], sview(bnb, 1, 3, 4, 48), sview(bnb, 1, 3, 4, 48),
                            op=ALU.mult), inc=("sV", 1))
                        E("vector", lambda v: v.reduce_sum(col(1), sview(bnb, 2, 3, 4, 48),
                                                           axis=AX.X), inc=("sV", 1))
                        E("vector", lambda v: v.reduce_sum(col(2), sqm[:, 0:4], axis=AX.X),
                          inc=("sV", 1))
                        tj["sqmS"] = sch.tok("sV")
                        E("vector", lambda v: v.tensor_single_scalar(col(4), col(2), 0.25,
                                                                     op=ALU.mult), inc=("sV", 1))
                        E("vector", lambda v: v.scalar_tensor_tensor(
                            col(5), col(1), 1.0 / 1024.0, col(4), op0=ALU.mult, op1=ALU.add),
                          inc=("sV", 1))
                        E("vector", lambda v: v.tensor_mul(col(6), col(3), col(3)), inc=("sV", 1))
                        E("vector", lambda v: v.tensor_sub(col(7), col(5), col(6)), inc=("sV", 1))
                        E("vector", lambda v: v.tensor_single_scalar(col(8), col(7), EPS,
                                                                     op=ALU.add), inc=("sV", 1))
                        E("vector", lambda v: v.scalar_tensor_tensor(
                            col(9), col(8), EPS, col(7), op0=ALU.mult, op1=ALU.add),
                          inc=("sV", 1))
                        tv_ab = sch.tok("sV")
                        E("scalar", lambda s: s.activation(col(10), col(9), AF.Sqrt),
                          waits=[tv_ab], inc=("sS", 1))
                        E("vector", lambda v: v.reciprocal(col(11), col(10)),
                          waits=[sch.tok("sS")], inc=("sV", 1))
                        E("vector", lambda v: v.tensor_mul(col(16), col(11), col(14)),
                          inc=("sV", 1))
                        E("vector", lambda v: v.tensor_single_scalar(col(16), col(16), EPS,
                                                                     op=ALU.max), inc=("sV", 1))
                        E("vector", lambda v: v.tensor_mul(col(20), col(16), wM1[:]),
                          inc=("sV", 1))
                        tj["c1"] = sch.tok("sV")

                    def stTr1(j):
                        if j >= NT:
                            return
                        tj = tiles[j]
                        for q4 in range(2):
                            half = q4 % 2
                            for ci in range(4):
                                c = q4 * 4 + ci
                                E("tensor", lambda t, c=c, half=half, ci=ci: t.transpose(
                                    ptr2[:, half * 2048 + ci * 512: half * 2048 + ci * 512 + 128],
                                    rb1[:, c * 128:(c + 1) * 128], ident_b[:]),
                                  waits=([tj["r1"]] if ci == 0 else []) +
                                        ([st["ptr"].get(half)] if ci == 0 else []),
                                  inc=("sT", 1))
                            ttr = sch.tok("sT")
                            psrc = bass.AP(ptr2, half * 2048, [[4096, 128], [512, 4], [1, 128]])
                            dst = bass.AP(r1T, (j % 2) * 128 + q4 * 1024,
                                          [[2048, 128], [256, 4], [1, 128]])
                            tok = E("scalar", lambda s, dst=dst, psrc=psrc: s.copy(dst, psrc),
                                    waits=[ttr], inc=("sS", 1))
                            st["ptr"][half] = tok
                        tj["r1T_evac"] = sch.tok("sS")
                        tj["tr1_done"] = sch.tok("sT")

                    def stB1(j):
                        tj = tiles[j]
                        tprev = tiles.get(j - 1, {})
                        col = tj["col"]
                        tj["ev1"] = {}
                        gelu_toks = []
                        for f in range(8):
                            bank = pm[f % 2]
                            wmm = [tj["r1T_evac"], st["pm"].get(f % 2)]
                            if f == 0:
                                wmm.append(tok_w)
                            for k in range(8):
                                E("tensor", lambda t, f=f, k=k, bank=bank, j=j: t.matmul(
                                    bank[:],
                                    r1T[:, k * 256 + (j % 2) * 128: k * 256 + (j % 2) * 128 + 128],
                                    w1T[:, k * F + f * 512: k * F + (f + 1) * 512],
                                    start=(k == 0), stop=(k == 7)),
                                  waits=wmm if k == 0 else [], inc=("sT", 1))
                            tmm = sch.tok("sT")
                            wev = [tmm, tj["c1"]]
                            if f == 0:
                                wev.append(tprev.get("hfree"))
                            tok = E("vector", lambda v, f=f, bank=bank, col=col:
                                    v.scalar_tensor_tensor(
                                hbuf[:, f * 512:(f + 1) * 512], bank[:], col(20),
                                b1c[:, f * 512:(f + 1) * 512], op0=ALU.mult, op1=ALU.add),
                              waits=wev, inc=("sV", 1))
                            st["pm"][f % 2] = tok
                            tj["ev1"][f] = tok
                            if f == 3 or f == 7:
                                h0 = 0 if f == 3 else 2048
                                E("scalar", lambda s, h0=h0: s.activation(
                                    hbuf[:, h0:h0 + 2048], hbuf[:, h0:h0 + 2048], AF.Gelu),
                                  waits=[tok], inc=("sS", 1))
                                gelu_toks.append(sch.tok("sS"))
                        tj["gelu"] = gelu_toks
                        tj["mm1_done"] = sch.tok("sT")

                    def stC(j):
                        tj = tiles[j]
                        tnext = tiles.get(j + 1, {})
                        t2 = tiles.get(j - 2, {})
                        col = tj["col"]
                        E("vector", lambda v: v.reduce_max(col(0), hbuf[:], axis=AX.X,
                                                           apply_absolute_value=True),
                          waits=[tj["gelu"][1]], inc=("sV", 1))
                        E("vector", lambda v: v.tensor_single_scalar(
                            col(1), col(0), 1e-30, op=ALU.max), inc=("sV", 1))
                        E("vector", lambda v: v.reciprocal(col(1), col(1)), inc=("sV", 1))
                        E("vector", lambda v: v.tensor_single_scalar(
                            col(12), col(1), 127.0, op=ALU.mult), inc=("sV", 1))
                        tsig2 = sch.tok("sV")
                        tj["r2"] = []
                        for qc in range(4):
                            p2 = qc % 2
                            wz = [tsig2]
                            if qc == 0:
                                wz += [tnext.get("r1"), t2.get("zq0free2")]
                            elif qc == 1:
                                wz.append(t2.get("zq1free"))
                            else:
                                wz.append(tj["r2"][qc - 2])
                            E("vector", lambda v, qc=qc, p2=p2, col=col: v.tensor_scalar(
                                zq[p2][:], hbuf[:, qc * 1024:(qc + 1) * 1024], col(12), 256.0,
                                op0=ALU.mult, op1=ALU.add), waits=wz, inc=("sV", 1))
                            tz2 = sch.tok("sV")
                            wq_ = [tz2] + ([tnext.get("r1")] if qc == 0 else [])
                            E("scalar", lambda s, p2=p2: s.copy(qi[p2][:], zq[p2][:]),
                              waits=wq_, inc=("sS", 1))
                            tq2 = sch.tok("sS")
                            E("gpsimd", lambda g, p2=p2: g.tensor_tensor(
                                zq[p2][:], zq[p2][:], qi[p2][:], op=ALU.subtract),
                              waits=[tq2], inc=("sG", 1))
                            E("gpsimd", lambda g, p2=p2: g.tensor_scalar(
                                zq[p2][:], zq[p2][:], -0.5, 0.0, op0=ALU.add, op1=ALU.is_ge),
                              inc=("sG", 1))
                            E("gpsimd", lambda g, p2=p2: g.tensor_single_scalar(
                                zq[p2][:], zq[p2][:], -256.0, op=ALU.add), inc=("sG", 1))
                            tok = E("gpsimd", lambda g, qc=qc, p2=p2, j=j: g.tensor_tensor(
                                gbuf[j % 2][:, qc * 1024:(qc + 1) * 1024], zq[p2][:],
                                qi[p2][:], op=ALU.add),
                              waits=[t2.get("trg_done")] if qc == 0 else [], inc=("sG", 1))
                            tj["r2"].append(tok)
                        tj["zq0free2"] = tj["r2"][2]
                        tj["zq0free"] = tj["r2"][2]
                        tj["zq1free"] = tj["r2"][3]
                        tj["hfree"] = sch.tok("sV")
                        for i8 in range(8):
                            w = []
                            if i8 == 0:
                                w = [tj["gelu"][1], t2.get("sqm2V")]
                            E("vector", lambda v, i8=i8: v.bn_stats(
                                bnh[:, i8 * 6:(i8 + 1) * 6],
                                hbuf[:, i8 * 512:(i8 + 1) * 512]), waits=w, inc=("sV", 1))
                        E("vector", lambda v: v.tensor_tensor(
                            sqm2[:, 0:16], sview(bnh, 1, 3, 16, 48), sview(bnh, 1, 3, 16, 48),
                            op=ALU.mult), inc=("sV", 1))
                        E("vector", lambda v: v.reduce_sum(col(2), sview(bnh, 2, 3, 16, 48),
                                                           axis=AX.X), inc=("sV", 1))
                        E("vector", lambda v: v.reduce_sum(col(4), sqm2[:, 0:16], axis=AX.X),
                          inc=("sV", 1))
                        tj["sqm2V"] = sch.tok("sV")
                        E("vector", lambda v: v.tensor_single_scalar(col(4), col(4), 1.0 / 16,
                                                                     op=ALU.mult), inc=("sV", 1))
                        E("vector", lambda v: v.scalar_tensor_tensor(
                            col(5), col(2), 1.0 / F, col(4), op0=ALU.mult, op1=ALU.add),
                          inc=("sV", 1))
                        E("vector", lambda v: v.tensor_single_scalar(col(5), col(5), EPS,
                                                                     op=ALU.add), inc=("sV", 1))
                        tv_eh = sch.tok("sV")
                        E("scalar", lambda s: s.activation(col(2), col(5), AF.Sqrt),
                          waits=[tv_eh], inc=("sS", 1))
                        E("vector", lambda v: v.reciprocal(col(2), col(2)),
                          waits=[sch.tok("sS")], inc=("sV", 1))
                        E("vector", lambda v: v.tensor_mul(col(13), col(2), col(0)),
                          inc=("sV", 1))
                        E("vector", lambda v: v.tensor_single_scalar(col(13), col(13), EPS,
                                                                     op=ALU.max), inc=("sV", 1))
                        E("vector", lambda v: v.tensor_mul(col(13), col(13), wM2h[:]),
                          inc=("sV", 1))
                        tj["c2"] = sch.tok("sV")
                        tj["colfree"] = sch.tok("sS")

                    def stB2(j):
                        if j < 0:
                            return
                        tj = tiles[j]
                        col = tj["col"]
                        for q4 in range(8):
                            half = q4 % 2
                            for ci in range(4):
                                fc = q4 * 4 + ci
                                E("tensor", lambda t, fc=fc, half=half, ci=ci, j=j: t.transpose(
                                    ptr2[:, half * 2048 + ci * 512: half * 2048 + ci * 512 + 128],
                                    gbuf[j % 2][:, fc * 128:(fc + 1) * 128], ident_b[:]),
                                  waits=([tj["r2"][fc // 8]] if ci == 0 else []) +
                                        ([st["ptr"].get(half)] if ci == 0 else []),
                                  inc=("sT", 1))
                            ttr = sch.tok("sT")
                            psrc = bass.AP(ptr2, half * 2048, [[4096, 128], [512, 4], [1, 128]])
                            tok = E("scalar", lambda s, psrc=psrc, half=half: s.copy(
                                gstg2[half][:], psrc),
                                waits=[ttr, st["gstg"].get(half)], inc=("sS", 1))
                            st["ptr"][half] = tok
                            wmm2 = [tok]
                            if q4 == 0:
                                wmm2 += [st["po_ev"], tok_w]
                            for pi in range(4):
                                fc = q4 * 4 + pi
                                for n in range(2):
                                    E("tensor", lambda t, fc=fc, n=n, pi=pi, half=half: t.matmul(
                                        po[n][:], gstg2[half][:, pi * 128:(pi + 1) * 128],
                                        w2T[:, fc * D + n * 512: fc * D + (n + 1) * 512],
                                        start=(fc == 0), stop=(fc == 31)),
                                      waits=wmm2 if (pi == 0 and n == 0) else [],
                                      inc=("sT", 1))
                            st["gstg"][half] = sch.tok("sT")
                        tj["trg_done"] = sch.tok("sT")
                        tmm2 = tj["trg_done"]
                        xt = xb[j % 3][:]
                        for n in range(2):
                            E("vector", lambda v, n=n, xt=xt, col=col: v.scalar_tensor_tensor(
                                xt[:, n * 512:(n + 1) * 512], po[n][:], col(13),
                                xt[:, n * 512:(n + 1) * 512], op0=ALU.mult, op1=ALU.add),
                              waits=[tmm2, tj["c2"]] if n == 0 else [], inc=("sV", 1))
                        st["po_ev"] = sch.tok("sV")
                        E("vector", lambda v, xt=xt: v.tensor_add(xt, xt, b2h[:]),
                          inc=("sV", 1))
                        tout = sch.tok("sV")
                        b, t = gtile(j)
                        E("sync", lambda q, j=j, b=b, t=t: q.dma_start(
                            dst_ap[b, t:t + 128, :], xb[j % 3][:]),
                          waits=[tout], inc=("sO", 16))
                        tj["stored"] = sch.tok("sO")
                        store_toks_out.append(tj["stored"])

                    emit_load(0)
                    emit_load(1)
                    stA(0)
                    stTr1(0)
                    for j in range(NT):
                        stA(j + 1)
                        stB1(j)
                        stB2(j - 1)
                        emit_load(j + 2)
                        stTr1(j + 1)
                        stC(j)
                    stB2(NT - 1)
                    sch.flush(blk)

        def griffin_phase(src_ap, dst_ap, src_store_toks, store_toks_out, entry):
            with ExitStack() as ph:
                psb = lambda n, shape, dt=F32: ph.enter_context(
                    nc.sbuf_tensor("g2" + n, shape, dt))
                waT = psb("_waT", [128, 8 * D], BF16)
                wxT = psb("_wxT", [128, 8 * D], BF16)
                xb = [psb("_xb0", [128, 4 * D]), psb("_xb1", [128, 4 * D])]
                xn_tm = psb("_xn", [128, D])
                xnT_f = psb("_xnTf", [128, 4096])
                xnT_b = psb("_xnTb", [128, 4096], BF16)
                a_fm = psb("_a", [128, 4096])
                b_fm = psb("_b", [128, 4096])
                it_fm = psb("_it", [128, 4096])
                rt_tmp = psb("_rt", [128, 512])
                tsc = psb("_tsc", [128, 512])
                pbuf = psb("_pb", [128, 512])
                bnb = psb("_bnb2", [128, 12])
                sqm = psb("_sqm2", [128, 4])
                pg = [ph.enter_context(nc.psum_tensor(f"g2_pg{i}", [128, 512], F32))
                      for i in range(4)]
                ptf = [ph.enter_context(nc.psum_tensor(f"g2_pt{i}", [128, 512], F32))
                       for i in range(2)]

                with nc.Block("phase2") as blk:
                    for k in range(8):
                        E("sync", lambda q, k=k: q.dma_start(
                            waT[:, k * D:(k + 1) * D], wqaT[k * 128:(k + 1) * 128, :]),
                          waits=[tok_p0a, tok_p0b] + entry, inc=("sW", 16))
                        E("sync", lambda q, k=k: q.dma_start(
                            wxT[:, k * D:(k + 1) * D], wqxT[k * 128:(k + 1) * 128, :]),
                          waits=[tok_p0a, tok_p0b], inc=("sW", 16))
                    tok_w = sch.tok("sW")

                    load_toks = {}
                    NT2 = B_core * NST2_SEQ

                    def emit_load(ti):
                        par = ti % 2
                        sl_ = "sLA" if par == 0 else "sLB"
                        waits = [tok_w]
                        if ti >= 2:
                            waits.append(store_toks_out[4 * (ti - 2) + 3])
                        if src_store_toks:
                            waits.append(src_store_toks[-1])
                        for m in range(4):
                            b, t = gtile(4 * ti + m)
                            E("sync", lambda q, par=par, m=m, b=b, t=t: q.dma_start(
                                xb[par][:, m * D:(m + 1) * D], src_ap[b, t:t + 128, :]),
                              waits=waits if m == 0 else [], inc=(sl_, 16))
                        load_toks[ti] = sch.tok(sl_)

                    emit_load(0)
                    if NT2 > 1:
                        emit_load(1)

                    st2 = dict(ptf={}, xn_tr=None, pg_read={}, sguard={}, scan=None,
                               b_done=None)

                    g2m = os.environ.get("ATH_G2", "lmbsy")
                    for ti in range(NT2):
                        par = ti % 2
                        jseq = ti % NST2_SEQ
                        tload = load_toks[ti]
                        for m in range(4):
                            cb = NC_ * (m % 2)
                            col = lambda i: colp[:, cb + i:cb + i + 1]
                            xt = xb[par][:, m * D:(m + 1) * D]
                            E("vector", lambda v, xt=xt: v.bn_stats(bnb[:, 0:6], xt[:, 0:512]),
                              waits=[tload, st2["sguard"].get(m % 2)], inc=("sV", 1))
                            E("vector", lambda v, xt=xt: v.bn_stats(bnb[:, 6:12],
                                                                    xt[:, 512:1024]),
                              inc=("sV", 1))
                            tbn = sch.tok("sV")
                            E("scalar", lambda s: s.activation(
                                sqm[:, 0:4], sview(bnb, 1, 3, 4, 12), AF.Square),
                              waits=[tbn], inc=("sS", 1))
                            tsq = sch.tok("sS")
                            E("vector", lambda v, col=col: v.reduce_sum(
                                col(0), sview(bnb, 1, 3, 4, 12), axis=AX.X),
                              waits=[tbn], inc=("sV", 1))
                            E("vector", lambda v, col=col: v.reduce_sum(
                                col(1), sview(bnb, 2, 3, 4, 12), axis=AX.X), inc=("sV", 1))
                            E("vector", lambda v, col=col: v.reduce_sum(
                                col(2), sqm[:, 0:4], axis=AX.X), waits=[tsq], inc=("sV", 1))
                            E("vector", lambda v, col=col: v.tensor_single_scalar(
                                col(3), col(0), 0.25, op=ALU.mult), inc=("sV", 1))
                            E("vector", lambda v, col=col: v.tensor_single_scalar(
                                col(4), col(2), 0.25, op=ALU.mult), inc=("sV", 1))
                            E("vector", lambda v, col=col: v.scalar_tensor_tensor(
                                col(5), col(1), 1.0 / 1024.0, col(4), op0=ALU.mult,
                                op1=ALU.add), inc=("sV", 1))
                            E("vector", lambda v, col=col: v.tensor_mul(col(6), col(3), col(3)),
                              inc=("sV", 1))
                            E("vector", lambda v, col=col: v.tensor_sub(col(7), col(5), col(6)),
                              inc=("sV", 1))
                            E("vector", lambda v, col=col: v.tensor_single_scalar(
                                col(8), col(7), EPS, op=ALU.add), inc=("sV", 1))
                            tv_a = sch.tok("sV")
                            E("scalar", lambda s, col=col: s.activation(col(9), col(8), AF.Sqrt),
                              waits=[tv_a], inc=("sS", 1))
                            E("vector", lambda v, col=col: v.reciprocal(col(10), col(9)),
                              waits=[sch.tok("sS")], inc=("sV", 1))
                            E("vector", lambda v, col=col: v.scalar_tensor_tensor(
                                col(11), col(3), -1.0, col(10), op0=ALU.mult, op1=ALU.mult),
                              inc=("sV", 1))
                            tv_c = sch.tok("sV")
                            E("scalar", lambda s, xt=xt, col=col: s.activation(
                                xn_tm[:], xt, AF.Identity, bias=col(11), scale=col(10)),
                              waits=[tv_c, st2["xn_tr"]], inc=("sS", 1))
                            txn = sch.tok("sS")
                            for c in range(8):
                                pr = c % 2
                                wtr = [txn] + ([st2["ptf"][pr]] if pr in st2["ptf"] else [])
                                if (pr, "s") in st2["ptf"]:
                                    wtr.append(st2["ptf"][(pr, "s")])
                                E("tensor", lambda t, c=c, pr=pr: t.transpose(
                                    ptf[pr][:, 0:128], xn_tm[:, c * 128:(c + 1) * 128],
                                    ident_f[:]), waits=wtr,
                                  inc=("sT", 1))
                                ttr = sch.tok("sT")
                                cc = c * 512 + m * 128
                                t1 = E("vector", lambda v, pr=pr, cc=cc: v.tensor_copy(
                                    xnT_f[:, cc:cc + 128], ptf[pr][:, 0:128]),
                                  waits=[ttr], inc=("sV", 1))
                                t2 = E("scalar", lambda s, cc=cc: s.copy(
                                    xnT_b[:, cc:cc + 128], xnT_f[:, cc:cc + 128]),
                                  waits=[t1], inc=("sS", 1))
                                st2["ptf"][pr] = t1
                            st2["xn_tr"] = sch.tok("sT")
                            st2["sguard"][m % 2] = sch.tok("sS")
                        txnb = sch.tok("sS")
                        txnf = sch.tok("sV")
                        scan_toks = []
                        for c in (range(8) if "m" in g2m else []):
                            bk_rt = pg[(2 * c) % 4]
                            bk_it = pg[(2 * c + 1) % 4]
                            wmm = [txnb, st2["pg_read"].get((2 * c) % 4),
                                   st2["pg_read"].get((2 * c + 1) % 4),
                                   st2["scan"], tok_w if c == 0 else None]
                            for k in range(8):
                                E("tensor", lambda t, c=c, k=k, bk=bk_rt: t.matmul(
                                    bk[:], waT[:, k * D + c * 128: k * D + (c + 1) * 128],
                                    xnT_b[:, k * 512:(k + 1) * 512],
                                    start=(k == 0), stop=(k == 7)),
                                  waits=wmm if k == 0 else [], inc=("sT", 1))
                            for k in range(8):
                                E("tensor", lambda t, c=c, k=k, bk=bk_it: t.matmul(
                                    bk[:], wxT[:, k * D + c * 128: k * D + (c + 1) * 128],
                                    xnT_b[:, k * 512:(k + 1) * 512],
                                    start=(k == 0), stop=(k == 7)), inc=("sT", 1))
                            tmm = sch.tok("sT")
                            E("scalar", lambda s, c=c, bk=bk_rt: s.activation(
                                rt_tmp[:], bk[:], AF.Sigmoid, bias=ba_fm[:, c:c + 1]),
                              waits=[tmm, st2["b_done"]], inc=("sS", 1))
                            E("scalar", lambda s, c=c: s.activation(
                                a_fm[:, c * 512:(c + 1) * 512], rt_tmp[:], AF.Exp,
                                scale=coef_fm[:, c:c + 1]), inc=("sS", 1))
                            st2["pg_read"][(2 * c) % 4] = sch.tok("sS")
                            E("scalar", lambda s, c=c, bk=bk_it: s.activation(
                                it_fm[:, c * 512:(c + 1) * 512], bk[:], AF.Sigmoid,
                                bias=bx_fm[:, c:c + 1]), inc=("sS", 1))
                            st2["pg_read"][(2 * c + 1) % 4] = sch.tok("sS")
                            ta = sch.tok("sS")
                            if "b" not in g2m:
                                scan_toks.append(sch.tok("sS"))
                                continue
                            E("scalar", lambda s, c=c: s.activation(
                                tsc[:], a_fm[:, c * 512:(c + 1) * 512], AF.Square),
                              waits=[ta], inc=("sS", 1))
                            E("scalar", lambda s: s.activation(
                                tsc[:], tsc[:], AF.Sqrt, bias=1.0, scale=-1.0), inc=("sS", 1))
                            tb_ = sch.tok("sS")
                            E("vector", lambda v, c=c: v.tensor_mul(
                                pbuf[:], it_fm[:, c * 512:(c + 1) * 512],
                                xnT_f[:, c * 512:(c + 1) * 512]),
                              waits=[ta, txnf], inc=("sV", 1))
                            tokb = E("vector", lambda v, c=c: v.tensor_mul(
                                b_fm[:, c * 512:(c + 1) * 512], tsc[:], pbuf[:]),
                              waits=[tb_], inc=("sV", 1))
                            st2["b_done"] = tokb
                            if "s" not in g2m:
                                scan_toks.append(sch.tok("sV"))
                                continue
                            if jseq == 0:
                                E("vector", lambda v, c=c: v.tensor_tensor_scan(
                                    xnT_f[:, c * 512:(c + 1) * 512],
                                    a_fm[:, c * 512:(c + 1) * 512],
                                    b_fm[:, c * 512:(c + 1) * 512],
                                    0.0, op0=ALU.mult, op1=ALU.add), inc=("sV", 1))
                            else:
                                E("vector", lambda v, c=c: v.tensor_tensor_scan(
                                    xnT_f[:, c * 512:(c + 1) * 512],
                                    a_fm[:, c * 512:(c + 1) * 512],
                                    b_fm[:, c * 512:(c + 1) * 512],
                                    carry[:, c:c + 1], op0=ALU.mult, op1=ALU.add),
                                  inc=("sV", 1))
                            E("vector", lambda v, c=c: v.tensor_copy(
                                carry[:, c:c + 1], xnT_f[:, c * 512 + 511:c * 512 + 512]),
                              inc=("sV", 1))
                            scan_toks.append(sch.tok("sV"))
                        if scan_toks:
                            st2["scan"] = scan_toks[-1]
                        for c in (range(8) if ("y" in g2m and "s" in g2m and "m" in g2m)
                                  else []):
                            for m in range(4):
                                pr = (c * 4 + m) % 2
                                wtr = [scan_toks[c]] + ([st2["ptf"][pr]] if pr in st2["ptf"]
                                                        else [])
                                if (pr, "s") in st2["ptf"]:
                                    wtr.append(st2["ptf"][(pr, "s")])
                                cc = c * 512 + m * 128
                                E("tensor", lambda t, pr=pr, cc=cc: t.transpose(
                                    ptf[pr][:, 0:128], xnT_f[:, cc:cc + 128], ident_f[:]),
                                  waits=wtr, inc=("sT", 1))
                                ttr = sch.tok("sT")
                                dd = m * D + c * 128
                                tok = E("vector", lambda v, pr=pr, dd=dd, par=par: v.tensor_add(
                                    xb[par][:, dd:dd + 128], xb[par][:, dd:dd + 128],
                                    ptf[pr][:, 0:128]), waits=[ttr], inc=("sV", 1))
                                st2["ptf"][pr] = tok
                                st2["ptf"].pop((pr, "s"), None)
                        tdone = sch.tok("sV")
                        for m in range(4):
                            b, t = gtile(4 * ti + m)
                            E("sync", lambda q, par=par, m=m, b=b, t=t: q.dma_start(
                                dst_ap[b, t:t + 128, :], xb[par][:, m * D:(m + 1) * D]),
                              waits=[tdone] if m == 0 else [], inc=("sO", 16))
                            store_toks_out.append(sch.tok("sO"))
                        if ti + 2 < NT2:
                            emit_load(ti + 2)
                    sch.flush(blk)

        def entry_toks():
            return [sch.tok("sO"), sch.tok("sV"), sch.tok("sS"), sch.tok("sT"),
                    sch.tok("sLA"), sch.tok("sLB")]

        phases = os.environ.get("ATH_PHASES", "123")
        p1_stores = []
        if "1" in phases:
            ffn_phase("f1", x_in, xs1, lambda st: None, p1_stores, entry_toks())
        p2_stores = []
        if "2" in phases:
            griffin_phase(xs1 if "1" in phases else x_in, xs2,
                          p1_stores, p2_stores, entry_toks())
        if "3" in phases:
            src3 = xs2 if "2" in phases else (xs1 if "1" in phases else x_in)
            p3_stores = []
            ffn_phase("f3", src3, y_out,
                      (lambda st: p2_stores[-1]) if p2_stores else
                      ((lambda st: p1_stores[-1]) if p1_stores else (lambda st: None)),
                      p3_stores, entry_toks())

        with nc.Block("fin") as blk:
            E("sync", lambda q: q.nop(),
              waits=[sch.tok("sO"), sch.tok("sOA"), sch.tok("sOB"), sch.tok("sW"),
                     sch.tok("sLA"), sch.tok("sLB"), sch.tok("sC")])
            sch.flush(blk)

    return nc


_NC_CACHE = {}


def _get_nc(B_core, S_seq):
    key = (B_core, S_seq)
    if key not in _NC_CACHE:
        _NC_CACHE[key] = build_program(B_core, S_seq)
    return _NC_CACHE[key]


def kernel(L0_attn_x, L0_attn_Wa, L0_attn_Wx, L0_attn_ba, L0_attn_bx, L0_attn_Lam,
           L0_ffn_w1, L0_ffn_b1, L0_ffn_w2, L0_ffn_b2):
    x = np.ascontiguousarray(np.asarray(L0_attn_x, dtype=np.float32))
    B, S, _ = x.shape
    B_core = B // NCORES
    nc = _get_nc(B_core, S)
    common = dict(
        L0_attn_Wa=np.ascontiguousarray(np.asarray(L0_attn_Wa, dtype=np.float32)),
        L0_attn_Wx=np.ascontiguousarray(np.asarray(L0_attn_Wx, dtype=np.float32)),
        L0_attn_ba=np.ascontiguousarray(np.asarray(L0_attn_ba, dtype=np.float32)),
        L0_attn_bx=np.ascontiguousarray(np.asarray(L0_attn_bx, dtype=np.float32)),
        L0_attn_Lam=np.ascontiguousarray(np.asarray(L0_attn_Lam, dtype=np.float32)),
        L0_ffn_w1=np.ascontiguousarray(np.asarray(L0_ffn_w1, dtype=np.float32)),
        L0_ffn_b1=np.ascontiguousarray(np.asarray(L0_ffn_b1, dtype=np.float32)),
        L0_ffn_w2=np.ascontiguousarray(np.asarray(L0_ffn_w2, dtype=np.float32)),
        L0_ffn_b2=np.ascontiguousarray(np.asarray(L0_ffn_b2, dtype=np.float32)),
        ident_f32=np.eye(128, dtype=np.float32),
        ident_bf16=np.eye(128).astype(ml_dtypes.bfloat16),
    )
    in_maps = []
    for c in range(NCORES):
        m = dict(common)
        m["L0_attn_x"] = np.ascontiguousarray(x[c * B_core:(c + 1) * B_core])
        in_maps.append(m)
    res = run_bass_kernel_spmd(nc, in_maps, core_ids=list(range(NCORES)))
    return np.concatenate([np.asarray(r["y"], dtype=np.float32) for r in res.results],
                          axis=0)


# version 11
# speedup vs baseline: 1.9437x; 1.9437x over previous
import os
import numpy as np
import ml_dtypes
from contextlib import ExitStack

import concourse.bass as bass
import concourse.mybir as mybir
from concourse.bass_utils import run_bass_kernel_spmd

F32 = mybir.dt.float32
BF16 = mybir.dt.bfloat16
I32 = mybir.dt.int32
AF = mybir.ActivationFunctionType
ALU = mybir.AluOpType
AX = mybir.AxisListType

D = 1024
F = 4096
NCORES = 8
EPS = 1e-5
C_GRIF = 8.0

ENGINES = ("sync", "tensor", "vector", "scalar", "gpsimd")


def sview(t, off, stride, count, rowlen, parts=128):
    return bass.AP(t, off, [[rowlen, parts], [stride, count]])


class Sched:

    def __init__(self, nc, semnames):
        self.nc = nc
        self.sems = {}
        self.count = {n: 0 for n in semnames}
        self.semnames = semnames
        self.q = {e: [] for e in ENGINES}

    def make_sems(self, stack):
        for n in self.semnames:
            self.sems[n] = stack.enter_context(self.nc.semaphore(n))

    OWN = {"vector": "sV", "scalar": "sS", "gpsimd": "sG"}

    def emit(self, eng, fn, inc=None, waits=()):
        waits = [w for w in waits if w is not None]
        for s, v in waits:
            if v > self.count[s]:
                raise AssertionError(f"wait on future {s}: {v} > {self.count[s]}")
        own = self.OWN.get(eng)
        if own is not None:
            waits = [(own, self.count[own])] + waits
        self.q[eng].append((tuple(waits), fn, inc))
        if inc is not None:
            self.count[inc[0]] += inc[1]
        return (inc[0], self.count[inc[0]]) if inc is not None else None

    def tok(self, sem):
        return (sem, self.count[sem])

    def copy(self, eng, dst, src, waits=()):
        if eng == "scalar":
            return self.emit("scalar", lambda s, dst=dst, src=src: s.copy(dst, src),
                             inc=("sS", 1), waits=waits)
        return self.emit("vector", lambda v, dst=dst, src=src: v.tensor_copy(dst, src),
                         inc=("sV", 1), waits=waits)

    def flush(self, block):
        for eng in ENGINES:
            ops = self.q[eng]
            if not ops:
                continue

            def body(engine, ops=ops):
                lastw = {}
                for waits, fn, inc in ops:
                    for s, v in waits:
                        if lastw.get(s, -1) >= v or v <= 0:
                            continue
                        engine.wait_ge(self.sems[s], v)
                        lastw[s] = v
                    if fn is not None:
                        ins = fn(engine)
                        if inc is not None:
                            ins.then_inc(self.sems[inc[0]], inc[1])

            getattr(block, eng)(body)
        self.q = {e: [] for e in ENGINES}


def build_program(B_core=2, S_seq=4096):
    NTOK = B_core * S_seq
    T1 = 256
    NST1 = NTOK // T1
    NST2_SEQ = S_seq // 512
    GPERSEQ = S_seq // 128

    nc = bass.Bass("TRN2", target_bir_lowering=False)

    x_in = nc.dram_tensor("L0_attn_x", [B_core, S_seq, D], F32, kind="ExternalInput").ap()
    wa_in = nc.dram_tensor("L0_attn_Wa", [D, D], F32, kind="ExternalInput").ap()
    wx_in = nc.dram_tensor("L0_attn_Wx", [D, D], F32, kind="ExternalInput").ap()
    ba_in = nc.dram_tensor("L0_attn_ba", [D], F32, kind="ExternalInput").ap()
    bx_in = nc.dram_tensor("L0_attn_bx", [D], F32, kind="ExternalInput").ap()
    lam_in = nc.dram_tensor("L0_attn_Lam", [D], F32, kind="ExternalInput").ap()
    w1_in = nc.dram_tensor("L0_ffn_w1", [F, D], F32, kind="ExternalInput").ap()
    b1_in = nc.dram_tensor("L0_ffn_b1", [F], F32, kind="ExternalInput").ap()
    w2_in = nc.dram_tensor("L0_ffn_w2", [D, F], F32, kind="ExternalInput").ap()
    b2_in = nc.dram_tensor("L0_ffn_b2", [D], F32, kind="ExternalInput").ap()
    id_f_in = nc.dram_tensor("ident_f32", [128, 128], F32, kind="ExternalInput").ap()
    id_b_in = nc.dram_tensor("ident_bf16", [128, 128], BF16, kind="ExternalInput").ap()
    y_out = nc.dram_tensor("y", [B_core, S_seq, D], F32, kind="ExternalOutput").ap()

    wq1T = nc.dram_tensor("wq1T", [D, F], BF16, kind="Internal").ap()
    wq2T = nc.dram_tensor("wq2T", [F, D], BF16, kind="Internal").ap()
    wqaT = nc.dram_tensor("wqaT", [D, D], BF16, kind="Internal").ap()
    wqxT = nc.dram_tensor("wqxT", [D, D], BF16, kind="Internal").ap()
    xs1 = nc.dram_tensor("xs1", [B_core, S_seq, D], F32, kind="Internal").ap()
    xs2 = nc.dram_tensor("xs2", [B_core, S_seq, D], F32, kind="Internal").ap()

    sch = Sched(nc, ["sC", "sW", "sLA", "sLB", "sOA", "sOB", "sO", "sT", "sV", "sS", "sG"])
    E = sch.emit

    def gtile(g):
        return g // GPERSEQ, (g % GPERSEQ) * 128

    with ExitStack() as top:
        sch.make_sems(top)
        sb = lambda name, shape, dt=F32: top.enter_context(nc.sbuf_tensor(name, shape, dt))

        ident_f = sb("ident_f", [128, 128], F32)
        ident_b = sb("ident_b", [128, 128], BF16)
        ones_row = sb("ones_row", [1, 128], F32)
        onescol = sb("onescol", [128, 1], F32)
        c256 = sb("c256", [128, 1], F32)
        c16 = sb("c16", [128, 1], F32)
        coef_fm = sb("coef_fm", [128, 8], F32)
        ba_fm = sb("ba_fm", [128, 8], F32)
        bx_fm = sb("bx_fm", [128, 8], F32)
        carry = sb("carry", [128, 8], F32)
        b1c = sb("b1c", [128, F], BF16)
        b2h = sb("b2h", [128, D], BF16)
        wM1 = sb("wM1", [128, 1], F32)
        wM2h = sb("wM2h", [128, 1], F32)
        sw1b = sb("sw1b", [128, 1], F32)
        sw2b = sb("sw2b", [128, 1], F32)
        NC_ = 21
        colp = sb("colp", [128, 3 * NC_], F32)

        with ExitStack() as ph:
            psb = lambda name, shape, dt=F32: ph.enter_context(nc.sbuf_tensor(name, shape, dt))
            wld = [psb("p0_wld0", [128, F]), psb("p0_wld1", [128, F])]
            zw = psb("p0_zw", [128, F])
            qw = psb("p0_qw", [128, F], I32)
            tern = psb("p0_tern", [128, F], BF16)
            twbf = psb("p0_twbf", [128, F], BF16)
            acc32 = psb("p0_acc", [128, 64])
            acc8 = psb("p0_acc8", [128, 32])
            corrw = psb("p0_corrw", [128, F], BF16)
            corrw2 = psb("p0_corrw2", [128, F], BF16)
            stg = [psb("p0_stg0", [128, 128], BF16), psb("p0_stg1", [128, 128], BF16)]
            rowbuf = psb("p0_row", [1, F])
            vec8 = psb("p0_vec8", [8, 128])
            cell = psb("p0_cell", [1, 4])
            pst = [ph.enter_context(nc.psum_tensor("p0_pst0", [128, 1024], BF16)),
                   ph.enter_context(nc.psum_tensor("p0_pst1", [128, 1024], BF16))]
            ps_bc = ph.enter_context(nc.psum_tensor("p0_bc", [128, 512], F32))

            with nc.Block("phase0") as blk:
                E("vector", lambda v: v.memset(ones_row[:], 1.0), inc=("sV", 1))
                E("vector", lambda v: v.memset(onescol[:], 1.0), inc=("sV", 1))
                E("vector", lambda v: v.memset(c256[:], 256.0), inc=("sV", 1))
                E("vector", lambda v: v.memset(c16[:], 16.0), inc=("sV", 1))
                E("vector", lambda v: v.memset(carry[:], 0.0), inc=("sV", 1))
                tconst = sch.tok("sV")
                E("sync", lambda q: q.dma_start(ident_f[:], id_f_in), inc=("sC", 16))
                E("sync", lambda q: q.dma_start(ident_b[:], id_b_in), inc=("sC", 16))
                tid = sch.tok("sC")
                for vin, vdst in ((lam_in, coef_fm), (ba_in, ba_fm), (bx_in, bx_fm)):
                    E("sync", lambda q, vin=vin: q.dma_start(
                        vec8[:], vin.rearrange("(c p) -> c p", p=128)),
                      waits=[sch.tok("sT")], inc=("sC", 16))
                    E("tensor", lambda t: t.transpose(ps_bc[:, 0:8], vec8[:],
                                                      ident_f[0:8, 0:8]),
                      waits=[sch.tok("sC"), sch.tok("sV")], inc=("sT", 1))
                    E("vector", lambda v, vdst=vdst: v.tensor_copy(vdst[:], ps_bc[:, 0:8]),
                      waits=[sch.tok("sT")], inc=("sV", 1))
                E("sync", lambda q: q.dma_start(rowbuf[0:1, 0:F], b1_in[None, :]),
                  waits=[sch.tok("sT")], inc=("sC", 16))
                tq6 = sch.tok("sC")
                E("scalar", lambda s: s.activation(coef_fm[:], coef_fm[:], AF.Exp, scale=-1.0),
                  waits=[tq6, sch.tok("sV")], inc=("sS", 1))
                E("scalar", lambda s: s.activation(coef_fm[:], coef_fm[:], AF.Ln, bias=1.0),
                  inc=("sS", 1))
                E("vector", lambda v: v.tensor_single_scalar(coef_fm[:], coef_fm[:], -C_GRIF,
                                                             op=ALU.mult),
                  waits=[sch.tok("sS")], inc=("sV", 1))

                for s5 in range(F // 512):
                    sl = slice(s5 * 512, (s5 + 1) * 512)
                    E("tensor", lambda t, sl=sl: t.matmul(ps_bc[:], ones_row[:],
                                                          rowbuf[0:1, sl], start=True, stop=True),
                      waits=[tq6, tconst, sch.tok("sV")], inc=("sT", 1))
                    E("vector", lambda v, sl=sl: v.tensor_copy(b1c[:, sl], ps_bc[:]),
                      waits=[sch.tok("sT")], inc=("sV", 1))
                E("sync", lambda q: q.dma_start(rowbuf[0:1, 0:D], b2_in[None, :]),
                  waits=[sch.tok("sV")], inc=("sC", 16))
                tq7 = sch.tok("sC")
                for s5 in range(D // 512):
                    sl = slice(s5 * 512, (s5 + 1) * 512)
                    E("tensor", lambda t, sl=sl: t.matmul(ps_bc[:], ones_row[:],
                                                          rowbuf[0:1, sl], start=True, stop=True),
                      waits=[tq7, sch.tok("sS")], inc=("sT", 1))
                    E("scalar", lambda s, sl=sl: s.activation(b2h[:, sl], ps_bc[:],
                                                              AF.Copy, scale=0.5),
                      waits=[sch.tok("sT")], inc=("sS", 1))

                def mean_abs(w_ap, nrows, ncols, acc_base, out_sw, out_m, deq):
                    nch = nrows // 128
                    nsb = ncols // 128
                    for ci in range(nch):
                        wld_t = wld[ci % 2][:, 0:ncols]
                        sl_ = "sLA" if ci % 2 == 0 else "sLB"
                        w = [sch.tok("sS"), sch.tok("sV")]
                        E("sync", lambda q, ci=ci, wld_t=wld_t, w_ap=w_ap: q.dma_start(
                            wld_t, w_ap[ci * 128:(ci + 1) * 128, :]), waits=w, inc=(sl_, 16))
                        tq = sch.tok(sl_)
                        E("scalar", lambda s, wld_t=wld_t: s.activation(
                            zw[:, 0:ncols], wld_t, AF.Abs), waits=[tq], inc=("sS", 1))
                        tz = sch.tok("sS")
                        for sb in range(nsb):
                            E("vector", lambda v, sb=sb: v.reduce_sum(
                                acc8[:, sb:sb + 1], zw[:, sb * 128:(sb + 1) * 128],
                                axis=AX.X), waits=[tz] if sb == 0 else [], inc=("sV", 1))
                        E("vector", lambda v, ci=ci: v.reduce_sum(
                            acc32[:, acc_base + ci:acc_base + ci + 1], acc8[:, 0:nsb],
                            axis=AX.X), inc=("sV", 1))
                    E("vector", lambda v: v.reduce_sum(
                        colp[:, 0:1], acc32[:, acc_base:acc_base + nch], axis=AX.X),
                      waits=[sch.tok("sS")], inc=("sV", 1))
                    E("tensor", lambda t: t.matmul(ps_bc[0:1, 0:1], colp[:, 0:1], onescol[:],
                                                   start=True, stop=True),
                      waits=[sch.tok("sV")], inc=("sT", 1))
                    E("vector", lambda v: v.tensor_copy(cell[0:1, 0:1], ps_bc[0:1, 0:1]),
                      waits=[sch.tok("sT")], inc=("sV", 1))
                    E("vector", lambda v: v.tensor_single_scalar(
                        cell[0:1, 1:2], cell[0:1, 0:1], 1.0 / (nrows * ncols), op=ALU.mult),
                      inc=("sV", 1))
                    E("vector", lambda v: v.tensor_single_scalar(
                        cell[0:1, 1:2], cell[0:1, 1:2], EPS, op=ALU.max), inc=("sV", 1))
                    E("vector", lambda v: v.reciprocal(cell[0:1, 2:3], cell[0:1, 1:2]),
                      inc=("sV", 1))
                    tv2 = sch.tok("sV")
                    E("tensor", lambda t: t.matmul(ps_bc[:, 0:1], ones_row[:], cell[0:1, 2:3],
                                                   start=True, stop=True),
                      waits=[tv2], inc=("sT", 1))
                    E("vector", lambda v, out_sw=out_sw: v.tensor_copy(out_sw, ps_bc[:, 0:1]),
                      waits=[sch.tok("sT")], inc=("sV", 1))
                    E("tensor", lambda t: t.matmul(ps_bc[:, 0:1], ones_row[:], cell[0:1, 1:2],
                                                   start=True, stop=True),
                      waits=[sch.tok("sV")], inc=("sT", 1))
                    E("scalar", lambda s, out_m=out_m, deq=deq: s.activation(
                        out_m, ps_bc[:, 0:1], AF.Copy, scale=deq),
                      waits=[sch.tok("sT")], inc=("sS", 1))

                mean_abs(w1_in, F, D, 0, sw1b[:], wM1[:], 1.0 / 127.0)
                mean_abs(w2_in, D, F, 40, sw2b[:], wM2h[:], 0.5 / 127.0)

                def quant_transpose(w_ap, nrows, ncols, swcol, out_T, do_quant, pctr=[0]):
                    nch = nrows // 128
                    npc = ncols // 128
                    for ci in range(nch):
                        wld_t = wld[ci % 2][:, 0:ncols]
                        sl_ = "sLA" if ci % 2 == 0 else "sLB"
                        w = [sch.tok("sT"), sch.tok("sS"), sch.tok("sV")]
                        E("sync", lambda q, ci=ci, wld_t=wld_t, w_ap=w_ap: q.dma_start(
                            wld_t, w_ap[ci * 128:(ci + 1) * 128, :]), waits=w, inc=(sl_, 16))
                        tq = sch.tok(sl_)
                        if do_quant:
                            E("scalar", lambda s, wld_t=wld_t, swcol=swcol: s.activation(
                                zw[:, 0:ncols], wld_t, AF.Identity, scale=swcol),
                              waits=[tq, sch.tok("sV")], inc=("sS", 1))
                            E("scalar", lambda s: s.copy(qw[:, 0:ncols], zw[:, 0:ncols]),
                              inc=("sS", 1))
                            tqq = sch.tok("sS")
                            E("vector", lambda v: v.scalar_tensor_tensor(
                                corrw[:, 0:ncols], zw[:, 0:ncols], -0.5, qw[:, 0:ncols],
                                op0=ALU.add, op1=ALU.is_ge), waits=[tqq], inc=("sV", 1))
                            E("vector", lambda v: v.scalar_tensor_tensor(
                                corrw2[:, 0:ncols], zw[:, 0:ncols], 0.5, qw[:, 0:ncols],
                                op0=ALU.add, op1=ALU.is_le), inc=("sV", 1))
                            E("vector", lambda v: v.scalar_tensor_tensor(
                                zw[:, 0:ncols], corrw[:, 0:ncols], 1.0, qw[:, 0:ncols],
                                op0=ALU.mult, op1=ALU.add), inc=("sV", 1))
                            E("vector", lambda v: v.tensor_sub(
                                zw[:, 0:ncols], zw[:, 0:ncols], corrw2[:, 0:ncols]),
                              inc=("sV", 1))
                            E("vector", lambda v: v.tensor_scalar(
                                tern[:, 0:ncols], zw[:, 0:ncols], 1.0, -1.0,
                                op0=ALU.min, op1=ALU.max),
                              waits=[sch.tok("sT")], inc=("sV", 1))
                            src = tern
                        else:
                            E("vector", lambda v, wld_t=wld_t: v.tensor_copy(
                                twbf[:, 0:ncols], wld_t),
                              waits=[tq, sch.tok("sT")], inc=("sV", 1))
                            src = twbf
                        tsrc = sch.tok("sV")
                        for pi in range(npc):
                            gp = pctr[0]; pctr[0] += 1
                            pr = gp % 2
                            so_ = "sOA" if pr == 0 else "sOB"
                            wtr = [tsrc]
                            if pi >= 2:
                                wtr.append(sch.tok("sV"))
                            E("tensor", lambda t, pi=pi, pr=pr, src=src: t.transpose(
                                pst[pr][:, 0:128], src[:, pi * 128:(pi + 1) * 128], ident_b[:]),
                              waits=wtr, inc=("sT", 1))
                            tt = sch.tok("sT")
                            wev = [tt]
                            if gp >= 2:
                                wev.append(sch.tok(so_))
                            E("vector", lambda v, pr=pr: v.tensor_copy(stg[pr][:],
                                                                       pst[pr][:, 0:128]),
                              waits=wev, inc=("sV", 1))
                            tv = sch.tok("sV")
                            E("sync", lambda q, ci=ci, pi=pi, pr=pr, out_T=out_T: q.dma_start(
                                out_T[pi * 128:(pi + 1) * 128, ci * 128:(ci + 1) * 128],
                                stg[pr][:]),
                              waits=[tv], inc=(so_, 16))

                quant_transpose(w1_in, F, D, sw1b[:], wq1T, True)
                quant_transpose(w2_in, D, F, sw2b[:], wq2T, True)
                quant_transpose(wa_in, D, D, None, wqaT, False)
                quant_transpose(wx_in, D, D, None, wqxT, False)
                sch.flush(blk)

        tok_p0a = sch.tok("sOA")
        tok_p0b = sch.tok("sOB")

        def ffn_phase(name, src_ap, dst_ap, src_tok_fn, store_toks_out, entry):
            NT = NTOK // 128
            with ExitStack() as ph:
                psb = lambda n, shape, dt=F32: ph.enter_context(
                    nc.sbuf_tensor(name + n, shape, dt))
                w1T = psb("_w1T", [128, 8 * F], BF16)
                w2T = psb("_w2T", [128, 32 * D], BF16)
                xb = [psb(f"_xb{i}", [128, D]) for i in range(3)]
                rb1 = psb("_rb1", [128, D], BF16)
                r1T = psb("_r1T", [128, 2048], BF16)
                hbuf = psb("_h", [128, F])
                zq = [psb(f"_zq{i}", [128, 1024]) for i in range(2)]
                qi = [psb(f"_qi{i}", [128, 1024], mybir.dt.int16) for i in range(2)]
                gbuf = [psb(f"_g{i}", [128, F], BF16) for i in range(2)]
                gstg2_ = psb("_gs2", [128, 1024], BF16)
                gstg2 = [gstg2_[:, 0:512], gstg2_[:, 512:1024]]
                bnh = psb("_bnh", [128, 48])
                bnb = bnh
                sqm2 = psb("_sqm2", [128, 16])
                sqm = sqm2
                pm = [ph.enter_context(nc.psum_tensor(f"{name}_pm{i}", [128, 512], F32))
                      for i in range(2)]
                po = [ph.enter_context(nc.psum_tensor(f"{name}_po{i}", [128, 512], F32))
                      for i in range(2)]
                ptr2 = ph.enter_context(nc.psum_tensor(name + "_pt2", [128, 4096], BF16))

                with nc.Block(name) as blk:
                    for k in range(8):
                        E("sync", lambda q, k=k: q.dma_start(
                            w1T[:, k * F:(k + 1) * F], wq1T[k * 128:(k + 1) * 128, :]),
                          waits=[tok_p0a, tok_p0b] + entry, inc=("sW", 16))
                    for fc in range(32):
                        E("sync", lambda q, fc=fc: q.dma_start(
                            w2T[:, fc * D:(fc + 1) * D], wq2T[fc * 128:(fc + 1) * 128, :]),
                          waits=[tok_p0a, tok_p0b] + entry, inc=("sW", 16))
                    tok_w = sch.tok("sW")

                    tiles = {}
                    st = dict(pm={}, po_ev=None, ptr={}, gstg={})

                    def emit_load(j):
                        if j >= NT:
                            return
                        sl_ = "sLA" if j % 2 == 0 else "sLB"
                        waits = [tok_w, src_tok_fn(j)]
                        if j >= 3:
                            waits.append(tiles[j - 3]["stored"])
                        if j >= 2:
                            waits.append(tiles[j - 2]["load"])
                        b, t = gtile(j)
                        E("sync", lambda q, j=j, b=b, t=t: q.dma_start(
                            xb[j % 3][:], src_ap[b, t:t + 128, :]),
                          waits=waits, inc=(sl_, 16))
                        tiles.setdefault(j, {})["load"] = sch.tok(sl_)

                    def stA(j):
                        if j >= NT:
                            return
                        tj = tiles.setdefault(j, {})
                        cb = NC_ * (j % 3)
                        col = lambda i: colp[:, cb + i:cb + i + 1]
                        tj["col"] = col
                        t2 = tiles.get(j - 2, {})
                        t3 = tiles.get(j - 3, {})
                        xt = xb[j % 3][:]
                        E("vector", lambda v: v.reduce_sum(col(0), xt, axis=AX.X),
                          waits=[tj["load"], t3.get("colfree")], inc=("sV", 1))
                        E("vector", lambda v: v.tensor_single_scalar(
                            col(3), col(0), 1.0 / 1024.0, op=ALU.mult), inc=("sV", 1))
                        E("vector", lambda v: v.reduce_max(col(12), xt, axis=AX.X),
                          inc=("sV", 1))
                        E("vector", lambda v: v.tensor_reduce(col(13), xt, axis=AX.X,
                                                              op=ALU.min), inc=("sV", 1))
                        E("vector", lambda v: v.tensor_sub(col(14), col(12), col(3)),
                          inc=("sV", 1))
                        E("vector", lambda v: v.tensor_sub(col(15), col(3), col(13)),
                          inc=("sV", 1))
                        E("vector", lambda v: v.tensor_max(col(14), col(14), col(15)),
                          inc=("sV", 1))
                        E("vector", lambda v: v.tensor_single_scalar(
                            col(15), col(14), 1e-30, op=ALU.max), inc=("sV", 1))
                        E("vector", lambda v: v.reciprocal(col(15), col(15)), inc=("sV", 1))
                        E("vector", lambda v: v.tensor_single_scalar(
                            col(18), col(15), 127.0, op=ALU.mult), inc=("sV", 1))
                        E("vector", lambda v: v.scalar_tensor_tensor(
                            col(19), col(3), -1.0, col(18), op0=ALU.mult, op1=ALU.mult),
                          inc=("sV", 1))
                        E("vector", lambda v: v.tensor_single_scalar(col(19), col(19), 256.0,
                                                                     op=ALU.add), inc=("sV", 1))
                        tcols = sch.tok("sV")
                        tj["tcols"] = tcols
                        E("vector", lambda v: v.tensor_scalar(
                            zq[0][:], xt, col(18), col(19), op0=ALU.mult, op1=ALU.add),
                          waits=[tcols, t2.get("zq0free"),
                                 tiles.get(j - 1, {}).get("r1")], inc=("sV", 1))
                        tz1 = sch.tok("sV")
                        E("scalar", lambda s: s.copy(qi[0][:], zq[0][:]), waits=[tz1],
                          inc=("sS", 1))
                        tq1 = sch.tok("sS")
                        E("vector", lambda v: v.scalar_tensor_tensor(
                            zq[0][:], zq[0][:], -0.5, qi[0][:], op0=ALU.add, op1=ALU.is_ge),
                          waits=[tq1], inc=("sV", 1))
                        E("vector", lambda v: v.scalar_tensor_tensor(
                            rb1[:], zq[0][:], -256.0, qi[0][:], op0=ALU.add, op1=ALU.add),
                          waits=[tiles.get(j - 1, {}).get("tr1_done")], inc=("sV", 1))
                        tj["r1"] = sch.tok("sV")
                        E("vector", lambda v: v.bn_stats(bnb[:, 0:6], xt[:, 0:512]),
                          waits=[t2.get("sqmS")], inc=("sV", 1))
                        E("vector", lambda v: v.bn_stats(bnb[:, 6:12], xt[:, 512:1024]),
                          inc=("sV", 1))
                        tbn = sch.tok("sV")
                        E("vector", lambda v: v.tensor_tensor(
                            sqm[:, 0:4], sview(bnb, 1, 3, 4, 48), sview(bnb, 1, 3, 4, 48),
                            op=ALU.mult), inc=("sV", 1))
                        E("vector", lambda v: v.reduce_sum(col(1), sview(bnb, 2, 3, 4, 48),
                                                           axis=AX.X), inc=("sV", 1))
                        E("vector", lambda v: v.reduce_sum(col(2), sqm[:, 0:4], axis=AX.X),
                          inc=("sV", 1))
                        tj["sqmS"] = sch.tok("sV")
                        E("vector", lambda v: v.tensor_single_scalar(col(4), col(2), 0.25,
                                                                     op=ALU.mult), inc=("sV", 1))
                        E("vector", lambda v: v.scalar_tensor_tensor(
                            col(5), col(1), 1.0 / 1024.0, col(4), op0=ALU.mult, op1=ALU.add),
                          inc=("sV", 1))
                        E("vector", lambda v: v.tensor_mul(col(6), col(3), col(3)), inc=("sV", 1))
                        E("vector", lambda v: v.tensor_sub(col(7), col(5), col(6)), inc=("sV", 1))
                        E("vector", lambda v: v.tensor_single_scalar(col(8), col(7), EPS,
                                                                     op=ALU.add), inc=("sV", 1))
                        E("vector", lambda v: v.scalar_tensor_tensor(
                            col(9), col(8), EPS, col(7), op0=ALU.mult, op1=ALU.add),
                          inc=("sV", 1))
                        tv_ab = sch.tok("sV")
                        E("scalar", lambda s: s.activation(col(10), col(9), AF.Sqrt),
                          waits=[tv_ab], inc=("sS", 1))
                        E("vector", lambda v: v.reciprocal(col(11), col(10)),
                          waits=[sch.tok("sS")], inc=("sV", 1))
                        E("vector", lambda v: v.tensor_mul(col(16), col(11), col(14)),
                          inc=("sV", 1))
                        E("vector", lambda v: v.tensor_single_scalar(col(16), col(16), EPS,
                                                                     op=ALU.max), inc=("sV", 1))
                        E("vector", lambda v: v.tensor_mul(col(20), col(16), wM1[:]),
                          inc=("sV", 1))
                        tj["c1"] = sch.tok("sV")

                    def stTr1(j):
                        if j >= NT:
                            return
                        tj = tiles[j]
                        for q4 in range(2):
                            half = q4 % 2
                            for ci in range(4):
                                c = q4 * 4 + ci
                                E("tensor", lambda t, c=c, half=half, ci=ci: t.transpose(
                                    ptr2[:, half * 2048 + ci * 512: half * 2048 + ci * 512 + 128],
                                    rb1[:, c * 128:(c + 1) * 128], ident_b[:]),
                                  waits=([tj["r1"]] if ci == 0 else []) +
                                        ([st["ptr"].get(half)] if ci == 0 else []),
                                  inc=("sT", 1))
                            ttr = sch.tok("sT")
                            psrc = bass.AP(ptr2, half * 2048, [[4096, 128], [512, 4], [1, 128]])
                            dst = bass.AP(r1T, (j % 2) * 128 + q4 * 1024,
                                          [[2048, 128], [256, 4], [1, 128]])
                            tok = E("scalar", lambda s, dst=dst, psrc=psrc: s.copy(dst, psrc),
                                    waits=[ttr], inc=("sS", 1))
                            st["ptr"][half] = tok
                        tj["r1T_evac"] = sch.tok("sS")
                        tj["tr1_done"] = sch.tok("sT")

                    def stB1(j):
                        tj = tiles[j]
                        tprev = tiles.get(j - 1, {})
                        col = tj["col"]
                        tj["ev1"] = {}
                        gelu_toks = []
                        for f in range(8):
                            bank = pm[f % 2]
                            wmm = [tj["r1T_evac"], st["pm"].get(f % 2)]
                            if f == 0:
                                wmm.append(tok_w)
                            for k in range(8):
                                E("tensor", lambda t, f=f, k=k, bank=bank, j=j: t.matmul(
                                    bank[:],
                                    r1T[:, k * 256 + (j % 2) * 128: k * 256 + (j % 2) * 128 + 128],
                                    w1T[:, k * F + f * 512: k * F + (f + 1) * 512],
                                    start=(k == 0), stop=(k == 7)),
                                  waits=wmm if k == 0 else [], inc=("sT", 1))
                            tmm = sch.tok("sT")
                            wev = [tmm, tj["c1"]]
                            if f == 0:
                                wev.append(tprev.get("hfree"))
                            tok = E("vector", lambda v, f=f, bank=bank, col=col:
                                    v.scalar_tensor_tensor(
                                hbuf[:, f * 512:(f + 1) * 512], bank[:], col(20),
                                b1c[:, f * 512:(f + 1) * 512], op0=ALU.mult, op1=ALU.add),
                              waits=wev, inc=("sV", 1))
                            st["pm"][f % 2] = tok
                            tj["ev1"][f] = tok
                            if f == 3 or f == 7:
                                h0 = 0 if f == 3 else 2048
                                E("scalar", lambda s, h0=h0: s.activation(
                                    hbuf[:, h0:h0 + 2048], hbuf[:, h0:h0 + 2048], AF.Gelu),
                                  waits=[tok], inc=("sS", 1))
                                gelu_toks.append(sch.tok("sS"))
                        tj["gelu"] = gelu_toks
                        tj["mm1_done"] = sch.tok("sT")

                    def stC(j):
                        tj = tiles[j]
                        tnext = tiles.get(j + 1, {})
                        t2 = tiles.get(j - 2, {})
                        col = tj["col"]
                        E("vector", lambda v: v.reduce_max(col(0), hbuf[:], axis=AX.X,
                                                           apply_absolute_value=True),
                          waits=[tj["gelu"][1]], inc=("sV", 1))
                        E("vector", lambda v: v.tensor_single_scalar(
                            col(1), col(0), 1e-30, op=ALU.max), inc=("sV", 1))
                        E("vector", lambda v: v.reciprocal(col(1), col(1)), inc=("sV", 1))
                        E("vector", lambda v: v.tensor_single_scalar(
                            col(12), col(1), 127.0, op=ALU.mult), inc=("sV", 1))
                        tsig2 = sch.tok("sV")
                        tj["r2"] = []
                        for qc in range(4):
                            p2 = qc % 2
                            wz = [tsig2]
                            if qc == 0:
                                wz += [tnext.get("r1"), t2.get("zq0free2")]
                            elif qc == 1:
                                wz.append(t2.get("zq1free"))
                            else:
                                wz.append(tj["r2"][qc - 2])
                            E("vector", lambda v, qc=qc, p2=p2, col=col: v.tensor_scalar(
                                zq[p2][:], hbuf[:, qc * 1024:(qc + 1) * 1024], col(12), 256.0,
                                op0=ALU.mult, op1=ALU.add), waits=wz, inc=("sV", 1))
                            tz2 = sch.tok("sV")
                            wq_ = [tz2] + ([tnext.get("r1")] if qc == 0 else [])
                            E("scalar", lambda s, p2=p2: s.copy(qi[p2][:], zq[p2][:]),
                              waits=wq_, inc=("sS", 1))
                            tq2 = sch.tok("sS")
                            E("vector", lambda v, p2=p2: v.scalar_tensor_tensor(
                                zq[p2][:], zq[p2][:], -0.5, qi[p2][:],
                                op0=ALU.add, op1=ALU.is_ge), waits=[tq2], inc=("sV", 1))
                            tok = E("vector", lambda v, qc=qc, p2=p2, j=j: v.scalar_tensor_tensor(
                                gbuf[j % 2][:, qc * 1024:(qc + 1) * 1024], zq[p2][:], -256.0,
                                qi[p2][:], op0=ALU.add, op1=ALU.add),
                              waits=[t2.get("trg_done")] if qc == 0 else [], inc=("sV", 1))
                            tj["r2"].append(tok)
                        tj["zq0free2"] = tj["r2"][2]
                        tj["zq0free"] = tj["r2"][2]
                        tj["zq1free"] = tj["r2"][3]
                        tj["hfree"] = sch.tok("sV")
                        for i8 in range(8):
                            w = []
                            if i8 == 0:
                                w = [tj["gelu"][1], t2.get("sqm2V")]
                            E("vector", lambda v, i8=i8: v.bn_stats(
                                bnh[:, i8 * 6:(i8 + 1) * 6],
                                hbuf[:, i8 * 512:(i8 + 1) * 512]), waits=w, inc=("sV", 1))
                        E("vector", lambda v: v.tensor_tensor(
                            sqm2[:, 0:16], sview(bnh, 1, 3, 16, 48), sview(bnh, 1, 3, 16, 48),
                            op=ALU.mult), inc=("sV", 1))
                        E("vector", lambda v: v.reduce_sum(col(2), sview(bnh, 2, 3, 16, 48),
                                                           axis=AX.X), inc=("sV", 1))
                        E("vector", lambda v: v.reduce_sum(col(4), sqm2[:, 0:16], axis=AX.X),
                          inc=("sV", 1))
                        tj["sqm2V"] = sch.tok("sV")
                        E("vector", lambda v: v.tensor_single_scalar(col(4), col(4), 1.0 / 16,
                                                                     op=ALU.mult), inc=("sV", 1))
                        E("vector", lambda v: v.scalar_tensor_tensor(
                            col(5), col(2), 1.0 / F, col(4), op0=ALU.mult, op1=ALU.add),
                          inc=("sV", 1))
                        E("vector", lambda v: v.tensor_single_scalar(col(5), col(5), EPS,
                                                                     op=ALU.add), inc=("sV", 1))
                        tv_eh = sch.tok("sV")
                        E("scalar", lambda s: s.activation(col(2), col(5), AF.Sqrt),
                          waits=[tv_eh], inc=("sS", 1))
                        E("vector", lambda v: v.reciprocal(col(2), col(2)),
                          waits=[sch.tok("sS")], inc=("sV", 1))
                        E("vector", lambda v: v.tensor_mul(col(13), col(2), col(0)),
                          inc=("sV", 1))
                        E("vector", lambda v: v.tensor_single_scalar(col(13), col(13), EPS,
                                                                     op=ALU.max), inc=("sV", 1))
                        E("vector", lambda v: v.tensor_mul(col(13), col(13), wM2h[:]),
                          inc=("sV", 1))
                        tj["c2"] = sch.tok("sV")
                        tj["colfree"] = sch.tok("sS")

                    def stB2(j):
                        if j < 0:
                            return
                        tj = tiles[j]
                        col = tj["col"]
                        for q4 in range(8):
                            half = q4 % 2
                            for ci in range(4):
                                fc = q4 * 4 + ci
                                E("tensor", lambda t, fc=fc, half=half, ci=ci, j=j: t.transpose(
                                    ptr2[:, half * 2048 + ci * 512: half * 2048 + ci * 512 + 128],
                                    gbuf[j % 2][:, fc * 128:(fc + 1) * 128], ident_b[:]),
                                  waits=([tj["r2"][fc // 8]] if ci == 0 else []) +
                                        ([st["ptr"].get(half)] if ci == 0 else []),
                                  inc=("sT", 1))
                            ttr = sch.tok("sT")
                            psrc = bass.AP(ptr2, half * 2048, [[4096, 128], [512, 4], [1, 128]])
                            tok = E("scalar", lambda s, psrc=psrc, half=half: s.copy(
                                gstg2[half][:], psrc),
                                waits=[ttr, st["gstg"].get(half)], inc=("sS", 1))
                            st["ptr"][half] = tok
                            wmm2 = [tok]
                            if q4 == 0:
                                wmm2 += [st["po_ev"], tok_w]
                            for pi in range(4):
                                fc = q4 * 4 + pi
                                for n in range(2):
                                    E("tensor", lambda t, fc=fc, n=n, pi=pi, half=half: t.matmul(
                                        po[n][:], gstg2[half][:, pi * 128:(pi + 1) * 128],
                                        w2T[:, fc * D + n * 512: fc * D + (n + 1) * 512],
                                        start=(fc == 0), stop=(fc == 31)),
                                      waits=wmm2 if (pi == 0 and n == 0) else [],
                                      inc=("sT", 1))
                            st["gstg"][half] = sch.tok("sT")
                        tj["trg_done"] = sch.tok("sT")
                        tmm2 = tj["trg_done"]
                        xt = xb[j % 3][:]
                        for n in range(2):
                            E("vector", lambda v, n=n, xt=xt, col=col: v.scalar_tensor_tensor(
                                xt[:, n * 512:(n + 1) * 512], po[n][:], col(13),
                                xt[:, n * 512:(n + 1) * 512], op0=ALU.mult, op1=ALU.add),
                              waits=[tmm2, tj["c2"]] if n == 0 else [], inc=("sV", 1))
                        st["po_ev"] = sch.tok("sV")
                        E("vector", lambda v, xt=xt: v.tensor_add(xt, xt, b2h[:]),
                          inc=("sV", 1))
                        tout = sch.tok("sV")
                        b, t = gtile(j)
                        E("sync", lambda q, j=j, b=b, t=t: q.dma_start(
                            dst_ap[b, t:t + 128, :], xb[j % 3][:]),
                          waits=[tout], inc=("sO", 16))
                        tj["stored"] = sch.tok("sO")
                        store_toks_out.append(tj["stored"])

                    emit_load(0)
                    emit_load(1)
                    stA(0)
                    stTr1(0)
                    for j in range(NT):
                        stA(j + 1)
                        stB1(j)
                        stB2(j - 1)
                        emit_load(j + 2)
                        stTr1(j + 1)
                        stC(j)
                    stB2(NT - 1)
                    sch.flush(blk)

        def griffin_phase(src_ap, dst_ap, src_store_toks, store_toks_out, entry):
            with ExitStack() as ph:
                psb = lambda n, shape, dt=F32: ph.enter_context(
                    nc.sbuf_tensor("g2" + n, shape, dt))
                waT = psb("_waT", [128, 8 * D], BF16)
                wxT = psb("_wxT", [128, 8 * D], BF16)
                xb = [psb("_xb0", [128, 4 * D]), psb("_xb1", [128, 4 * D])]
                xn_tm = psb("_xn", [128, D])
                xnT_f = psb("_xnTf", [128, 4096])
                xnT_b = psb("_xnTb", [128, 4096], BF16)
                a_fm = psb("_a", [128, 4096])
                b_fm = psb("_b", [128, 4096])
                it_fm = psb("_it", [128, 4096])
                rt_tmp = psb("_rt", [128, 512])
                tsc = psb("_tsc", [128, 512])
                pbuf = psb("_pb", [128, 512])
                bnb = psb("_bnb2", [128, 12])
                sqm = psb("_sqm2", [128, 4])
                pg = [ph.enter_context(nc.psum_tensor(f"g2_pg{i}", [128, 512], F32))
                      for i in range(4)]
                ptf = [ph.enter_context(nc.psum_tensor(f"g2_pt{i}", [128, 512], F32))
                       for i in range(2)]

                with nc.Block("phase2") as blk:
                    for k in range(8):
                        E("sync", lambda q, k=k: q.dma_start(
                            waT[:, k * D:(k + 1) * D], wqaT[k * 128:(k + 1) * 128, :]),
                          waits=[tok_p0a, tok_p0b] + entry, inc=("sW", 16))
                        E("sync", lambda q, k=k: q.dma_start(
                            wxT[:, k * D:(k + 1) * D], wqxT[k * 128:(k + 1) * 128, :]),
                          waits=[tok_p0a, tok_p0b], inc=("sW", 16))
                    tok_w = sch.tok("sW")

                    load_toks = {}
                    NT2 = B_core * NST2_SEQ

                    def emit_load(ti):
                        par = ti % 2
                        sl_ = "sLA" if par == 0 else "sLB"
                        waits = [tok_w]
                        if ti >= 2:
                            waits.append(store_toks_out[4 * (ti - 2) + 3])
                        if src_store_toks:
                            waits.append(src_store_toks[-1])
                        for m in range(4):
                            b, t = gtile(4 * ti + m)
                            E("sync", lambda q, par=par, m=m, b=b, t=t: q.dma_start(
                                xb[par][:, m * D:(m + 1) * D], src_ap[b, t:t + 128, :]),
                              waits=waits if m == 0 else [], inc=(sl_, 16))
                        load_toks[ti] = sch.tok(sl_)

                    emit_load(0)
                    if NT2 > 1:
                        emit_load(1)

                    st2 = dict(ptf={}, xn_tr=None, pg_read={}, sguard={}, scan=None,
                               b_done=None)

                    g2m = os.environ.get("ATH_G2", "lmbsy")
                    for ti in range(NT2):
                        par = ti % 2
                        jseq = ti % NST2_SEQ
                        tload = load_toks[ti]
                        for m in range(4):
                            cb = NC_ * (m % 2)
                            col = lambda i: colp[:, cb + i:cb + i + 1]
                            xt = xb[par][:, m * D:(m + 1) * D]
                            E("vector", lambda v, xt=xt: v.bn_stats(bnb[:, 0:6], xt[:, 0:512]),
                              waits=[tload, st2["sguard"].get(m % 2)], inc=("sV", 1))
                            E("vector", lambda v, xt=xt: v.bn_stats(bnb[:, 6:12],
                                                                    xt[:, 512:1024]),
                              inc=("sV", 1))
                            tbn = sch.tok("sV")
                            E("scalar", lambda s: s.activation(
                                sqm[:, 0:4], sview(bnb, 1, 3, 4, 12), AF.Square),
                              waits=[tbn], inc=("sS", 1))
                            tsq = sch.tok("sS")
                            E("vector", lambda v, col=col: v.reduce_sum(
                                col(0), sview(bnb, 1, 3, 4, 12), axis=AX.X),
                              waits=[tbn], inc=("sV", 1))
                            E("vector", lambda v, col=col: v.reduce_sum(
                                col(1), sview(bnb, 2, 3, 4, 12), axis=AX.X), inc=("sV", 1))
                            E("vector", lambda v, col=col: v.reduce_sum(
                                col(2), sqm[:, 0:4], axis=AX.X), waits=[tsq], inc=("sV", 1))
                            E("vector", lambda v, col=col: v.tensor_single_scalar(
                                col(3), col(0), 0.25, op=ALU.mult), inc=("sV", 1))
                            E("vector", lambda v, col=col: v.tensor_single_scalar(
                                col(4), col(2), 0.25, op=ALU.mult), inc=("sV", 1))
                            E("vector", lambda v, col=col: v.scalar_tensor_tensor(
                                col(5), col(1), 1.0 / 1024.0, col(4), op0=ALU.mult,
                                op1=ALU.add), inc=("sV", 1))
                            E("vector", lambda v, col=col: v.tensor_mul(col(6), col(3), col(3)),
                              inc=("sV", 1))
                            E("vector", lambda v, col=col: v.tensor_sub(col(7), col(5), col(6)),
                              inc=("sV", 1))
                            E("vector", lambda v, col=col: v.tensor_single_scalar(
                                col(8), col(7), EPS, op=ALU.add), inc=("sV", 1))
                            tv_a = sch.tok("sV")
                            E("scalar", lambda s, col=col: s.activation(col(9), col(8), AF.Sqrt),
                              waits=[tv_a], inc=("sS", 1))
                            E("vector", lambda v, col=col: v.reciprocal(col(10), col(9)),
                              waits=[sch.tok("sS")], inc=("sV", 1))
                            E("vector", lambda v, col=col: v.scalar_tensor_tensor(
                                col(11), col(3), -1.0, col(10), op0=ALU.mult, op1=ALU.mult),
                              inc=("sV", 1))
                            tv_c = sch.tok("sV")
                            E("scalar", lambda s, xt=xt, col=col: s.activation(
                                xn_tm[:], xt, AF.Identity, bias=col(11), scale=col(10)),
                              waits=[tv_c, st2["xn_tr"]], inc=("sS", 1))
                            txn = sch.tok("sS")
                            for c in range(8):
                                pr = c % 2
                                wtr = [txn] + ([st2["ptf"][pr]] if pr in st2["ptf"] else [])
                                if (pr, "s") in st2["ptf"]:
                                    wtr.append(st2["ptf"][(pr, "s")])
                                E("tensor", lambda t, c=c, pr=pr: t.transpose(
                                    ptf[pr][:, 0:128], xn_tm[:, c * 128:(c + 1) * 128],
                                    ident_f[:]), waits=wtr,
                                  inc=("sT", 1))
                                ttr = sch.tok("sT")
                                cc = c * 512 + m * 128
                                t1 = E("vector", lambda v, pr=pr, cc=cc: v.tensor_copy(
                                    xnT_f[:, cc:cc + 128], ptf[pr][:, 0:128]),
                                  waits=[ttr], inc=("sV", 1))
                                t2 = E("scalar", lambda s, cc=cc: s.copy(
                                    xnT_b[:, cc:cc + 128], xnT_f[:, cc:cc + 128]),
                                  waits=[t1], inc=("sS", 1))
                                st2["ptf"][pr] = t1
                            st2["xn_tr"] = sch.tok("sT")
                            st2["sguard"][m % 2] = sch.tok("sS")
                        txnb = sch.tok("sS")
                        txnf = sch.tok("sV")
                        scan_toks = []
                        for c in (range(8) if "m" in g2m else []):
                            bk_rt = pg[(2 * c) % 4]
                            bk_it = pg[(2 * c + 1) % 4]
                            wmm = [txnb, st2["pg_read"].get((2 * c) % 4),
                                   st2["pg_read"].get((2 * c + 1) % 4),
                                   st2["scan"], tok_w if c == 0 else None]
                            for k in range(8):
                                E("tensor", lambda t, c=c, k=k, bk=bk_rt: t.matmul(
                                    bk[:], waT[:, k * D + c * 128: k * D + (c + 1) * 128],
                                    xnT_b[:, k * 512:(k + 1) * 512],
                                    start=(k == 0), stop=(k == 7)),
                                  waits=wmm if k == 0 else [], inc=("sT", 1))
                            for k in range(8):
                                E("tensor", lambda t, c=c, k=k, bk=bk_it: t.matmul(
                                    bk[:], wxT[:, k * D + c * 128: k * D + (c + 1) * 128],
                                    xnT_b[:, k * 512:(k + 1) * 512],
                                    start=(k == 0), stop=(k == 7)), inc=("sT", 1))
                            tmm = sch.tok("sT")
                            E("scalar", lambda s, c=c, bk=bk_rt: s.activation(
                                rt_tmp[:], bk[:], AF.Sigmoid, bias=ba_fm[:, c:c + 1]),
                              waits=[tmm, st2["b_done"]], inc=("sS", 1))
                            E("scalar", lambda s, c=c: s.activation(
                                a_fm[:, c * 512:(c + 1) * 512], rt_tmp[:], AF.Exp,
                                scale=coef_fm[:, c:c + 1]), inc=("sS", 1))
                            st2["pg_read"][(2 * c) % 4] = sch.tok("sS")
                            E("scalar", lambda s, c=c, bk=bk_it: s.activation(
                                it_fm[:, c * 512:(c + 1) * 512], bk[:], AF.Sigmoid,
                                bias=bx_fm[:, c:c + 1]), inc=("sS", 1))
                            st2["pg_read"][(2 * c + 1) % 4] = sch.tok("sS")
                            ta = sch.tok("sS")
                            if "b" not in g2m:
                                scan_toks.append(sch.tok("sS"))
                                continue
                            E("scalar", lambda s, c=c: s.activation(
                                tsc[:], a_fm[:, c * 512:(c + 1) * 512], AF.Square),
                              waits=[ta], inc=("sS", 1))
                            E("scalar", lambda s: s.activation(
                                tsc[:], tsc[:], AF.Sqrt, bias=1.0, scale=-1.0), inc=("sS", 1))
                            tb_ = sch.tok("sS")
                            E("vector", lambda v, c=c: v.tensor_mul(
                                pbuf[:], it_fm[:, c * 512:(c + 1) * 512],
                                xnT_f[:, c * 512:(c + 1) * 512]),
                              waits=[ta, txnf], inc=("sV", 1))
                            tokb = E("vector", lambda v, c=c: v.tensor_mul(
                                b_fm[:, c * 512:(c + 1) * 512], tsc[:], pbuf[:]),
                              waits=[tb_], inc=("sV", 1))
                            st2["b_done"] = tokb
                            if "s" not in g2m:
                                scan_toks.append(sch.tok("sV"))
                                continue
                            if jseq == 0:
                                E("vector", lambda v, c=c: v.tensor_tensor_scan(
                                    xnT_f[:, c * 512:(c + 1) * 512],
                                    a_fm[:, c * 512:(c + 1) * 512],
                                    b_fm[:, c * 512:(c + 1) * 512],
                                    0.0, op0=ALU.mult, op1=ALU.add), inc=("sV", 1))
                            else:
                                E("vector", lambda v, c=c: v.tensor_tensor_scan(
                                    xnT_f[:, c * 512:(c + 1) * 512],
                                    a_fm[:, c * 512:(c + 1) * 512],
                                    b_fm[:, c * 512:(c + 1) * 512],
                                    carry[:, c:c + 1], op0=ALU.mult, op1=ALU.add),
                                  inc=("sV", 1))
                            E("vector", lambda v, c=c: v.tensor_copy(
                                carry[:, c:c + 1], xnT_f[:, c * 512 + 511:c * 512 + 512]),
                              inc=("sV", 1))
                            scan_toks.append(sch.tok("sV"))
                        if scan_toks:
                            st2["scan"] = scan_toks[-1]
                        for c in (range(8) if ("y" in g2m and "s" in g2m and "m" in g2m)
                                  else []):
                            for m in range(4):
                                pr = (c * 4 + m) % 2
                                wtr = [scan_toks[c]] + ([st2["ptf"][pr]] if pr in st2["ptf"]
                                                        else [])
                                if (pr, "s") in st2["ptf"]:
                                    wtr.append(st2["ptf"][(pr, "s")])
                                cc = c * 512 + m * 128
                                E("tensor", lambda t, pr=pr, cc=cc: t.transpose(
                                    ptf[pr][:, 0:128], xnT_f[:, cc:cc + 128], ident_f[:]),
                                  waits=wtr, inc=("sT", 1))
                                ttr = sch.tok("sT")
                                dd = m * D + c * 128
                                tok = E("vector", lambda v, pr=pr, dd=dd, par=par: v.tensor_add(
                                    xb[par][:, dd:dd + 128], xb[par][:, dd:dd + 128],
                                    ptf[pr][:, 0:128]), waits=[ttr], inc=("sV", 1))
                                st2["ptf"][pr] = tok
                                st2["ptf"].pop((pr, "s"), None)
                        tdone = sch.tok("sV")
                        for m in range(4):
                            b, t = gtile(4 * ti + m)
                            E("sync", lambda q, par=par, m=m, b=b, t=t: q.dma_start(
                                dst_ap[b, t:t + 128, :], xb[par][:, m * D:(m + 1) * D]),
                              waits=[tdone] if m == 0 else [], inc=("sO", 16))
                            store_toks_out.append(sch.tok("sO"))
                        if ti + 2 < NT2:
                            emit_load(ti + 2)
                    sch.flush(blk)

        def entry_toks():
            return [sch.tok("sO"), sch.tok("sV"), sch.tok("sS"), sch.tok("sT"),
                    sch.tok("sLA"), sch.tok("sLB")]

        phases = os.environ.get("ATH_PHASES", "123")
        p1_stores = []
        if "1" in phases:
            ffn_phase("f1", x_in, xs1, lambda st: None, p1_stores, entry_toks())
        p2_stores = []
        if "2" in phases:
            griffin_phase(xs1 if "1" in phases else x_in, xs2,
                          p1_stores, p2_stores, entry_toks())
        if "3" in phases:
            src3 = xs2 if "2" in phases else (xs1 if "1" in phases else x_in)
            p3_stores = []
            ffn_phase("f3", src3, y_out,
                      (lambda st: p2_stores[-1]) if p2_stores else
                      ((lambda st: p1_stores[-1]) if p1_stores else (lambda st: None)),
                      p3_stores, entry_toks())

        with nc.Block("fin") as blk:
            E("sync", lambda q: q.nop(),
              waits=[sch.tok("sO"), sch.tok("sOA"), sch.tok("sOB"), sch.tok("sW"),
                     sch.tok("sLA"), sch.tok("sLB"), sch.tok("sC")])
            sch.flush(blk)

    return nc


_NC_CACHE = {}


def _get_nc(B_core, S_seq):
    key = (B_core, S_seq)
    if key not in _NC_CACHE:
        _NC_CACHE[key] = build_program(B_core, S_seq)
    return _NC_CACHE[key]


def kernel(L0_attn_x, L0_attn_Wa, L0_attn_Wx, L0_attn_ba, L0_attn_bx, L0_attn_Lam,
           L0_ffn_w1, L0_ffn_b1, L0_ffn_w2, L0_ffn_b2):
    x = np.ascontiguousarray(np.asarray(L0_attn_x, dtype=np.float32))
    B, S, _ = x.shape
    B_core = B // NCORES
    nc = _get_nc(B_core, S)
    common = dict(
        L0_attn_Wa=np.ascontiguousarray(np.asarray(L0_attn_Wa, dtype=np.float32)),
        L0_attn_Wx=np.ascontiguousarray(np.asarray(L0_attn_Wx, dtype=np.float32)),
        L0_attn_ba=np.ascontiguousarray(np.asarray(L0_attn_ba, dtype=np.float32)),
        L0_attn_bx=np.ascontiguousarray(np.asarray(L0_attn_bx, dtype=np.float32)),
        L0_attn_Lam=np.ascontiguousarray(np.asarray(L0_attn_Lam, dtype=np.float32)),
        L0_ffn_w1=np.ascontiguousarray(np.asarray(L0_ffn_w1, dtype=np.float32)),
        L0_ffn_b1=np.ascontiguousarray(np.asarray(L0_ffn_b1, dtype=np.float32)),
        L0_ffn_w2=np.ascontiguousarray(np.asarray(L0_ffn_w2, dtype=np.float32)),
        L0_ffn_b2=np.ascontiguousarray(np.asarray(L0_ffn_b2, dtype=np.float32)),
        ident_f32=np.eye(128, dtype=np.float32),
        ident_bf16=np.eye(128).astype(ml_dtypes.bfloat16),
    )
    in_maps = []
    for c in range(NCORES):
        m = dict(common)
        m["L0_attn_x"] = np.ascontiguousarray(x[c * B_core:(c + 1) * B_core])
        in_maps.append(m)
    res = run_bass_kernel_spmd(nc, in_maps, core_ids=list(range(NCORES)))
    return np.concatenate([np.asarray(r["y"], dtype=np.float32) for r in res.results],
                          axis=0)


# version 12
# speedup vs baseline: 1.9986x; 1.0282x over previous
import os
import numpy as np
import ml_dtypes
from contextlib import ExitStack

import concourse.bass as bass
import concourse.mybir as mybir
from concourse.bass_utils import run_bass_kernel_spmd

F32 = mybir.dt.float32
BF16 = mybir.dt.bfloat16
I32 = mybir.dt.int32
AF = mybir.ActivationFunctionType
ALU = mybir.AluOpType
AX = mybir.AxisListType

D = 1024
F = 4096
NCORES = 8
EPS = 1e-5
C_GRIF = 8.0

ENGINES = ("sync", "tensor", "vector", "scalar", "gpsimd")


def sview(t, off, stride, count, rowlen, parts=128):
    return bass.AP(t, off, [[rowlen, parts], [stride, count]])


class Sched:

    def __init__(self, nc, semnames):
        self.nc = nc
        self.sems = {}
        self.count = {n: 0 for n in semnames}
        self.semnames = semnames
        self.q = {e: [] for e in ENGINES}

    def make_sems(self, stack):
        for n in self.semnames:
            self.sems[n] = stack.enter_context(self.nc.semaphore(n))

    OWN = {"vector": "sV", "scalar": "sS", "gpsimd": "sG"}

    def emit(self, eng, fn, inc=None, waits=()):
        waits = [w for w in waits if w is not None]
        for s, v in waits:
            if v > self.count[s]:
                raise AssertionError(f"wait on future {s}: {v} > {self.count[s]}")
        own = self.OWN.get(eng)
        if own is not None:
            waits = [(own, self.count[own])] + waits
        self.q[eng].append((tuple(waits), fn, inc))
        if inc is not None:
            self.count[inc[0]] += inc[1]
        return (inc[0], self.count[inc[0]]) if inc is not None else None

    def tok(self, sem):
        return (sem, self.count[sem])

    def copy(self, eng, dst, src, waits=()):
        if eng == "scalar":
            return self.emit("scalar", lambda s, dst=dst, src=src: s.copy(dst, src),
                             inc=("sS", 1), waits=waits)
        return self.emit("vector", lambda v, dst=dst, src=src: v.tensor_copy(dst, src),
                         inc=("sV", 1), waits=waits)

    def flush(self, block):
        for eng in ENGINES:
            ops = self.q[eng]
            if not ops:
                continue

            def body(engine, ops=ops):
                lastw = {}
                for waits, fn, inc in ops:
                    for s, v in waits:
                        if lastw.get(s, -1) >= v or v <= 0:
                            continue
                        engine.wait_ge(self.sems[s], v)
                        lastw[s] = v
                    if fn is not None:
                        ins = fn(engine)
                        if inc is not None:
                            ins.then_inc(self.sems[inc[0]], inc[1])

            getattr(block, eng)(body)
        self.q = {e: [] for e in ENGINES}


def build_program(B_core=2, S_seq=4096):
    NTOK = B_core * S_seq
    T1 = 256
    NST1 = NTOK // T1
    NST2_SEQ = S_seq // 512
    GPERSEQ = S_seq // 128

    nc = bass.Bass("TRN2", target_bir_lowering=False)

    x_in = nc.dram_tensor("L0_attn_x", [B_core, S_seq, D], F32, kind="ExternalInput").ap()
    wa_in = nc.dram_tensor("L0_attn_Wa", [D, D], F32, kind="ExternalInput").ap()
    wx_in = nc.dram_tensor("L0_attn_Wx", [D, D], F32, kind="ExternalInput").ap()
    ba_in = nc.dram_tensor("L0_attn_ba", [D], F32, kind="ExternalInput").ap()
    bx_in = nc.dram_tensor("L0_attn_bx", [D], F32, kind="ExternalInput").ap()
    lam_in = nc.dram_tensor("L0_attn_Lam", [D], F32, kind="ExternalInput").ap()
    w1_in = nc.dram_tensor("L0_ffn_w1", [F, D], F32, kind="ExternalInput").ap()
    b1_in = nc.dram_tensor("L0_ffn_b1", [F], F32, kind="ExternalInput").ap()
    w2_in = nc.dram_tensor("L0_ffn_w2", [D, F], F32, kind="ExternalInput").ap()
    b2_in = nc.dram_tensor("L0_ffn_b2", [D], F32, kind="ExternalInput").ap()
    id_f_in = nc.dram_tensor("ident_f32", [128, 128], F32, kind="ExternalInput").ap()
    id_b_in = nc.dram_tensor("ident_bf16", [128, 128], BF16, kind="ExternalInput").ap()
    y_out = nc.dram_tensor("y", [B_core, S_seq, D], F32, kind="ExternalOutput").ap()

    wq1T = nc.dram_tensor("wq1T", [D, F], BF16, kind="Internal").ap()
    wq2T = nc.dram_tensor("wq2T", [F, D], BF16, kind="Internal").ap()
    wqaT = nc.dram_tensor("wqaT", [D, D], BF16, kind="Internal").ap()
    wqxT = nc.dram_tensor("wqxT", [D, D], BF16, kind="Internal").ap()
    xs1 = nc.dram_tensor("xs1", [B_core, S_seq, D], F32, kind="Internal").ap()
    xs2 = nc.dram_tensor("xs2", [B_core, S_seq, D], F32, kind="Internal").ap()

    sch = Sched(nc, ["sC", "sW", "sLA", "sLB", "sOA", "sOB", "sO", "sT", "sV", "sS", "sG"])
    E = sch.emit

    def gtile(g):
        return g // GPERSEQ, (g % GPERSEQ) * 128

    with ExitStack() as top:
        sch.make_sems(top)
        sb = lambda name, shape, dt=F32: top.enter_context(nc.sbuf_tensor(name, shape, dt))

        ident_f = sb("ident_f", [128, 128], F32)
        ident_b = sb("ident_b", [128, 128], BF16)
        ones_row = sb("ones_row", [1, 128], F32)
        onescol = sb("onescol", [128, 1], F32)
        c256 = sb("c256", [128, 1], F32)
        c16 = sb("c16", [128, 1], F32)
        coef_fm = sb("coef_fm", [128, 8], F32)
        ba_fm = sb("ba_fm", [128, 8], F32)
        bx_fm = sb("bx_fm", [128, 8], F32)
        carry = sb("carry", [128, 8], F32)
        b1c = sb("b1c", [128, F], BF16)
        b2h = sb("b2h", [128, D], BF16)
        wM1 = sb("wM1", [128, 1], F32)
        wM2h = sb("wM2h", [128, 1], F32)
        sw1b = sb("sw1b", [128, 1], F32)
        sw2b = sb("sw2b", [128, 1], F32)
        NC_ = 21
        colp = sb("colp", [128, 3 * NC_], F32)

        with ExitStack() as ph:
            psb = lambda name, shape, dt=F32: ph.enter_context(nc.sbuf_tensor(name, shape, dt))
            wld = [psb("p0_wld0", [128, F]), psb("p0_wld1", [128, F])]
            zw = psb("p0_zw", [128, F])
            qw = psb("p0_qw", [128, F], I32)
            tern = psb("p0_tern", [128, F], BF16)
            twbf = psb("p0_twbf", [128, F], BF16)
            acc32 = psb("p0_acc", [128, 64])
            acc8 = psb("p0_acc8", [128, 32])
            corrw = psb("p0_corrw", [128, F], BF16)
            corrw2 = psb("p0_corrw2", [128, F], BF16)
            stg = [psb("p0_stg0", [128, 512], BF16), psb("p0_stg1", [128, 512], BF16)]
            rowbuf = psb("p0_row", [1, F])
            vec8 = psb("p0_vec8", [8, 128])
            cell = psb("p0_cell", [1, 4])
            pst4 = ph.enter_context(nc.psum_tensor("p0_pst4", [128, 4096], BF16))
            ps_bc = ph.enter_context(nc.psum_tensor("p0_bc", [128, 512], F32))

            with nc.Block("phase0") as blk:
                E("vector", lambda v: v.memset(ones_row[:], 1.0), inc=("sV", 1))
                E("vector", lambda v: v.memset(onescol[:], 1.0), inc=("sV", 1))
                E("vector", lambda v: v.memset(c256[:], 256.0), inc=("sV", 1))
                E("vector", lambda v: v.memset(c16[:], 16.0), inc=("sV", 1))
                E("vector", lambda v: v.memset(carry[:], 0.0), inc=("sV", 1))
                tconst = sch.tok("sV")
                E("sync", lambda q: q.dma_start(ident_f[:], id_f_in), inc=("sC", 16))
                E("sync", lambda q: q.dma_start(ident_b[:], id_b_in), inc=("sC", 16))
                tid = sch.tok("sC")
                for vin, vdst in ((lam_in, coef_fm), (ba_in, ba_fm), (bx_in, bx_fm)):
                    E("sync", lambda q, vin=vin: q.dma_start(
                        vec8[:], vin.rearrange("(c p) -> c p", p=128)),
                      waits=[sch.tok("sT")], inc=("sC", 16))
                    E("tensor", lambda t: t.transpose(ps_bc[:, 0:8], vec8[:],
                                                      ident_f[0:8, 0:8]),
                      waits=[sch.tok("sC"), sch.tok("sV")], inc=("sT", 1))
                    E("vector", lambda v, vdst=vdst: v.tensor_copy(vdst[:], ps_bc[:, 0:8]),
                      waits=[sch.tok("sT")], inc=("sV", 1))
                E("sync", lambda q: q.dma_start(rowbuf[0:1, 0:F], b1_in[None, :]),
                  waits=[sch.tok("sT")], inc=("sC", 16))
                tq6 = sch.tok("sC")
                E("scalar", lambda s: s.activation(coef_fm[:], coef_fm[:], AF.Exp, scale=-1.0),
                  waits=[tq6, sch.tok("sV")], inc=("sS", 1))
                E("scalar", lambda s: s.activation(coef_fm[:], coef_fm[:], AF.Ln, bias=1.0),
                  inc=("sS", 1))
                E("vector", lambda v: v.tensor_single_scalar(coef_fm[:], coef_fm[:], -C_GRIF,
                                                             op=ALU.mult),
                  waits=[sch.tok("sS")], inc=("sV", 1))

                for s5 in range(F // 512):
                    sl = slice(s5 * 512, (s5 + 1) * 512)
                    E("tensor", lambda t, sl=sl: t.matmul(ps_bc[:], ones_row[:],
                                                          rowbuf[0:1, sl], start=True, stop=True),
                      waits=[tq6, tconst, sch.tok("sV")], inc=("sT", 1))
                    E("vector", lambda v, sl=sl: v.tensor_copy(b1c[:, sl], ps_bc[:]),
                      waits=[sch.tok("sT")], inc=("sV", 1))
                E("sync", lambda q: q.dma_start(rowbuf[0:1, 0:D], b2_in[None, :]),
                  waits=[sch.tok("sV")], inc=("sC", 16))
                tq7 = sch.tok("sC")
                for s5 in range(D // 512):
                    sl = slice(s5 * 512, (s5 + 1) * 512)
                    E("tensor", lambda t, sl=sl: t.matmul(ps_bc[:], ones_row[:],
                                                          rowbuf[0:1, sl], start=True, stop=True),
                      waits=[tq7, sch.tok("sS")], inc=("sT", 1))
                    E("scalar", lambda s, sl=sl: s.activation(b2h[:, sl], ps_bc[:],
                                                              AF.Copy, scale=0.5),
                      waits=[sch.tok("sT")], inc=("sS", 1))

                def mean_abs(w_ap, nrows, ncols, acc_base, out_sw, out_m, deq):
                    nch = nrows // 128
                    nsb = ncols // 128
                    for ci in range(nch):
                        wld_t = wld[ci % 2][:, 0:ncols]
                        sl_ = "sLA" if ci % 2 == 0 else "sLB"
                        w = [sch.tok("sS"), sch.tok("sV")]
                        E("sync", lambda q, ci=ci, wld_t=wld_t, w_ap=w_ap: q.dma_start(
                            wld_t, w_ap[ci * 128:(ci + 1) * 128, :]), waits=w, inc=(sl_, 16))
                        tq = sch.tok(sl_)
                        E("scalar", lambda s, wld_t=wld_t: s.activation(
                            zw[:, 0:ncols], wld_t, AF.Abs), waits=[tq], inc=("sS", 1))
                        tz = sch.tok("sS")
                        for sb in range(nsb):
                            E("vector", lambda v, sb=sb: v.reduce_sum(
                                acc8[:, sb:sb + 1], zw[:, sb * 128:(sb + 1) * 128],
                                axis=AX.X), waits=[tz] if sb == 0 else [], inc=("sV", 1))
                        E("vector", lambda v, ci=ci: v.reduce_sum(
                            acc32[:, acc_base + ci:acc_base + ci + 1], acc8[:, 0:nsb],
                            axis=AX.X), inc=("sV", 1))
                    E("vector", lambda v: v.reduce_sum(
                        colp[:, 0:1], acc32[:, acc_base:acc_base + nch], axis=AX.X),
                      waits=[sch.tok("sS")], inc=("sV", 1))
                    E("tensor", lambda t: t.matmul(ps_bc[0:1, 0:1], colp[:, 0:1], onescol[:],
                                                   start=True, stop=True),
                      waits=[sch.tok("sV")], inc=("sT", 1))
                    E("vector", lambda v: v.tensor_copy(cell[0:1, 0:1], ps_bc[0:1, 0:1]),
                      waits=[sch.tok("sT")], inc=("sV", 1))
                    E("vector", lambda v: v.tensor_single_scalar(
                        cell[0:1, 1:2], cell[0:1, 0:1], 1.0 / (nrows * ncols), op=ALU.mult),
                      inc=("sV", 1))
                    E("vector", lambda v: v.tensor_single_scalar(
                        cell[0:1, 1:2], cell[0:1, 1:2], EPS, op=ALU.max), inc=("sV", 1))
                    E("vector", lambda v: v.reciprocal(cell[0:1, 2:3], cell[0:1, 1:2]),
                      inc=("sV", 1))
                    tv2 = sch.tok("sV")
                    E("tensor", lambda t: t.matmul(ps_bc[:, 0:1], ones_row[:], cell[0:1, 2:3],
                                                   start=True, stop=True),
                      waits=[tv2], inc=("sT", 1))
                    E("vector", lambda v, out_sw=out_sw: v.tensor_copy(out_sw, ps_bc[:, 0:1]),
                      waits=[sch.tok("sT")], inc=("sV", 1))
                    E("tensor", lambda t: t.matmul(ps_bc[:, 0:1], ones_row[:], cell[0:1, 1:2],
                                                   start=True, stop=True),
                      waits=[sch.tok("sV")], inc=("sT", 1))
                    E("scalar", lambda s, out_m=out_m, deq=deq: s.activation(
                        out_m, ps_bc[:, 0:1], AF.Copy, scale=deq),
                      waits=[sch.tok("sT")], inc=("sS", 1))

                mean_abs(w1_in, F, D, 0, sw1b[:], wM1[:], 1.0 / 127.0)
                mean_abs(w2_in, D, F, 40, sw2b[:], wM2h[:], 0.5 / 127.0)

                def quant_transpose(w_ap, nrows, ncols, swcol, out_T, do_quant, pctr=[0]):
                    nch = nrows // 128
                    npc = ncols // 128
                    for ci in range(nch):
                        wld_t = wld[ci % 2][:, 0:ncols]
                        sl_ = "sLA" if ci % 2 == 0 else "sLB"
                        w = [sch.tok("sT"), sch.tok("sS"), sch.tok("sV")]
                        E("sync", lambda q, ci=ci, wld_t=wld_t, w_ap=w_ap: q.dma_start(
                            wld_t, w_ap[ci * 128:(ci + 1) * 128, :]), waits=w, inc=(sl_, 16))
                        tq = sch.tok(sl_)
                        if do_quant:
                            E("scalar", lambda s, wld_t=wld_t, swcol=swcol: s.activation(
                                zw[:, 0:ncols], wld_t, AF.Identity, scale=swcol),
                              waits=[tq, sch.tok("sV")], inc=("sS", 1))
                            E("scalar", lambda s: s.copy(qw[:, 0:ncols], zw[:, 0:ncols]),
                              inc=("sS", 1))
                            tqq = sch.tok("sS")
                            E("vector", lambda v: v.scalar_tensor_tensor(
                                corrw[:, 0:ncols], zw[:, 0:ncols], -0.5, qw[:, 0:ncols],
                                op0=ALU.add, op1=ALU.is_ge), waits=[tqq], inc=("sV", 1))
                            E("vector", lambda v: v.scalar_tensor_tensor(
                                corrw2[:, 0:ncols], zw[:, 0:ncols], 0.5, qw[:, 0:ncols],
                                op0=ALU.add, op1=ALU.is_le), inc=("sV", 1))
                            E("vector", lambda v: v.scalar_tensor_tensor(
                                zw[:, 0:ncols], corrw[:, 0:ncols], 1.0, qw[:, 0:ncols],
                                op0=ALU.mult, op1=ALU.add), inc=("sV", 1))
                            E("vector", lambda v: v.tensor_sub(
                                zw[:, 0:ncols], zw[:, 0:ncols], corrw2[:, 0:ncols]),
                              inc=("sV", 1))
                            E("vector", lambda v: v.tensor_scalar(
                                tern[:, 0:ncols], zw[:, 0:ncols], 1.0, -1.0,
                                op0=ALU.min, op1=ALU.max),
                              waits=[sch.tok("sT")], inc=("sV", 1))
                            src = tern
                        else:
                            E("vector", lambda v, wld_t=wld_t: v.tensor_copy(
                                twbf[:, 0:ncols], wld_t),
                              waits=[tq, sch.tok("sT")], inc=("sV", 1))
                            src = twbf
                        tsrc = sch.tok("sV")
                        for q4 in range(npc // 4):
                            gp = pctr[0]; pctr[0] += 1
                            pr = gp % 2
                            so_ = "sOA" if pr == 0 else "sOB"
                            for ci4 in range(4):
                                pi = q4 * 4 + ci4
                                E("tensor", lambda t, pi=pi, pr=pr, ci4=ci4, src=src:
                                  t.transpose(
                                    pst4[:, pr * 2048 + ci4 * 512: pr * 2048 + ci4 * 512 + 128],
                                    src[:, pi * 128:(pi + 1) * 128], ident_b[:]),
                                  waits=([tsrc, sch.tok("sV")] if ci4 == 0 else []),
                                  inc=("sT", 1))
                            tt = sch.tok("sT")
                            wev = [tt]
                            if gp >= 2:
                                wev.append(sch.tok(so_))
                            psrc = bass.AP(pst4, pr * 2048, [[4096, 128], [512, 4], [1, 128]])
                            E("vector", lambda v, pr=pr, psrc=psrc: v.tensor_copy(
                                stg[pr][:], psrc), waits=wev, inc=("sV", 1))
                            tv = sch.tok("sV")
                            dst = bass.AP(out_T.tensor, q4 * 512 * out_T.shape[1] + ci * 128,
                                          [[out_T.shape[1], 128],
                                           [128 * out_T.shape[1], 4], [1, 128]])
                            E("sync", lambda q, dst=dst, pr=pr: q.dma_start(dst, stg[pr][:]),
                              waits=[tv], inc=(so_, 16))

                quant_transpose(w1_in, F, D, sw1b[:], wq1T, True)
                quant_transpose(w2_in, D, F, sw2b[:], wq2T, True)
                quant_transpose(wa_in, D, D, None, wqaT, False)
                quant_transpose(wx_in, D, D, None, wqxT, False)
                sch.flush(blk)

        tok_p0a = sch.tok("sOA")
        tok_p0b = sch.tok("sOB")

        def ffn_phase(name, src_ap, dst_ap, src_tok_fn, store_toks_out, entry):
            NT = NTOK // 128
            with ExitStack() as ph:
                psb = lambda n, shape, dt=F32: ph.enter_context(
                    nc.sbuf_tensor(name + n, shape, dt))
                w1T = psb("_w1T", [128, 8 * F], BF16)
                w2T = psb("_w2T", [128, 32 * D], BF16)
                xb = [psb(f"_xb{i}", [128, D]) for i in range(3)]
                rb1 = psb("_rb1", [128, D], BF16)
                r1T = psb("_r1T", [128, 2048], BF16)
                hbuf = psb("_h", [128, F])
                zq = [psb(f"_zq{i}", [128, 1024]) for i in range(2)]
                qi = [psb(f"_qi{i}", [128, 1024], mybir.dt.int16) for i in range(2)]
                gbuf = [psb(f"_g{i}", [128, F], BF16) for i in range(2)]
                gstg2_ = psb("_gs2", [128, 1024], BF16)
                gstg2 = [gstg2_[:, 0:512], gstg2_[:, 512:1024]]
                bnh = psb("_bnh", [128, 48])
                bnb = bnh
                sqm2 = psb("_sqm2", [128, 16])
                sqm = sqm2
                pm = [ph.enter_context(nc.psum_tensor(f"{name}_pm{i}", [128, 512], F32))
                      for i in range(2)]
                po = [ph.enter_context(nc.psum_tensor(f"{name}_po{i}", [128, 512], F32))
                      for i in range(2)]
                ptr2 = ph.enter_context(nc.psum_tensor(name + "_pt2", [128, 4096], BF16))

                with nc.Block(name) as blk:
                    for k in range(8):
                        E("sync", lambda q, k=k: q.dma_start(
                            w1T[:, k * F:(k + 1) * F], wq1T[k * 128:(k + 1) * 128, :]),
                          waits=[tok_p0a, tok_p0b] + entry, inc=("sW", 16))
                    for fc in range(32):
                        E("sync", lambda q, fc=fc: q.dma_start(
                            w2T[:, fc * D:(fc + 1) * D], wq2T[fc * 128:(fc + 1) * 128, :]),
                          waits=[tok_p0a, tok_p0b] + entry, inc=("sW", 16))
                    tok_w = sch.tok("sW")

                    tiles = {}
                    st = dict(pm={}, po_ev=None, ptr={}, gstg={})

                    def emit_load(j):
                        if j >= NT:
                            return
                        sl_ = "sLA" if j % 2 == 0 else "sLB"
                        waits = [tok_w, src_tok_fn(j)]
                        if j >= 3:
                            waits.append(tiles[j - 3]["stored"])
                        if j >= 2:
                            waits.append(tiles[j - 2]["load"])
                        b, t = gtile(j)
                        E("sync", lambda q, j=j, b=b, t=t: q.dma_start(
                            xb[j % 3][:], src_ap[b, t:t + 128, :]),
                          waits=waits, inc=(sl_, 16))
                        tiles.setdefault(j, {})["load"] = sch.tok(sl_)

                    def stA(j):
                        if j >= NT:
                            return
                        tj = tiles.setdefault(j, {})
                        cb = NC_ * (j % 3)
                        col = lambda i: colp[:, cb + i:cb + i + 1]
                        tj["col"] = col
                        t2 = tiles.get(j - 2, {})
                        t3 = tiles.get(j - 3, {})
                        xt = xb[j % 3][:]
                        E("vector", lambda v: v.reduce_sum(col(0), xt, axis=AX.X),
                          waits=[tj["load"], t3.get("colfree")], inc=("sV", 1))
                        E("vector", lambda v: v.tensor_single_scalar(
                            col(3), col(0), 1.0 / 1024.0, op=ALU.mult), inc=("sV", 1))
                        E("vector", lambda v: v.reduce_max(col(12), xt, axis=AX.X),
                          inc=("sV", 1))
                        E("vector", lambda v: v.tensor_reduce(col(13), xt, axis=AX.X,
                                                              op=ALU.min), inc=("sV", 1))
                        E("vector", lambda v: v.tensor_sub(col(14), col(12), col(3)),
                          inc=("sV", 1))
                        E("vector", lambda v: v.tensor_sub(col(15), col(3), col(13)),
                          inc=("sV", 1))
                        E("vector", lambda v: v.tensor_max(col(14), col(14), col(15)),
                          inc=("sV", 1))
                        E("vector", lambda v: v.tensor_single_scalar(
                            col(15), col(14), 1e-30, op=ALU.max), inc=("sV", 1))
                        E("vector", lambda v: v.reciprocal(col(15), col(15)), inc=("sV", 1))
                        E("vector", lambda v: v.tensor_single_scalar(
                            col(18), col(15), 127.0, op=ALU.mult), inc=("sV", 1))
                        E("vector", lambda v: v.scalar_tensor_tensor(
                            col(19), col(3), -1.0, col(18), op0=ALU.mult, op1=ALU.mult),
                          inc=("sV", 1))
                        E("vector", lambda v: v.tensor_single_scalar(col(19), col(19), 256.0,
                                                                     op=ALU.add), inc=("sV", 1))
                        tcols = sch.tok("sV")
                        tj["tcols"] = tcols
                        E("vector", lambda v: v.tensor_scalar(
                            zq[0][:], xt, col(18), col(19), op0=ALU.mult, op1=ALU.add),
                          waits=[tcols, t2.get("zq0free"),
                                 tiles.get(j - 1, {}).get("r1")], inc=("sV", 1))
                        tz1 = sch.tok("sV")
                        E("scalar", lambda s: s.copy(qi[0][:], zq[0][:]), waits=[tz1],
                          inc=("sS", 1))
                        tq1 = sch.tok("sS")
                        E("vector", lambda v: v.scalar_tensor_tensor(
                            zq[0][:], zq[0][:], -0.5, qi[0][:], op0=ALU.add, op1=ALU.is_ge),
                          waits=[tq1], inc=("sV", 1))
                        E("vector", lambda v: v.scalar_tensor_tensor(
                            rb1[:], zq[0][:], -256.0, qi[0][:], op0=ALU.add, op1=ALU.add),
                          waits=[tiles.get(j - 1, {}).get("tr1_done")], inc=("sV", 1))
                        tj["r1"] = sch.tok("sV")
                        E("vector", lambda v: v.bn_stats(bnb[:, 0:6], xt[:, 0:512]),
                          waits=[t2.get("sqmS")], inc=("sV", 1))
                        E("vector", lambda v: v.bn_stats(bnb[:, 6:12], xt[:, 512:1024]),
                          inc=("sV", 1))
                        tbn = sch.tok("sV")
                        E("vector", lambda v: v.tensor_tensor(
                            sqm[:, 0:4], sview(bnb, 1, 3, 4, 48), sview(bnb, 1, 3, 4, 48),
                            op=ALU.mult), inc=("sV", 1))
                        E("vector", lambda v: v.reduce_sum(col(1), sview(bnb, 2, 3, 4, 48),
                                                           axis=AX.X), inc=("sV", 1))
                        E("vector", lambda v: v.reduce_sum(col(2), sqm[:, 0:4], axis=AX.X),
                          inc=("sV", 1))
                        tj["sqmS"] = sch.tok("sV")
                        E("vector", lambda v: v.tensor_single_scalar(col(4), col(2), 0.25,
                                                                     op=ALU.mult), inc=("sV", 1))
                        E("vector", lambda v: v.scalar_tensor_tensor(
                            col(5), col(1), 1.0 / 1024.0, col(4), op0=ALU.mult, op1=ALU.add),
                          inc=("sV", 1))
                        E("vector", lambda v: v.tensor_mul(col(6), col(3), col(3)), inc=("sV", 1))
                        E("vector", lambda v: v.tensor_sub(col(7), col(5), col(6)), inc=("sV", 1))
                        E("vector", lambda v: v.tensor_single_scalar(col(8), col(7), EPS,
                                                                     op=ALU.add), inc=("sV", 1))
                        E("vector", lambda v: v.scalar_tensor_tensor(
                            col(9), col(8), EPS, col(7), op0=ALU.mult, op1=ALU.add),
                          inc=("sV", 1))
                        tv_ab = sch.tok("sV")
                        E("scalar", lambda s: s.activation(col(10), col(9), AF.Sqrt),
                          waits=[tv_ab], inc=("sS", 1))
                        E("vector", lambda v: v.reciprocal(col(11), col(10)),
                          waits=[sch.tok("sS")], inc=("sV", 1))
                        E("vector", lambda v: v.tensor_mul(col(16), col(11), col(14)),
                          inc=("sV", 1))
                        E("vector", lambda v: v.tensor_single_scalar(col(16), col(16), EPS,
                                                                     op=ALU.max), inc=("sV", 1))
                        E("vector", lambda v: v.tensor_mul(col(20), col(16), wM1[:]),
                          inc=("sV", 1))
                        tj["c1"] = sch.tok("sV")

                    def stTr1(j):
                        if j >= NT:
                            return
                        tj = tiles[j]
                        for q4 in range(2):
                            half = q4 % 2
                            for ci in range(4):
                                c = q4 * 4 + ci
                                E("tensor", lambda t, c=c, half=half, ci=ci: t.transpose(
                                    ptr2[:, half * 2048 + ci * 512: half * 2048 + ci * 512 + 128],
                                    rb1[:, c * 128:(c + 1) * 128], ident_b[:]),
                                  waits=([tj["r1"]] if ci == 0 else []) +
                                        ([st["ptr"].get(half)] if ci == 0 else []),
                                  inc=("sT", 1))
                            ttr = sch.tok("sT")
                            psrc = bass.AP(ptr2, half * 2048, [[4096, 128], [512, 4], [1, 128]])
                            dst = bass.AP(r1T, (j % 2) * 128 + q4 * 1024,
                                          [[2048, 128], [256, 4], [1, 128]])
                            tok = E("scalar", lambda s, dst=dst, psrc=psrc: s.copy(dst, psrc),
                                    waits=[ttr], inc=("sS", 1))
                            st["ptr"][half] = tok
                        tj["r1T_evac"] = sch.tok("sS")
                        tj["tr1_done"] = sch.tok("sT")

                    def stB1(j):
                        tj = tiles[j]
                        tprev = tiles.get(j - 1, {})
                        col = tj["col"]
                        tj["ev1"] = {}
                        gelu_toks = []
                        for f in range(8):
                            bank = pm[f % 2]
                            wmm = [tj["r1T_evac"], st["pm"].get(f % 2)]
                            if f == 0:
                                wmm.append(tok_w)
                            for k in range(8):
                                E("tensor", lambda t, f=f, k=k, bank=bank, j=j: t.matmul(
                                    bank[:],
                                    r1T[:, k * 256 + (j % 2) * 128: k * 256 + (j % 2) * 128 + 128],
                                    w1T[:, k * F + f * 512: k * F + (f + 1) * 512],
                                    start=(k == 0), stop=(k == 7)),
                                  waits=wmm if k == 0 else [], inc=("sT", 1))
                            tmm = sch.tok("sT")
                            wev = [tmm, tj["c1"]]
                            if f == 0:
                                wev.append(tprev.get("hfree"))
                            tok = E("vector", lambda v, f=f, bank=bank, col=col:
                                    v.scalar_tensor_tensor(
                                hbuf[:, f * 512:(f + 1) * 512], bank[:], col(20),
                                b1c[:, f * 512:(f + 1) * 512], op0=ALU.mult, op1=ALU.add),
                              waits=wev, inc=("sV", 1))
                            st["pm"][f % 2] = tok
                            tj["ev1"][f] = tok
                            if f == 3 or f == 7:
                                h0 = 0 if f == 3 else 2048
                                E("scalar", lambda s, h0=h0: s.activation(
                                    hbuf[:, h0:h0 + 2048], hbuf[:, h0:h0 + 2048], AF.Gelu),
                                  waits=[tok], inc=("sS", 1))
                                gelu_toks.append(sch.tok("sS"))
                        tj["gelu"] = gelu_toks
                        tj["mm1_done"] = sch.tok("sT")

                    def stC(j):
                        tj = tiles[j]
                        tnext = tiles.get(j + 1, {})
                        t2 = tiles.get(j - 2, {})
                        col = tj["col"]
                        E("vector", lambda v: v.reduce_max(col(0), hbuf[:], axis=AX.X,
                                                           apply_absolute_value=True),
                          waits=[tj["gelu"][1]], inc=("sV", 1))
                        E("vector", lambda v: v.tensor_single_scalar(
                            col(1), col(0), 1e-30, op=ALU.max), inc=("sV", 1))
                        E("vector", lambda v: v.reciprocal(col(1), col(1)), inc=("sV", 1))
                        E("vector", lambda v: v.tensor_single_scalar(
                            col(12), col(1), 127.0, op=ALU.mult), inc=("sV", 1))
                        tsig2 = sch.tok("sV")
                        tj["r2"] = []
                        for qc in range(4):
                            p2 = qc % 2
                            wz = [tsig2]
                            if qc == 0:
                                wz += [tnext.get("r1"), t2.get("zq0free2")]
                            elif qc == 1:
                                wz.append(t2.get("zq1free"))
                            else:
                                wz.append(tj["r2"][qc - 2])
                            E("vector", lambda v, qc=qc, p2=p2, col=col: v.tensor_scalar(
                                zq[p2][:], hbuf[:, qc * 1024:(qc + 1) * 1024], col(12), 256.0,
                                op0=ALU.mult, op1=ALU.add), waits=wz, inc=("sV", 1))
                            tz2 = sch.tok("sV")
                            wq_ = [tz2] + ([tnext.get("r1")] if qc == 0 else [])
                            E("scalar", lambda s, p2=p2: s.copy(qi[p2][:], zq[p2][:]),
                              waits=wq_, inc=("sS", 1))
                            tq2 = sch.tok("sS")
                            E("vector", lambda v, p2=p2: v.scalar_tensor_tensor(
                                zq[p2][:], zq[p2][:], -0.5, qi[p2][:],
                                op0=ALU.add, op1=ALU.is_ge), waits=[tq2], inc=("sV", 1))
                            tok = E("vector", lambda v, qc=qc, p2=p2, j=j: v.scalar_tensor_tensor(
                                gbuf[j % 2][:, qc * 1024:(qc + 1) * 1024], zq[p2][:], -256.0,
                                qi[p2][:], op0=ALU.add, op1=ALU.add),
                              waits=[t2.get("trg_done")] if qc == 0 else [], inc=("sV", 1))
                            tj["r2"].append(tok)
                        tj["zq0free2"] = tj["r2"][2]
                        tj["zq0free"] = tj["r2"][2]
                        tj["zq1free"] = tj["r2"][3]
                        tj["hfree"] = sch.tok("sV")
                        for i8 in range(8):
                            w = []
                            if i8 == 0:
                                w = [tj["gelu"][1], t2.get("sqm2V")]
                            E("vector", lambda v, i8=i8: v.bn_stats(
                                bnh[:, i8 * 6:(i8 + 1) * 6],
                                hbuf[:, i8 * 512:(i8 + 1) * 512]), waits=w, inc=("sV", 1))
                        E("vector", lambda v: v.tensor_tensor(
                            sqm2[:, 0:16], sview(bnh, 1, 3, 16, 48), sview(bnh, 1, 3, 16, 48),
                            op=ALU.mult), inc=("sV", 1))
                        E("vector", lambda v: v.reduce_sum(col(2), sview(bnh, 2, 3, 16, 48),
                                                           axis=AX.X), inc=("sV", 1))
                        E("vector", lambda v: v.reduce_sum(col(4), sqm2[:, 0:16], axis=AX.X),
                          inc=("sV", 1))
                        tj["sqm2V"] = sch.tok("sV")
                        E("vector", lambda v: v.tensor_single_scalar(col(4), col(4), 1.0 / 16,
                                                                     op=ALU.mult), inc=("sV", 1))
                        E("vector", lambda v: v.scalar_tensor_tensor(
                            col(5), col(2), 1.0 / F, col(4), op0=ALU.mult, op1=ALU.add),
                          inc=("sV", 1))
                        E("vector", lambda v: v.tensor_single_scalar(col(5), col(5), EPS,
                                                                     op=ALU.add), inc=("sV", 1))
                        tv_eh = sch.tok("sV")
                        E("scalar", lambda s: s.activation(col(2), col(5), AF.Sqrt),
                          waits=[tv_eh], inc=("sS", 1))
                        E("vector", lambda v: v.reciprocal(col(2), col(2)),
                          waits=[sch.tok("sS")], inc=("sV", 1))
                        E("vector", lambda v: v.tensor_mul(col(13), col(2), col(0)),
                          inc=("sV", 1))
                        E("vector", lambda v: v.tensor_single_scalar(col(13), col(13), EPS,
                                                                     op=ALU.max), inc=("sV", 1))
                        E("vector", lambda v: v.tensor_mul(col(13), col(13), wM2h[:]),
                          inc=("sV", 1))
                        tj["c2"] = sch.tok("sV")
                        tj["colfree"] = sch.tok("sS")

                    def stB2(j):
                        if j < 0:
                            return
                        tj = tiles[j]
                        col = tj["col"]
                        for q4 in range(8):
                            half = q4 % 2
                            for ci in range(4):
                                fc = q4 * 4 + ci
                                E("tensor", lambda t, fc=fc, half=half, ci=ci, j=j: t.transpose(
                                    ptr2[:, half * 2048 + ci * 512: half * 2048 + ci * 512 + 128],
                                    gbuf[j % 2][:, fc * 128:(fc + 1) * 128], ident_b[:]),
                                  waits=([tj["r2"][fc // 8]] if ci == 0 else []) +
                                        ([st["ptr"].get(half)] if ci == 0 else []),
                                  inc=("sT", 1))
                            ttr = sch.tok("sT")
                            psrc = bass.AP(ptr2, half * 2048, [[4096, 128], [512, 4], [1, 128]])
                            tok = E("scalar", lambda s, psrc=psrc, half=half: s.copy(
                                gstg2[half][:], psrc),
                                waits=[ttr, st["gstg"].get(half)], inc=("sS", 1))
                            st["ptr"][half] = tok
                            wmm2 = [tok]
                            if q4 == 0:
                                wmm2 += [st["po_ev"], tok_w]
                            for pi in range(4):
                                fc = q4 * 4 + pi
                                for n in range(2):
                                    E("tensor", lambda t, fc=fc, n=n, pi=pi, half=half: t.matmul(
                                        po[n][:], gstg2[half][:, pi * 128:(pi + 1) * 128],
                                        w2T[:, fc * D + n * 512: fc * D + (n + 1) * 512],
                                        start=(fc == 0), stop=(fc == 31)),
                                      waits=wmm2 if (pi == 0 and n == 0) else [],
                                      inc=("sT", 1))
                            st["gstg"][half] = sch.tok("sT")
                        tj["trg_done"] = sch.tok("sT")
                        tmm2 = tj["trg_done"]
                        xt = xb[j % 3][:]
                        for n in range(2):
                            E("vector", lambda v, n=n, xt=xt, col=col: v.scalar_tensor_tensor(
                                xt[:, n * 512:(n + 1) * 512], po[n][:], col(13),
                                xt[:, n * 512:(n + 1) * 512], op0=ALU.mult, op1=ALU.add),
                              waits=[tmm2, tj["c2"]] if n == 0 else [], inc=("sV", 1))
                        st["po_ev"] = sch.tok("sV")
                        E("vector", lambda v, xt=xt: v.tensor_add(xt, xt, b2h[:]),
                          inc=("sV", 1))
                        tout = sch.tok("sV")
                        b, t = gtile(j)
                        E("sync", lambda q, j=j, b=b, t=t: q.dma_start(
                            dst_ap[b, t:t + 128, :], xb[j % 3][:]),
                          waits=[tout], inc=("sO", 16))
                        tj["stored"] = sch.tok("sO")
                        store_toks_out.append(tj["stored"])

                    emit_load(0)
                    emit_load(1)
                    stA(0)
                    stTr1(0)
                    for j in range(NT):
                        stA(j + 1)
                        stB1(j)
                        stB2(j - 1)
                        emit_load(j + 2)
                        stTr1(j + 1)
                        stC(j)
                    stB2(NT - 1)
                    sch.flush(blk)

        def griffin_phase(src_ap, dst_ap, src_store_toks, store_toks_out, entry):
            with ExitStack() as ph:
                psb = lambda n, shape, dt=F32: ph.enter_context(
                    nc.sbuf_tensor("g2" + n, shape, dt))
                waT = psb("_waT", [128, 8 * D], BF16)
                wxT = psb("_wxT", [128, 8 * D], BF16)
                xb = [psb("_xb0", [128, 4 * D]), psb("_xb1", [128, 4 * D])]
                xn_tm = psb("_xn", [128, D])
                xnT_f = psb("_xnTf", [128, 4096])
                xnT_b = psb("_xnTb", [128, 4096], BF16)
                a_fm = psb("_a", [128, 4096])
                b_fm = psb("_b", [128, 4096])
                it_fm = psb("_it", [128, 4096])
                rt_tmp = psb("_rt", [128, 512])
                tsc = psb("_tsc", [128, 512])
                pbuf = psb("_pb", [128, 512])
                bnb = psb("_bnb2", [128, 12])
                sqm = psb("_sqm2", [128, 4])
                pg = [ph.enter_context(nc.psum_tensor(f"g2_pg{i}", [128, 512], F32))
                      for i in range(4)]
                ptf = [ph.enter_context(nc.psum_tensor(f"g2_pt{i}", [128, 512], F32))
                       for i in range(2)]

                with nc.Block("phase2") as blk:
                    for k in range(8):
                        E("sync", lambda q, k=k: q.dma_start(
                            waT[:, k * D:(k + 1) * D], wqaT[k * 128:(k + 1) * 128, :]),
                          waits=[tok_p0a, tok_p0b] + entry, inc=("sW", 16))
                        E("sync", lambda q, k=k: q.dma_start(
                            wxT[:, k * D:(k + 1) * D], wqxT[k * 128:(k + 1) * 128, :]),
                          waits=[tok_p0a, tok_p0b], inc=("sW", 16))
                    tok_w = sch.tok("sW")

                    load_toks = {}
                    NT2 = B_core * NST2_SEQ

                    def emit_load(ti):
                        par = ti % 2
                        sl_ = "sLA" if par == 0 else "sLB"
                        waits = [tok_w]
                        if ti >= 2:
                            waits.append(store_toks_out[4 * (ti - 2) + 3])
                        if src_store_toks:
                            waits.append(src_store_toks[-1])
                        for m in range(4):
                            b, t = gtile(4 * ti + m)
                            E("sync", lambda q, par=par, m=m, b=b, t=t: q.dma_start(
                                xb[par][:, m * D:(m + 1) * D], src_ap[b, t:t + 128, :]),
                              waits=waits if m == 0 else [], inc=(sl_, 16))
                        load_toks[ti] = sch.tok(sl_)

                    emit_load(0)
                    if NT2 > 1:
                        emit_load(1)

                    st2 = dict(ptf={}, xn_tr=None, pg_read={}, sguard={}, scan=None,
                               b_done=None)

                    g2m = os.environ.get("ATH_G2", "lmbsy")
                    for ti in range(NT2):
                        par = ti % 2
                        jseq = ti % NST2_SEQ
                        tload = load_toks[ti]
                        for m in range(4):
                            cb = NC_ * (m % 2)
                            col = lambda i: colp[:, cb + i:cb + i + 1]
                            xt = xb[par][:, m * D:(m + 1) * D]
                            E("vector", lambda v, xt=xt: v.bn_stats(bnb[:, 0:6], xt[:, 0:512]),
                              waits=[tload, st2["sguard"].get(m % 2)], inc=("sV", 1))
                            E("vector", lambda v, xt=xt: v.bn_stats(bnb[:, 6:12],
                                                                    xt[:, 512:1024]),
                              inc=("sV", 1))
                            tbn = sch.tok("sV")
                            E("scalar", lambda s: s.activation(
                                sqm[:, 0:4], sview(bnb, 1, 3, 4, 12), AF.Square),
                              waits=[tbn], inc=("sS", 1))
                            tsq = sch.tok("sS")
                            E("vector", lambda v, col=col: v.reduce_sum(
                                col(0), sview(bnb, 1, 3, 4, 12), axis=AX.X),
                              waits=[tbn], inc=("sV", 1))
                            E("vector", lambda v, col=col: v.reduce_sum(
                                col(1), sview(bnb, 2, 3, 4, 12), axis=AX.X), inc=("sV", 1))
                            E("vector", lambda v, col=col: v.reduce_sum(
                                col(2), sqm[:, 0:4], axis=AX.X), waits=[tsq], inc=("sV", 1))
                            E("vector", lambda v, col=col: v.tensor_single_scalar(
                                col(3), col(0), 0.25, op=ALU.mult), inc=("sV", 1))
                            E("vector", lambda v, col=col: v.tensor_single_scalar(
                                col(4), col(2), 0.25, op=ALU.mult), inc=("sV", 1))
                            E("vector", lambda v, col=col: v.scalar_tensor_tensor(
                                col(5), col(1), 1.0 / 1024.0, col(4), op0=ALU.mult,
                                op1=ALU.add), inc=("sV", 1))
                            E("vector", lambda v, col=col: v.tensor_mul(col(6), col(3), col(3)),
                              inc=("sV", 1))
                            E("vector", lambda v, col=col: v.tensor_sub(col(7), col(5), col(6)),
                              inc=("sV", 1))
                            E("vector", lambda v, col=col: v.tensor_single_scalar(
                                col(8), col(7), EPS, op=ALU.add), inc=("sV", 1))
                            tv_a = sch.tok("sV")
                            E("scalar", lambda s, col=col: s.activation(col(9), col(8), AF.Sqrt),
                              waits=[tv_a], inc=("sS", 1))
                            E("vector", lambda v, col=col: v.reciprocal(col(10), col(9)),
                              waits=[sch.tok("sS")], inc=("sV", 1))
                            E("vector", lambda v, col=col: v.scalar_tensor_tensor(
                                col(11), col(3), -1.0, col(10), op0=ALU.mult, op1=ALU.mult),
                              inc=("sV", 1))
                            tv_c = sch.tok("sV")
                            E("scalar", lambda s, xt=xt, col=col: s.activation(
                                xn_tm[:], xt, AF.Identity, bias=col(11), scale=col(10)),
                              waits=[tv_c, st2["xn_tr"]], inc=("sS", 1))
                            txn = sch.tok("sS")
                            for c in range(8):
                                pr = c % 2
                                wtr = [txn] + ([st2["ptf"][pr]] if pr in st2["ptf"] else [])
                                if (pr, "s") in st2["ptf"]:
                                    wtr.append(st2["ptf"][(pr, "s")])
                                E("tensor", lambda t, c=c, pr=pr: t.transpose(
                                    ptf[pr][:, 0:128], xn_tm[:, c * 128:(c + 1) * 128],
                                    ident_f[:]), waits=wtr,
                                  inc=("sT", 1))
                                ttr = sch.tok("sT")
                                cc = c * 512 + m * 128
                                t1 = E("vector", lambda v, pr=pr, cc=cc: v.tensor_copy(
                                    xnT_f[:, cc:cc + 128], ptf[pr][:, 0:128]),
                                  waits=[ttr], inc=("sV", 1))
                                t2 = E("scalar", lambda s, cc=cc: s.copy(
                                    xnT_b[:, cc:cc + 128], xnT_f[:, cc:cc + 128]),
                                  waits=[t1], inc=("sS", 1))
                                st2["ptf"][pr] = t1
                            st2["xn_tr"] = sch.tok("sT")
                            st2["sguard"][m % 2] = sch.tok("sS")
                        txnb = sch.tok("sS")
                        txnf = sch.tok("sV")
                        scan_toks = []
                        for c in (range(8) if "m" in g2m else []):
                            bk_rt = pg[(2 * c) % 4]
                            bk_it = pg[(2 * c + 1) % 4]
                            wmm = [txnb, st2["pg_read"].get((2 * c) % 4),
                                   st2["pg_read"].get((2 * c + 1) % 4),
                                   st2["scan"], tok_w if c == 0 else None]
                            for k in range(8):
                                E("tensor", lambda t, c=c, k=k, bk=bk_rt: t.matmul(
                                    bk[:], waT[:, k * D + c * 128: k * D + (c + 1) * 128],
                                    xnT_b[:, k * 512:(k + 1) * 512],
                                    start=(k == 0), stop=(k == 7)),
                                  waits=wmm if k == 0 else [], inc=("sT", 1))
                            for k in range(8):
                                E("tensor", lambda t, c=c, k=k, bk=bk_it: t.matmul(
                                    bk[:], wxT[:, k * D + c * 128: k * D + (c + 1) * 128],
                                    xnT_b[:, k * 512:(k + 1) * 512],
                                    start=(k == 0), stop=(k == 7)), inc=("sT", 1))
                            tmm = sch.tok("sT")
                            E("scalar", lambda s, c=c, bk=bk_rt: s.activation(
                                rt_tmp[:], bk[:], AF.Sigmoid, bias=ba_fm[:, c:c + 1]),
                              waits=[tmm, st2["b_done"]], inc=("sS", 1))
                            E("scalar", lambda s, c=c: s.activation(
                                a_fm[:, c * 512:(c + 1) * 512], rt_tmp[:], AF.Exp,
                                scale=coef_fm[:, c:c + 1]), inc=("sS", 1))
                            st2["pg_read"][(2 * c) % 4] = sch.tok("sS")
                            E("scalar", lambda s, c=c, bk=bk_it: s.activation(
                                it_fm[:, c * 512:(c + 1) * 512], bk[:], AF.Sigmoid,
                                bias=bx_fm[:, c:c + 1]), inc=("sS", 1))
                            st2["pg_read"][(2 * c + 1) % 4] = sch.tok("sS")
                            ta = sch.tok("sS")
                            if "b" not in g2m:
                                scan_toks.append(sch.tok("sS"))
                                continue
                            E("scalar", lambda s, c=c: s.activation(
                                tsc[:], a_fm[:, c * 512:(c + 1) * 512], AF.Square),
                              waits=[ta], inc=("sS", 1))
                            E("scalar", lambda s: s.activation(
                                tsc[:], tsc[:], AF.Sqrt, bias=1.0, scale=-1.0), inc=("sS", 1))
                            tb_ = sch.tok("sS")
                            E("vector", lambda v, c=c: v.tensor_mul(
                                pbuf[:], it_fm[:, c * 512:(c + 1) * 512],
                                xnT_f[:, c * 512:(c + 1) * 512]),
                              waits=[ta, txnf], inc=("sV", 1))
                            tokb = E("vector", lambda v, c=c: v.tensor_mul(
                                b_fm[:, c * 512:(c + 1) * 512], tsc[:], pbuf[:]),
                              waits=[tb_], inc=("sV", 1))
                            st2["b_done"] = tokb
                            if "s" not in g2m:
                                scan_toks.append(sch.tok("sV"))
                                continue
                            if jseq == 0:
                                E("vector", lambda v, c=c: v.tensor_tensor_scan(
                                    xnT_f[:, c * 512:(c + 1) * 512],
                                    a_fm[:, c * 512:(c + 1) * 512],
                                    b_fm[:, c * 512:(c + 1) * 512],
                                    0.0, op0=ALU.mult, op1=ALU.add), inc=("sV", 1))
                            else:
                                E("vector", lambda v, c=c: v.tensor_tensor_scan(
                                    xnT_f[:, c * 512:(c + 1) * 512],
                                    a_fm[:, c * 512:(c + 1) * 512],
                                    b_fm[:, c * 512:(c + 1) * 512],
                                    carry[:, c:c + 1], op0=ALU.mult, op1=ALU.add),
                                  inc=("sV", 1))
                            E("vector", lambda v, c=c: v.tensor_copy(
                                carry[:, c:c + 1], xnT_f[:, c * 512 + 511:c * 512 + 512]),
                              inc=("sV", 1))
                            scan_toks.append(sch.tok("sV"))
                        if scan_toks:
                            st2["scan"] = scan_toks[-1]
                        for c in (range(8) if ("y" in g2m and "s" in g2m and "m" in g2m)
                                  else []):
                            for m in range(4):
                                pr = (c * 4 + m) % 2
                                wtr = [scan_toks[c]] + ([st2["ptf"][pr]] if pr in st2["ptf"]
                                                        else [])
                                if (pr, "s") in st2["ptf"]:
                                    wtr.append(st2["ptf"][(pr, "s")])
                                cc = c * 512 + m * 128
                                E("tensor", lambda t, pr=pr, cc=cc: t.transpose(
                                    ptf[pr][:, 0:128], xnT_f[:, cc:cc + 128], ident_f[:]),
                                  waits=wtr, inc=("sT", 1))
                                ttr = sch.tok("sT")
                                dd = m * D + c * 128
                                tok = E("vector", lambda v, pr=pr, dd=dd, par=par: v.tensor_add(
                                    xb[par][:, dd:dd + 128], xb[par][:, dd:dd + 128],
                                    ptf[pr][:, 0:128]), waits=[ttr], inc=("sV", 1))
                                st2["ptf"][pr] = tok
                                st2["ptf"].pop((pr, "s"), None)
                        tdone = sch.tok("sV")
                        for m in range(4):
                            b, t = gtile(4 * ti + m)
                            E("sync", lambda q, par=par, m=m, b=b, t=t: q.dma_start(
                                dst_ap[b, t:t + 128, :], xb[par][:, m * D:(m + 1) * D]),
                              waits=[tdone] if m == 0 else [], inc=("sO", 16))
                            store_toks_out.append(sch.tok("sO"))
                        if ti + 2 < NT2:
                            emit_load(ti + 2)
                    sch.flush(blk)

        def entry_toks():
            return [sch.tok("sO"), sch.tok("sV"), sch.tok("sS"), sch.tok("sT"),
                    sch.tok("sLA"), sch.tok("sLB")]

        phases = os.environ.get("ATH_PHASES", "123")
        p1_stores = []
        if "1" in phases:
            ffn_phase("f1", x_in, xs1, lambda st: None, p1_stores, entry_toks())
        p2_stores = []
        if "2" in phases:
            griffin_phase(xs1 if "1" in phases else x_in, xs2,
                          p1_stores, p2_stores, entry_toks())
        if "3" in phases:
            src3 = xs2 if "2" in phases else (xs1 if "1" in phases else x_in)
            p3_stores = []
            ffn_phase("f3", src3, y_out,
                      (lambda st: p2_stores[-1]) if p2_stores else
                      ((lambda st: p1_stores[-1]) if p1_stores else (lambda st: None)),
                      p3_stores, entry_toks())

        with nc.Block("fin") as blk:
            E("sync", lambda q: q.nop(),
              waits=[sch.tok("sO"), sch.tok("sOA"), sch.tok("sOB"), sch.tok("sW"),
                     sch.tok("sLA"), sch.tok("sLB"), sch.tok("sC")])
            sch.flush(blk)

    return nc


_NC_CACHE = {}


def _get_nc(B_core, S_seq):
    key = (B_core, S_seq)
    if key not in _NC_CACHE:
        _NC_CACHE[key] = build_program(B_core, S_seq)
    return _NC_CACHE[key]


def kernel(L0_attn_x, L0_attn_Wa, L0_attn_Wx, L0_attn_ba, L0_attn_bx, L0_attn_Lam,
           L0_ffn_w1, L0_ffn_b1, L0_ffn_w2, L0_ffn_b2):
    x = np.ascontiguousarray(np.asarray(L0_attn_x, dtype=np.float32))
    B, S, _ = x.shape
    B_core = B // NCORES
    nc = _get_nc(B_core, S)
    common = dict(
        L0_attn_Wa=np.ascontiguousarray(np.asarray(L0_attn_Wa, dtype=np.float32)),
        L0_attn_Wx=np.ascontiguousarray(np.asarray(L0_attn_Wx, dtype=np.float32)),
        L0_attn_ba=np.ascontiguousarray(np.asarray(L0_attn_ba, dtype=np.float32)),
        L0_attn_bx=np.ascontiguousarray(np.asarray(L0_attn_bx, dtype=np.float32)),
        L0_attn_Lam=np.ascontiguousarray(np.asarray(L0_attn_Lam, dtype=np.float32)),
        L0_ffn_w1=np.ascontiguousarray(np.asarray(L0_ffn_w1, dtype=np.float32)),
        L0_ffn_b1=np.ascontiguousarray(np.asarray(L0_ffn_b1, dtype=np.float32)),
        L0_ffn_w2=np.ascontiguousarray(np.asarray(L0_ffn_w2, dtype=np.float32)),
        L0_ffn_b2=np.ascontiguousarray(np.asarray(L0_ffn_b2, dtype=np.float32)),
        ident_f32=np.eye(128, dtype=np.float32),
        ident_bf16=np.eye(128).astype(ml_dtypes.bfloat16),
    )
    in_maps = []
    for c in range(NCORES):
        m = dict(common)
        m["L0_attn_x"] = np.ascontiguousarray(x[c * B_core:(c + 1) * B_core])
        in_maps.append(m)
    res = run_bass_kernel_spmd(nc, in_maps, core_ids=list(range(NCORES)))
    return np.concatenate([np.asarray(r["y"], dtype=np.float32) for r in res.results],
                          axis=0)


# version 13
# speedup vs baseline: 2.0324x; 1.0169x over previous
import os
import numpy as np
import ml_dtypes
from contextlib import ExitStack

import concourse.bass as bass
import concourse.mybir as mybir
from concourse.bass_utils import run_bass_kernel_spmd

F32 = mybir.dt.float32
BF16 = mybir.dt.bfloat16
I32 = mybir.dt.int32
AF = mybir.ActivationFunctionType
ALU = mybir.AluOpType
AX = mybir.AxisListType

D = 1024
F = 4096
NCORES = 8
EPS = 1e-5
C_GRIF = 8.0

ENGINES = ("sync", "tensor", "vector", "scalar", "gpsimd")


def sview(t, off, stride, count, rowlen, parts=128):
    return bass.AP(t, off, [[rowlen, parts], [stride, count]])


class Sched:

    def __init__(self, nc, semnames):
        self.nc = nc
        self.sems = {}
        self.count = {n: 0 for n in semnames}
        self.semnames = semnames
        self.q = {e: [] for e in ENGINES}

    def make_sems(self, stack):
        for n in self.semnames:
            self.sems[n] = stack.enter_context(self.nc.semaphore(n))

    OWN = {"vector": "sV", "scalar": "sS", "gpsimd": "sG"}

    def emit(self, eng, fn, inc=None, waits=()):
        waits = [w for w in waits if w is not None]
        for s, v in waits:
            if v > self.count[s]:
                raise AssertionError(f"wait on future {s}: {v} > {self.count[s]}")
        own = self.OWN.get(eng)
        if own is not None:
            waits = [(own, self.count[own])] + waits
        self.q[eng].append((tuple(waits), fn, inc))
        if inc is not None:
            self.count[inc[0]] += inc[1]
        return (inc[0], self.count[inc[0]]) if inc is not None else None

    def tok(self, sem):
        return (sem, self.count[sem])

    def copy(self, eng, dst, src, waits=()):
        if eng == "scalar":
            return self.emit("scalar", lambda s, dst=dst, src=src: s.copy(dst, src),
                             inc=("sS", 1), waits=waits)
        return self.emit("vector", lambda v, dst=dst, src=src: v.tensor_copy(dst, src),
                         inc=("sV", 1), waits=waits)

    def flush(self, block):
        for eng in ENGINES:
            ops = self.q[eng]
            if not ops:
                continue

            def body(engine, ops=ops):
                lastw = {}
                for waits, fn, inc in ops:
                    for s, v in waits:
                        if lastw.get(s, -1) >= v or v <= 0:
                            continue
                        engine.wait_ge(self.sems[s], v)
                        lastw[s] = v
                    if fn is not None:
                        ins = fn(engine)
                        if inc is not None:
                            ins.then_inc(self.sems[inc[0]], inc[1])

            getattr(block, eng)(body)
        self.q = {e: [] for e in ENGINES}


def build_program(B_core=2, S_seq=4096):
    NTOK = B_core * S_seq
    T1 = 256
    NST1 = NTOK // T1
    NST2_SEQ = S_seq // 512
    GPERSEQ = S_seq // 128

    nc = bass.Bass("TRN2", target_bir_lowering=False)

    x_in = nc.dram_tensor("L0_attn_x", [B_core, S_seq, D], F32, kind="ExternalInput").ap()
    wa_in = nc.dram_tensor("L0_attn_Wa", [D, D], F32, kind="ExternalInput").ap()
    wx_in = nc.dram_tensor("L0_attn_Wx", [D, D], F32, kind="ExternalInput").ap()
    ba_in = nc.dram_tensor("L0_attn_ba", [D], F32, kind="ExternalInput").ap()
    bx_in = nc.dram_tensor("L0_attn_bx", [D], F32, kind="ExternalInput").ap()
    lam_in = nc.dram_tensor("L0_attn_Lam", [D], F32, kind="ExternalInput").ap()
    w1_in = nc.dram_tensor("L0_ffn_w1", [F, D], F32, kind="ExternalInput").ap()
    b1_in = nc.dram_tensor("L0_ffn_b1", [F], F32, kind="ExternalInput").ap()
    w2_in = nc.dram_tensor("L0_ffn_w2", [D, F], F32, kind="ExternalInput").ap()
    b2_in = nc.dram_tensor("L0_ffn_b2", [D], F32, kind="ExternalInput").ap()
    id_f_in = nc.dram_tensor("ident_f32", [128, 128], F32, kind="ExternalInput").ap()
    id_b_in = nc.dram_tensor("ident_bf16", [128, 128], BF16, kind="ExternalInput").ap()
    y_out = nc.dram_tensor("y", [B_core, S_seq, D], F32, kind="ExternalOutput").ap()

    wq1T = nc.dram_tensor("wq1T", [D, F], BF16, kind="Internal").ap()
    wq2T = nc.dram_tensor("wq2T", [F, D], BF16, kind="Internal").ap()
    wqaT = nc.dram_tensor("wqaT", [D, D], BF16, kind="Internal").ap()
    wqxT = nc.dram_tensor("wqxT", [D, D], BF16, kind="Internal").ap()
    xs1 = nc.dram_tensor("xs1", [B_core, S_seq, D], F32, kind="Internal").ap()
    xs2 = nc.dram_tensor("xs2", [B_core, S_seq, D], F32, kind="Internal").ap()

    sch = Sched(nc, ["sC", "sW", "sLA", "sLB", "sOA", "sOB", "sO", "sT", "sV", "sS", "sG"])
    E = sch.emit

    def gtile(g):
        return g // GPERSEQ, (g % GPERSEQ) * 128

    with ExitStack() as top:
        sch.make_sems(top)
        sb = lambda name, shape, dt=F32: top.enter_context(nc.sbuf_tensor(name, shape, dt))

        ident_f = sb("ident_f", [128, 128], F32)
        ident_b = sb("ident_b", [128, 128], BF16)
        ones_row = sb("ones_row", [1, 128], F32)
        onescol = sb("onescol", [128, 1], F32)
        c256 = sb("c256", [128, 1], F32)
        c16 = sb("c16", [128, 1], F32)
        coef_fm = sb("coef_fm", [128, 8], F32)
        ba_fm = sb("ba_fm", [128, 8], F32)
        bx_fm = sb("bx_fm", [128, 8], F32)
        carry = sb("carry", [128, 8], F32)
        b1c = sb("b1c", [128, F], BF16)
        b2h = sb("b2h", [128, D], BF16)
        wM1 = sb("wM1", [128, 1], F32)
        wM2h = sb("wM2h", [128, 1], F32)
        sw1b = sb("sw1b", [128, 1], F32)
        sw2b = sb("sw2b", [128, 1], F32)
        NC_ = 21
        colp = sb("colp", [128, 3 * NC_], F32)

        with ExitStack() as ph:
            psb = lambda name, shape, dt=F32: ph.enter_context(nc.sbuf_tensor(name, shape, dt))
            wld = [psb("p0_wld0", [128, F]), psb("p0_wld1", [128, F])]
            zw = psb("p0_zw", [128, F])
            qw = psb("p0_qw", [128, F], I32)
            tern = psb("p0_tern", [128, F], BF16)
            twbf = psb("p0_twbf", [128, F], BF16)
            acc32 = psb("p0_acc", [128, 64])
            acc8 = psb("p0_acc8", [128, 32])
            corrw = psb("p0_corrw", [128, F], BF16)
            corrw2 = psb("p0_corrw2", [128, F], BF16)
            stg = [psb("p0_stg0", [128, 512], BF16), psb("p0_stg1", [128, 512], BF16)]
            rowbuf = psb("p0_row", [1, F])
            vec8 = psb("p0_vec8", [8, 128])
            cell = psb("p0_cell", [1, 4])
            pst4 = ph.enter_context(nc.psum_tensor("p0_pst4", [128, 4096], BF16))
            ps_bc = ph.enter_context(nc.psum_tensor("p0_bc", [128, 512], F32))

            with nc.Block("phase0") as blk:
                E("vector", lambda v: v.memset(ones_row[:], 1.0), inc=("sV", 1))
                E("vector", lambda v: v.memset(onescol[:], 1.0), inc=("sV", 1))
                E("vector", lambda v: v.memset(c256[:], 256.0), inc=("sV", 1))
                E("vector", lambda v: v.memset(c16[:], 16.0), inc=("sV", 1))
                E("vector", lambda v: v.memset(carry[:], 0.0), inc=("sV", 1))
                tconst = sch.tok("sV")
                E("sync", lambda q: q.dma_start(ident_f[:], id_f_in), inc=("sC", 16))
                E("sync", lambda q: q.dma_start(ident_b[:], id_b_in), inc=("sC", 16))
                tid = sch.tok("sC")
                for vin, vdst in ((lam_in, coef_fm), (ba_in, ba_fm), (bx_in, bx_fm)):
                    E("sync", lambda q, vin=vin: q.dma_start(
                        vec8[:], vin.rearrange("(c p) -> c p", p=128)),
                      waits=[sch.tok("sT")], inc=("sC", 16))
                    E("tensor", lambda t: t.transpose(ps_bc[:, 0:8], vec8[:],
                                                      ident_f[0:8, 0:8]),
                      waits=[sch.tok("sC"), sch.tok("sV")], inc=("sT", 1))
                    E("vector", lambda v, vdst=vdst: v.tensor_copy(vdst[:], ps_bc[:, 0:8]),
                      waits=[sch.tok("sT")], inc=("sV", 1))
                E("sync", lambda q: q.dma_start(rowbuf[0:1, 0:F], b1_in[None, :]),
                  waits=[sch.tok("sT")], inc=("sC", 16))
                tq6 = sch.tok("sC")
                E("scalar", lambda s: s.activation(coef_fm[:], coef_fm[:], AF.Exp, scale=-1.0),
                  waits=[tq6, sch.tok("sV")], inc=("sS", 1))
                E("scalar", lambda s: s.activation(coef_fm[:], coef_fm[:], AF.Ln, bias=1.0),
                  inc=("sS", 1))
                E("vector", lambda v: v.tensor_single_scalar(coef_fm[:], coef_fm[:], -C_GRIF,
                                                             op=ALU.mult),
                  waits=[sch.tok("sS")], inc=("sV", 1))

                for s5 in range(F // 512):
                    sl = slice(s5 * 512, (s5 + 1) * 512)
                    E("tensor", lambda t, sl=sl: t.matmul(ps_bc[:], ones_row[:],
                                                          rowbuf[0:1, sl], start=True, stop=True),
                      waits=[tq6, tconst, sch.tok("sV")], inc=("sT", 1))
                    E("vector", lambda v, sl=sl: v.tensor_copy(b1c[:, sl], ps_bc[:]),
                      waits=[sch.tok("sT")], inc=("sV", 1))
                E("sync", lambda q: q.dma_start(rowbuf[0:1, 0:D], b2_in[None, :]),
                  waits=[sch.tok("sV")], inc=("sC", 16))
                tq7 = sch.tok("sC")
                for s5 in range(D // 512):
                    sl = slice(s5 * 512, (s5 + 1) * 512)
                    E("tensor", lambda t, sl=sl: t.matmul(ps_bc[:], ones_row[:],
                                                          rowbuf[0:1, sl], start=True, stop=True),
                      waits=[tq7, sch.tok("sS")], inc=("sT", 1))
                    E("scalar", lambda s, sl=sl: s.activation(b2h[:, sl], ps_bc[:],
                                                              AF.Copy, scale=0.5),
                      waits=[sch.tok("sT")], inc=("sS", 1))

                def mean_abs(w_ap, nrows, ncols, acc_base, out_sw, out_m, deq):
                    nch = nrows // 128
                    nsb = ncols // 128
                    for ci in range(nch):
                        wld_t = wld[ci % 2][:, 0:ncols]
                        sl_ = "sLA" if ci % 2 == 0 else "sLB"
                        w = [sch.tok("sS"), sch.tok("sV")]
                        E("sync", lambda q, ci=ci, wld_t=wld_t, w_ap=w_ap: q.dma_start(
                            wld_t, w_ap[ci * 128:(ci + 1) * 128, :]), waits=w, inc=(sl_, 16))
                        tq = sch.tok(sl_)
                        E("scalar", lambda s, wld_t=wld_t: s.activation(
                            zw[:, 0:ncols], wld_t, AF.Abs), waits=[tq], inc=("sS", 1))
                        tz = sch.tok("sS")
                        for sb in range(nsb):
                            E("vector", lambda v, sb=sb: v.reduce_sum(
                                acc8[:, sb:sb + 1], zw[:, sb * 128:(sb + 1) * 128],
                                axis=AX.X), waits=[tz] if sb == 0 else [], inc=("sV", 1))
                        E("vector", lambda v, ci=ci: v.reduce_sum(
                            acc32[:, acc_base + ci:acc_base + ci + 1], acc8[:, 0:nsb],
                            axis=AX.X), inc=("sV", 1))
                    E("vector", lambda v: v.reduce_sum(
                        colp[:, 0:1], acc32[:, acc_base:acc_base + nch], axis=AX.X),
                      waits=[sch.tok("sS")], inc=("sV", 1))
                    E("tensor", lambda t: t.matmul(ps_bc[0:1, 0:1], colp[:, 0:1], onescol[:],
                                                   start=True, stop=True),
                      waits=[sch.tok("sV")], inc=("sT", 1))
                    E("vector", lambda v: v.tensor_copy(cell[0:1, 0:1], ps_bc[0:1, 0:1]),
                      waits=[sch.tok("sT")], inc=("sV", 1))
                    E("vector", lambda v: v.tensor_single_scalar(
                        cell[0:1, 1:2], cell[0:1, 0:1], 1.0 / (nrows * ncols), op=ALU.mult),
                      inc=("sV", 1))
                    E("vector", lambda v: v.tensor_single_scalar(
                        cell[0:1, 1:2], cell[0:1, 1:2], EPS, op=ALU.max), inc=("sV", 1))
                    E("vector", lambda v: v.reciprocal(cell[0:1, 2:3], cell[0:1, 1:2]),
                      inc=("sV", 1))
                    tv2 = sch.tok("sV")
                    E("tensor", lambda t: t.matmul(ps_bc[:, 0:1], ones_row[:], cell[0:1, 2:3],
                                                   start=True, stop=True),
                      waits=[tv2], inc=("sT", 1))
                    E("vector", lambda v, out_sw=out_sw: v.tensor_copy(out_sw, ps_bc[:, 0:1]),
                      waits=[sch.tok("sT")], inc=("sV", 1))
                    E("tensor", lambda t: t.matmul(ps_bc[:, 0:1], ones_row[:], cell[0:1, 1:2],
                                                   start=True, stop=True),
                      waits=[sch.tok("sV")], inc=("sT", 1))
                    E("scalar", lambda s, out_m=out_m, deq=deq: s.activation(
                        out_m, ps_bc[:, 0:1], AF.Copy, scale=deq),
                      waits=[sch.tok("sT")], inc=("sS", 1))

                mean_abs(w1_in, F, D, 0, sw1b[:], wM1[:], 1.0 / 127.0)
                mean_abs(w2_in, D, F, 40, sw2b[:], wM2h[:], 0.5 / 127.0)

                def quant_transpose(w_ap, nrows, ncols, swcol, out_T, do_quant, pctr=[0]):
                    nch = nrows // 128
                    npc = ncols // 128
                    for ci in range(nch):
                        wld_t = wld[ci % 2][:, 0:ncols]
                        sl_ = "sLA" if ci % 2 == 0 else "sLB"
                        w = [sch.tok("sT"), sch.tok("sS"), sch.tok("sV")]
                        E("sync", lambda q, ci=ci, wld_t=wld_t, w_ap=w_ap: q.dma_start(
                            wld_t, w_ap[ci * 128:(ci + 1) * 128, :]), waits=w, inc=(sl_, 16))
                        tq = sch.tok(sl_)
                        if do_quant:
                            E("scalar", lambda s, wld_t=wld_t, swcol=swcol: s.activation(
                                zw[:, 0:ncols], wld_t, AF.Identity, scale=swcol),
                              waits=[tq, sch.tok("sV")], inc=("sS", 1))
                            E("scalar", lambda s: s.copy(qw[:, 0:ncols], zw[:, 0:ncols]),
                              inc=("sS", 1))
                            tqq = sch.tok("sS")
                            E("vector", lambda v: v.scalar_tensor_tensor(
                                corrw[:, 0:ncols], zw[:, 0:ncols], -0.5, qw[:, 0:ncols],
                                op0=ALU.add, op1=ALU.is_ge), waits=[tqq], inc=("sV", 1))
                            E("vector", lambda v: v.scalar_tensor_tensor(
                                corrw2[:, 0:ncols], zw[:, 0:ncols], 0.5, qw[:, 0:ncols],
                                op0=ALU.add, op1=ALU.is_le), inc=("sV", 1))
                            E("vector", lambda v: v.scalar_tensor_tensor(
                                zw[:, 0:ncols], corrw[:, 0:ncols], 1.0, qw[:, 0:ncols],
                                op0=ALU.mult, op1=ALU.add), inc=("sV", 1))
                            E("vector", lambda v: v.tensor_sub(
                                zw[:, 0:ncols], zw[:, 0:ncols], corrw2[:, 0:ncols]),
                              inc=("sV", 1))
                            E("vector", lambda v: v.tensor_scalar(
                                tern[:, 0:ncols], zw[:, 0:ncols], 1.0, -1.0,
                                op0=ALU.min, op1=ALU.max),
                              waits=[sch.tok("sT")], inc=("sV", 1))
                            src = tern
                        else:
                            E("vector", lambda v, wld_t=wld_t: v.tensor_copy(
                                twbf[:, 0:ncols], wld_t),
                              waits=[tq, sch.tok("sT")], inc=("sV", 1))
                            src = twbf
                        tsrc = sch.tok("sV")
                        for q4 in range(npc // 4):
                            gp = pctr[0]; pctr[0] += 1
                            pr = gp % 2
                            so_ = "sOA" if pr == 0 else "sOB"
                            for ci4 in range(4):
                                pi = q4 * 4 + ci4
                                E("tensor", lambda t, pi=pi, pr=pr, ci4=ci4, src=src:
                                  t.transpose(
                                    pst4[:, pr * 2048 + ci4 * 512: pr * 2048 + ci4 * 512 + 128],
                                    src[:, pi * 128:(pi + 1) * 128], ident_b[:]),
                                  waits=([tsrc, sch.tok("sV")] if ci4 == 0 else []),
                                  inc=("sT", 1))
                            tt = sch.tok("sT")
                            wev = [tt]
                            if gp >= 2:
                                wev.append(sch.tok(so_))
                            psrc = bass.AP(pst4, pr * 2048, [[4096, 128], [512, 4], [1, 128]])
                            E("vector", lambda v, pr=pr, psrc=psrc: v.tensor_copy(
                                stg[pr][:], psrc), waits=wev, inc=("sV", 1))
                            tv = sch.tok("sV")
                            dst = bass.AP(out_T.tensor, q4 * 512 * out_T.shape[1] + ci * 128,
                                          [[out_T.shape[1], 128],
                                           [128 * out_T.shape[1], 4], [1, 128]])
                            E("sync", lambda q, dst=dst, pr=pr: q.dma_start(dst, stg[pr][:]),
                              waits=[tv], inc=(so_, 16))

                quant_transpose(w1_in, F, D, sw1b[:], wq1T, True)
                quant_transpose(w2_in, D, F, sw2b[:], wq2T, True)
                quant_transpose(wa_in, D, D, None, wqaT, False)
                quant_transpose(wx_in, D, D, None, wqxT, False)
                sch.flush(blk)

        tok_p0a = sch.tok("sOA")
        tok_p0b = sch.tok("sOB")

        def ffn_phase(name, src_ap, dst_ap, src_tok_fn, store_toks_out, entry):
            NT = NTOK // 128
            with ExitStack() as ph:
                psb = lambda n, shape, dt=F32: ph.enter_context(
                    nc.sbuf_tensor(name + n, shape, dt))
                w1T = psb("_w1T", [128, 8 * F], BF16)
                w2T = psb("_w2T", [128, 32 * D], BF16)
                xb = [psb(f"_xb{i}", [128, D]) for i in range(3)]
                rb1 = psb("_rb1", [128, D], BF16)
                r1T = psb("_r1T", [128, 2048], BF16)
                hbuf = psb("_h", [128, F])
                zq = [psb(f"_zq{i}", [128, 1024]) for i in range(2)]
                qi = [psb(f"_qi{i}", [128, 1024], mybir.dt.int16) for i in range(2)]
                gbuf = [psb(f"_g{i}", [128, F], BF16) for i in range(2)]
                gstg2_ = psb("_gs2", [128, 1024], BF16)
                gstg2 = [gstg2_[:, 0:512], gstg2_[:, 512:1024]]
                bnh = psb("_bnh", [128, 48])
                bnb = bnh
                sqm2 = psb("_sqm2", [128, 16])
                sqm = sqm2
                pm = [ph.enter_context(nc.psum_tensor(f"{name}_pm{i}", [128, 512], F32))
                      for i in range(2)]
                po = [ph.enter_context(nc.psum_tensor(f"{name}_po{i}", [128, 512], F32))
                      for i in range(2)]
                ptr2 = ph.enter_context(nc.psum_tensor(name + "_pt2", [128, 4096], BF16))

                with nc.Block(name) as blk:
                    for k in range(8):
                        E("sync", lambda q, k=k: q.dma_start(
                            w1T[:, k * F:(k + 1) * F], wq1T[k * 128:(k + 1) * 128, :]),
                          waits=[tok_p0a, tok_p0b] + entry, inc=("sW", 16))
                    for fc in range(32):
                        E("sync", lambda q, fc=fc: q.dma_start(
                            w2T[:, fc * D:(fc + 1) * D], wq2T[fc * 128:(fc + 1) * 128, :]),
                          waits=[tok_p0a, tok_p0b] + entry, inc=("sW", 16))
                    tok_w = sch.tok("sW")

                    tiles = {}
                    st = dict(pm={}, po_ev=None, ptr={}, gstg={})

                    def emit_load(j):
                        if j >= NT:
                            return
                        sl_ = "sLA" if j % 2 == 0 else "sLB"
                        waits = [tok_w, src_tok_fn(j)]
                        if j >= 3:
                            waits.append(tiles[j - 3]["stored"])
                        if j >= 2:
                            waits.append(tiles[j - 2]["load"])
                        b, t = gtile(j)
                        E("sync", lambda q, j=j, b=b, t=t: q.dma_start(
                            xb[j % 3][:], src_ap[b, t:t + 128, :]),
                          waits=waits, inc=(sl_, 16))
                        tiles.setdefault(j, {})["load"] = sch.tok(sl_)

                    def stA(j):
                        if j >= NT:
                            return
                        tj = tiles.setdefault(j, {})
                        cb = NC_ * (j % 3)
                        col = lambda i: colp[:, cb + i:cb + i + 1]
                        tj["col"] = col
                        t2 = tiles.get(j - 2, {})
                        t3 = tiles.get(j - 3, {})
                        xt = xb[j % 3][:]
                        E("vector", lambda v: v.reduce_sum(col(0), xt, axis=AX.X),
                          waits=[tj["load"], t3.get("colfree")], inc=("sV", 1))
                        E("vector", lambda v: v.tensor_single_scalar(
                            col(3), col(0), 1.0 / 1024.0, op=ALU.mult), inc=("sV", 1))
                        E("vector", lambda v: v.reduce_max(col(12), xt, axis=AX.X),
                          inc=("sV", 1))
                        E("vector", lambda v: v.tensor_reduce(col(13), xt, axis=AX.X,
                                                              op=ALU.min), inc=("sV", 1))
                        E("vector", lambda v: v.tensor_sub(col(14), col(12), col(3)),
                          inc=("sV", 1))
                        E("vector", lambda v: v.tensor_sub(col(15), col(3), col(13)),
                          inc=("sV", 1))
                        E("vector", lambda v: v.tensor_max(col(14), col(14), col(15)),
                          inc=("sV", 1))
                        E("vector", lambda v: v.tensor_single_scalar(
                            col(15), col(14), 1e-30, op=ALU.max), inc=("sV", 1))
                        E("vector", lambda v: v.reciprocal(col(15), col(15)), inc=("sV", 1))
                        E("vector", lambda v: v.tensor_single_scalar(
                            col(18), col(15), 127.0, op=ALU.mult), inc=("sV", 1))
                        E("vector", lambda v: v.scalar_tensor_tensor(
                            col(19), col(3), -1.0, col(18), op0=ALU.mult, op1=ALU.mult),
                          inc=("sV", 1))
                        E("vector", lambda v: v.tensor_single_scalar(col(19), col(19), 256.0,
                                                                     op=ALU.add), inc=("sV", 1))
                        tcols = sch.tok("sV")
                        tj["tcols"] = tcols
                        E("vector", lambda v: v.tensor_scalar(
                            zq[0][:], xt, col(18), col(19), op0=ALU.mult, op1=ALU.add),
                          waits=[tcols, t2.get("zq0free"),
                                 tiles.get(j - 1, {}).get("r1")], inc=("sV", 1))
                        tz1 = sch.tok("sV")
                        E("scalar", lambda s: s.copy(qi[0][:], zq[0][:]), waits=[tz1],
                          inc=("sS", 1))
                        tq1 = sch.tok("sS")
                        E("vector", lambda v: v.scalar_tensor_tensor(
                            zq[0][:], zq[0][:], -0.5, qi[0][:], op0=ALU.add, op1=ALU.is_ge),
                          waits=[tq1], inc=("sV", 1))
                        E("vector", lambda v: v.scalar_tensor_tensor(
                            rb1[:], zq[0][:], -256.0, qi[0][:], op0=ALU.add, op1=ALU.add),
                          waits=[tiles.get(j - 1, {}).get("tr1_done")], inc=("sV", 1))
                        tj["r1"] = sch.tok("sV")
                        E("vector", lambda v: v.bn_stats(bnb[:, 0:6], xt[:, 0:512]),
                          waits=[t2.get("sqmS")], inc=("sV", 1))
                        E("vector", lambda v: v.bn_stats(bnb[:, 6:12], xt[:, 512:1024]),
                          inc=("sV", 1))
                        tbn = sch.tok("sV")
                        E("vector", lambda v: v.tensor_tensor(
                            sqm[:, 0:4], sview(bnb, 1, 3, 4, 48), sview(bnb, 1, 3, 4, 48),
                            op=ALU.mult), inc=("sV", 1))
                        E("vector", lambda v: v.reduce_sum(col(1), sview(bnb, 2, 3, 4, 48),
                                                           axis=AX.X), inc=("sV", 1))
                        E("vector", lambda v: v.reduce_sum(col(2), sqm[:, 0:4], axis=AX.X),
                          inc=("sV", 1))
                        tj["sqmS"] = sch.tok("sV")
                        E("vector", lambda v: v.tensor_single_scalar(col(4), col(2), 0.25,
                                                                     op=ALU.mult), inc=("sV", 1))
                        E("vector", lambda v: v.scalar_tensor_tensor(
                            col(5), col(1), 1.0 / 1024.0, col(4), op0=ALU.mult, op1=ALU.add),
                          inc=("sV", 1))
                        E("vector", lambda v: v.tensor_mul(col(6), col(3), col(3)), inc=("sV", 1))
                        E("vector", lambda v: v.tensor_sub(col(7), col(5), col(6)), inc=("sV", 1))
                        E("vector", lambda v: v.tensor_single_scalar(col(8), col(7), EPS,
                                                                     op=ALU.add), inc=("sV", 1))
                        E("vector", lambda v: v.scalar_tensor_tensor(
                            col(9), col(8), EPS, col(7), op0=ALU.mult, op1=ALU.add),
                          inc=("sV", 1))
                        tv_ab = sch.tok("sV")
                        E("scalar", lambda s: s.activation(col(10), col(9), AF.Sqrt),
                          waits=[tv_ab], inc=("sS", 1))
                        E("vector", lambda v: v.reciprocal(col(11), col(10)),
                          waits=[sch.tok("sS")], inc=("sV", 1))
                        E("vector", lambda v: v.tensor_mul(col(16), col(11), col(14)),
                          inc=("sV", 1))
                        E("vector", lambda v: v.tensor_single_scalar(col(16), col(16), EPS,
                                                                     op=ALU.max), inc=("sV", 1))
                        E("vector", lambda v: v.tensor_mul(col(20), col(16), wM1[:]),
                          inc=("sV", 1))
                        tj["c1"] = sch.tok("sV")

                    def stTr1(j):
                        if j >= NT:
                            return
                        tj = tiles[j]
                        for q4 in range(2):
                            half = q4 % 2
                            for ci in range(4):
                                c = q4 * 4 + ci
                                E("tensor", lambda t, c=c, half=half, ci=ci: t.transpose(
                                    ptr2[:, half * 2048 + ci * 512: half * 2048 + ci * 512 + 128],
                                    rb1[:, c * 128:(c + 1) * 128], ident_b[:]),
                                  waits=([tj["r1"]] if ci == 0 else []) +
                                        ([st["ptr"].get(half)] if ci == 0 else []),
                                  inc=("sT", 1))
                            ttr = sch.tok("sT")
                            psrc = bass.AP(ptr2, half * 2048, [[4096, 128], [512, 4], [1, 128]])
                            dst = bass.AP(r1T, (j % 2) * 128 + q4 * 1024,
                                          [[2048, 128], [256, 4], [1, 128]])
                            tok = E("scalar", lambda s, dst=dst, psrc=psrc: s.copy(dst, psrc),
                                    waits=[ttr], inc=("sS", 1))
                            st["ptr"][half] = tok
                        tj["r1T_evac"] = sch.tok("sS")
                        tj["tr1_done"] = sch.tok("sT")

                    def stB1(j):
                        tj = tiles[j]
                        tprev = tiles.get(j - 1, {})
                        col = tj["col"]
                        tj["ev1"] = {}
                        gelu_toks = []
                        for f in range(8):
                            bank = pm[f % 2]
                            wmm = [tj["r1T_evac"], st["pm"].get(f % 2)]
                            if f == 0:
                                wmm.append(tok_w)
                            for k in range(8):
                                E("tensor", lambda t, f=f, k=k, bank=bank, j=j: t.matmul(
                                    bank[:],
                                    r1T[:, k * 256 + (j % 2) * 128: k * 256 + (j % 2) * 128 + 128],
                                    w1T[:, k * F + f * 512: k * F + (f + 1) * 512],
                                    start=(k == 0), stop=(k == 7)),
                                  waits=wmm if k == 0 else [], inc=("sT", 1))
                            tmm = sch.tok("sT")
                            wev = [tmm, tj["c1"]]
                            if f == 0:
                                wev.append(tprev.get("hfree"))
                            tok = E("vector", lambda v, f=f, bank=bank, col=col:
                                    v.scalar_tensor_tensor(
                                hbuf[:, f * 512:(f + 1) * 512], bank[:], col(20),
                                b1c[:, f * 512:(f + 1) * 512], op0=ALU.mult, op1=ALU.add),
                              waits=wev, inc=("sV", 1))
                            st["pm"][f % 2] = tok
                            tj["ev1"][f] = tok
                            if f == 3 or f == 7:
                                h0 = 0 if f == 3 else 2048
                                E("scalar", lambda s, h0=h0: s.activation(
                                    hbuf[:, h0:h0 + 2048], hbuf[:, h0:h0 + 2048], AF.Gelu),
                                  waits=[tok], inc=("sS", 1))
                                gelu_toks.append(sch.tok("sS"))
                        tj["gelu"] = gelu_toks
                        tj["mm1_done"] = sch.tok("sT")

                    def stC(j):
                        tj = tiles[j]
                        tnext = tiles.get(j + 1, {})
                        t2 = tiles.get(j - 2, {})
                        col = tj["col"]
                        E("vector", lambda v: v.reduce_max(col(0), hbuf[:], axis=AX.X,
                                                           apply_absolute_value=True),
                          waits=[tj["gelu"][1]], inc=("sV", 1))
                        E("vector", lambda v: v.tensor_single_scalar(
                            col(1), col(0), 1e-30, op=ALU.max), inc=("sV", 1))
                        E("vector", lambda v: v.reciprocal(col(1), col(1)), inc=("sV", 1))
                        E("vector", lambda v: v.tensor_single_scalar(
                            col(12), col(1), 127.0, op=ALU.mult), inc=("sV", 1))
                        tsig2 = sch.tok("sV")
                        tj["r2"] = []
                        for qc in range(4):
                            p2 = qc % 2
                            wz = [tsig2]
                            if qc == 0:
                                wz += [tnext.get("r1"), t2.get("zq0free2")]
                            elif qc == 1:
                                wz.append(t2.get("zq1free"))
                            else:
                                wz.append(tj["r2"][qc - 2])
                            E("vector", lambda v, qc=qc, p2=p2, col=col: v.tensor_scalar(
                                zq[p2][:], hbuf[:, qc * 1024:(qc + 1) * 1024], col(12), 256.0,
                                op0=ALU.mult, op1=ALU.add), waits=wz, inc=("sV", 1))
                            tz2 = sch.tok("sV")
                            wq_ = [tz2] + ([tnext.get("r1")] if qc == 0 else [])
                            E("scalar", lambda s, p2=p2: s.copy(qi[p2][:], zq[p2][:]),
                              waits=wq_, inc=("sS", 1))
                            tq2 = sch.tok("sS")
                            E("vector", lambda v, p2=p2: v.scalar_tensor_tensor(
                                zq[p2][:], zq[p2][:], -0.5, qi[p2][:],
                                op0=ALU.add, op1=ALU.is_ge), waits=[tq2], inc=("sV", 1))
                            tok = E("vector", lambda v, qc=qc, p2=p2, j=j: v.scalar_tensor_tensor(
                                gbuf[j % 2][:, qc * 1024:(qc + 1) * 1024], zq[p2][:], -256.0,
                                qi[p2][:], op0=ALU.add, op1=ALU.add),
                              waits=[t2.get("trg_done")] if qc == 0 else [], inc=("sV", 1))
                            tj["r2"].append(tok)
                        tj["zq0free2"] = tj["r2"][2]
                        tj["zq0free"] = tj["r2"][2]
                        tj["zq1free"] = tj["r2"][3]
                        tj["hfree"] = sch.tok("sV")
                        for i8 in range(8):
                            w = []
                            if i8 == 0:
                                w = [tj["gelu"][1], t2.get("sqm2V")]
                            E("vector", lambda v, i8=i8: v.bn_stats(
                                bnh[:, i8 * 6:(i8 + 1) * 6],
                                hbuf[:, i8 * 512:(i8 + 1) * 512]), waits=w, inc=("sV", 1))
                        E("vector", lambda v: v.tensor_tensor(
                            sqm2[:, 0:16], sview(bnh, 1, 3, 16, 48), sview(bnh, 1, 3, 16, 48),
                            op=ALU.mult), inc=("sV", 1))
                        E("vector", lambda v: v.reduce_sum(col(2), sview(bnh, 2, 3, 16, 48),
                                                           axis=AX.X), inc=("sV", 1))
                        E("vector", lambda v: v.reduce_sum(col(4), sqm2[:, 0:16], axis=AX.X),
                          inc=("sV", 1))
                        tj["sqm2V"] = sch.tok("sV")
                        E("vector", lambda v: v.tensor_single_scalar(col(4), col(4), 1.0 / 16,
                                                                     op=ALU.mult), inc=("sV", 1))
                        E("vector", lambda v: v.scalar_tensor_tensor(
                            col(5), col(2), 1.0 / F, col(4), op0=ALU.mult, op1=ALU.add),
                          inc=("sV", 1))
                        E("vector", lambda v: v.tensor_single_scalar(col(5), col(5), EPS,
                                                                     op=ALU.add), inc=("sV", 1))
                        tv_eh = sch.tok("sV")
                        E("scalar", lambda s: s.activation(col(2), col(5), AF.Sqrt),
                          waits=[tv_eh], inc=("sS", 1))
                        E("vector", lambda v: v.reciprocal(col(2), col(2)),
                          waits=[sch.tok("sS")], inc=("sV", 1))
                        E("vector", lambda v: v.tensor_mul(col(13), col(2), col(0)),
                          inc=("sV", 1))
                        E("vector", lambda v: v.tensor_single_scalar(col(13), col(13), EPS,
                                                                     op=ALU.max), inc=("sV", 1))
                        E("vector", lambda v: v.tensor_mul(col(13), col(13), wM2h[:]),
                          inc=("sV", 1))
                        tj["c2"] = sch.tok("sV")
                        tj["colfree"] = sch.tok("sS")

                    def stB2(j):
                        if j < 0:
                            return
                        tj = tiles[j]
                        col = tj["col"]
                        for q4 in range(8):
                            half = q4 % 2
                            for ci in range(4):
                                fc = q4 * 4 + ci
                                E("tensor", lambda t, fc=fc, half=half, ci=ci, j=j: t.transpose(
                                    ptr2[:, half * 2048 + ci * 512: half * 2048 + ci * 512 + 128],
                                    gbuf[j % 2][:, fc * 128:(fc + 1) * 128], ident_b[:]),
                                  waits=([tj["r2"][fc // 8]] if ci == 0 else []) +
                                        ([st["ptr"].get(half)] if ci == 0 else []),
                                  inc=("sT", 1))
                            ttr = sch.tok("sT")
                            psrc = bass.AP(ptr2, half * 2048, [[4096, 128], [512, 4], [1, 128]])
                            tok = E("scalar", lambda s, psrc=psrc, half=half: s.copy(
                                gstg2[half][:], psrc),
                                waits=[ttr, st["gstg"].get(half)], inc=("sS", 1))
                            st["ptr"][half] = tok
                            wmm2 = [tok]
                            if q4 == 0:
                                wmm2 += [st["po_ev"], tok_w]
                            for pi in range(4):
                                fc = q4 * 4 + pi
                                for n in range(2):
                                    E("tensor", lambda t, fc=fc, n=n, pi=pi, half=half: t.matmul(
                                        po[n][:], gstg2[half][:, pi * 128:(pi + 1) * 128],
                                        w2T[:, fc * D + n * 512: fc * D + (n + 1) * 512],
                                        start=(fc == 0), stop=(fc == 31)),
                                      waits=wmm2 if (pi == 0 and n == 0) else [],
                                      inc=("sT", 1))
                            st["gstg"][half] = sch.tok("sT")
                        tj["trg_done"] = sch.tok("sT")
                        tmm2 = tj["trg_done"]
                        xt = xb[j % 3][:]
                        for n in range(2):
                            E("vector", lambda v, n=n, xt=xt, col=col: v.scalar_tensor_tensor(
                                xt[:, n * 512:(n + 1) * 512], po[n][:], col(13),
                                xt[:, n * 512:(n + 1) * 512], op0=ALU.mult, op1=ALU.add),
                              waits=[tmm2, tj["c2"]] if n == 0 else [], inc=("sV", 1))
                        st["po_ev"] = sch.tok("sV")
                        E("vector", lambda v, xt=xt: v.tensor_add(xt, xt, b2h[:]),
                          inc=("sV", 1))
                        tout = sch.tok("sV")
                        b, t = gtile(j)
                        E("sync", lambda q, j=j, b=b, t=t: q.dma_start(
                            dst_ap[b, t:t + 128, :], xb[j % 3][:]),
                          waits=[tout], inc=("sO", 16))
                        tj["stored"] = sch.tok("sO")
                        store_toks_out.append(tj["stored"])

                    emit_load(0)
                    emit_load(1)
                    stA(0)
                    stTr1(0)
                    for j in range(NT):
                        stA(j + 1)
                        stB1(j)
                        stB2(j - 1)
                        emit_load(j + 2)
                        stTr1(j + 1)
                        stC(j)
                    stB2(NT - 1)
                    sch.flush(blk)

        def griffin_phase(src_ap, dst_ap, src_store_toks, store_toks_out, entry):
            NT2 = B_core * NST2_SEQ
            with ExitStack() as ph:
                psb = lambda n, shape, dt=F32: ph.enter_context(
                    nc.sbuf_tensor("g2" + n, shape, dt))
                waT = psb("_waT", [128, 8 * D], BF16)
                wxT = psb("_wxT", [128, 8 * D], BF16)
                xb = [psb(f"_xb{i}", [128, 4 * D]) for i in range(3)]
                xn_tm = psb("_xn", [128, D])
                xnT_f = [psb(f"_xnTf{i}", [128, 4096]) for i in range(2)]
                xnT_b = [psb(f"_xnTb{i}", [128, 4096], BF16) for i in range(2)]
                a_fm = psb("_a", [128, 4096])
                b_fm = psb("_b", [128, 4096])
                it_fm = psb("_it", [128, 4096])
                rt_tmp = psb("_rt", [128, 512])
                tsc = psb("_tsc", [128, 512])
                pbuf = psb("_pb", [128, 512])
                bnb = psb("_bnb2", [128, 12])
                sqm = psb("_sqm2", [128, 4])
                pg = [ph.enter_context(nc.psum_tensor(f"g2_pg{i}", [128, 512], F32))
                      for i in range(4)]
                ptf4 = ph.enter_context(nc.psum_tensor("g2_pt4", [128, 1024], F32))

                with nc.Block("phase2") as blk:
                    for k in range(8):
                        E("sync", lambda q, k=k: q.dma_start(
                            waT[:, k * D:(k + 1) * D], wqaT[k * 128:(k + 1) * 128, :]),
                          waits=[tok_p0a, tok_p0b] + entry, inc=("sW", 16))
                        E("sync", lambda q, k=k: q.dma_start(
                            wxT[:, k * D:(k + 1) * D], wqxT[k * 128:(k + 1) * 128, :]),
                          waits=[tok_p0a, tok_p0b] + entry, inc=("sW", 16))
                    tok_w = sch.tok("sW")

                    tiles = {}
                    st = dict(ptf={}, pg={})

                    def emit_load(ti):
                        if ti >= NT2:
                            return
                        sl_ = "sLA" if ti % 2 == 0 else "sLB"
                        waits = [tok_w]
                        if src_store_toks:
                            waits.append(src_store_toks[-1])
                        if ti >= 3:
                            waits.append(tiles[ti - 3]["stored"])
                        if ti >= 2:
                            waits.append(tiles[ti - 2]["load"])
                        for m in range(4):
                            b, t = gtile(4 * ti + m)
                            E("sync", lambda q, ti=ti, m=m, b=b, t=t: q.dma_start(
                                xb[ti % 3][:, m * D:(m + 1) * D], src_ap[b, t:t + 128, :]),
                              waits=waits if m == 0 else [], inc=(sl_, 16))
                        tiles.setdefault(ti, {})["load"] = sch.tok(sl_)

                    def stLN(ti):
                        if ti >= NT2:
                            return
                        tj = tiles.setdefault(ti, {})
                        t2 = tiles.get(ti - 2, {})
                        for m in range(4):
                            cb = NC_ * (m % 2)
                            col = lambda i: colp[:, cb + i:cb + i + 1]
                            xt = xb[ti % 3][:, m * D:(m + 1) * D]
                            E("vector", lambda v, xt=xt: v.bn_stats(bnb[:, 0:6], xt[:, 0:512]),
                              waits=[tj["load"], tj.get("colfree%d" % (m % 2)),
                                     t2.get("bnfree")], inc=("sV", 1))
                            E("vector", lambda v, xt=xt: v.bn_stats(bnb[:, 6:12],
                                                                    xt[:, 512:1024]),
                              inc=("sV", 1))
                            tbn = sch.tok("sV")
                            E("vector", lambda v, col=col: v.tensor_tensor(
                                sqm[:, 0:4], sview(bnb, 1, 3, 4, 12), sview(bnb, 1, 3, 4, 12),
                                op=ALU.mult), inc=("sV", 1))
                            E("vector", lambda v, col=col: v.reduce_sum(
                                col(0), sview(bnb, 1, 3, 4, 12), axis=AX.X), inc=("sV", 1))
                            E("vector", lambda v, col=col: v.reduce_sum(
                                col(1), sview(bnb, 2, 3, 4, 12), axis=AX.X), inc=("sV", 1))
                            E("vector", lambda v, col=col: v.reduce_sum(
                                col(2), sqm[:, 0:4], axis=AX.X), inc=("sV", 1))
                            E("vector", lambda v, col=col: v.tensor_single_scalar(
                                col(3), col(0), 0.25, op=ALU.mult), inc=("sV", 1))
                            E("vector", lambda v, col=col: v.tensor_single_scalar(
                                col(4), col(2), 0.25, op=ALU.mult), inc=("sV", 1))
                            E("vector", lambda v, col=col: v.scalar_tensor_tensor(
                                col(5), col(1), 1.0 / 1024.0, col(4), op0=ALU.mult,
                                op1=ALU.add), inc=("sV", 1))
                            E("vector", lambda v, col=col: v.tensor_mul(col(6), col(3), col(3)),
                              inc=("sV", 1))
                            E("vector", lambda v, col=col: v.tensor_sub(col(7), col(5), col(6)),
                              inc=("sV", 1))
                            E("vector", lambda v, col=col: v.tensor_single_scalar(
                                col(8), col(7), EPS, op=ALU.add), inc=("sV", 1))
                            tv_a = sch.tok("sV")
                            E("scalar", lambda s, col=col: s.activation(col(9), col(8), AF.Sqrt),
                              waits=[tv_a], inc=("sS", 1))
                            E("vector", lambda v, col=col: v.reciprocal(col(10), col(9)),
                              waits=[sch.tok("sS")], inc=("sV", 1))
                            E("vector", lambda v, col=col: v.scalar_tensor_tensor(
                                col(11), col(3), -1.0, col(10), op0=ALU.mult, op1=ALU.mult),
                              inc=("sV", 1))
                            tv_c = sch.tok("sV")
                            E("scalar", lambda s, xt=xt, col=col: s.activation(
                                xn_tm[:], xt, AF.Identity, bias=col(11), scale=col(10)),
                              waits=[tv_c, tj.get("xn_tr")], inc=("sS", 1))
                            txn = sch.tok("sS")
                            tj["colfree%d" % (m % 2)] = txn
                            for q4 in range(2):
                                half = q4 % 2
                                for ci in range(4):
                                    c = q4 * 4 + ci
                                    E("tensor", lambda t, c=c, half=half, ci=ci: t.transpose(
                                        ptf4[:, half * 512 + ci * 128:
                                             half * 512 + ci * 128 + 128],
                                        xn_tm[:, c * 128:(c + 1) * 128], ident_f[:]),
                                      waits=([txn, st["ptf"].get(half)] if ci == 0 else []),
                                      inc=("sT", 1))
                                ttr = sch.tok("sT")
                                psrc = bass.AP(ptf4, half * 512,
                                               [[1024, 128], [128, 4], [1, 128]])
                                dstf = bass.AP(xnT_f[ti % 2],
                                               (q4 * 4) * 512 + m * 128,
                                               [[4096, 128], [512, 4], [1, 128]])
                                t1 = E("vector", lambda v, dstf=dstf, psrc=psrc: v.tensor_copy(
                                    dstf, psrc), waits=[ttr], inc=("sV", 1))
                                dstb = bass.AP(xnT_b[ti % 2],
                                               (q4 * 4) * 512 + m * 128,
                                               [[4096, 128], [512, 4], [1, 128]])
                                srcf = dstf
                                E("scalar", lambda s, dstb=dstb, srcf=srcf: s.copy(dstb, srcf),
                                  waits=[t1], inc=("sS", 1))
                                st["ptf"][half] = t1
                            tj["xn_tr"] = sch.tok("sT")
                        tj["xnT_b"] = sch.tok("sS")
                        tj["xnT_f"] = sch.tok("sV")

                    def stMM(ti):
                        tj = tiles[ti]
                        tprev = tiles.get(ti - 1, {})
                        jseq = ti % NST2_SEQ
                        scan_toks = []
                        for c in range(8):
                            bk_rt = pg[(2 * c) % 4]
                            bk_it = pg[(2 * c + 1) % 4]
                            wmm = [tj["xnT_b"], st["pg"].get((2 * c) % 4),
                                   st["pg"].get((2 * c + 1) % 4)]
                            if c == 0:
                                wmm.append(tok_w)
                            for k in range(8):
                                E("tensor", lambda t, c=c, k=k, bk=bk_rt, ti=ti: t.matmul(
                                    bk[:], waT[:, k * D + c * 128: k * D + (c + 1) * 128],
                                    xnT_b[ti % 2][:, k * 512:(k + 1) * 512],
                                    start=(k == 0), stop=(k == 7)),
                                  waits=wmm if k == 0 else [], inc=("sT", 1))
                            for k in range(8):
                                E("tensor", lambda t, c=c, k=k, bk=bk_it, ti=ti: t.matmul(
                                    bk[:], wxT[:, k * D + c * 128: k * D + (c + 1) * 128],
                                    xnT_b[ti % 2][:, k * 512:(k + 1) * 512],
                                    start=(k == 0), stop=(k == 7)), inc=("sT", 1))
                            tmm = sch.tok("sT")
                            E("scalar", lambda s, c=c, bk=bk_rt: s.activation(
                                rt_tmp[:], bk[:], AF.Sigmoid, bias=ba_fm[:, c:c + 1]),
                              waits=[tmm, tprev.get("scan%d" % c)], inc=("sS", 1))
                            E("scalar", lambda s, c=c: s.activation(
                                a_fm[:, c * 512:(c + 1) * 512], rt_tmp[:], AF.Exp,
                                scale=coef_fm[:, c:c + 1]), inc=("sS", 1))
                            st["pg"][(2 * c) % 4] = sch.tok("sS")
                            E("scalar", lambda s, c=c, bk=bk_it: s.activation(
                                it_fm[:, c * 512:(c + 1) * 512], bk[:], AF.Sigmoid,
                                bias=bx_fm[:, c:c + 1]), inc=("sS", 1))
                            st["pg"][(2 * c + 1) % 4] = sch.tok("sS")
                            ta = sch.tok("sS")
                            E("scalar", lambda s, c=c: s.activation(
                                tsc[:], a_fm[:, c * 512:(c + 1) * 512], AF.Square),
                              waits=[ta, st.get("b_done")], inc=("sS", 1))
                            E("scalar", lambda s: s.activation(
                                tsc[:], tsc[:], AF.Sqrt, bias=1.0, scale=-1.0), inc=("sS", 1))
                            tb_ = sch.tok("sS")
                            E("vector", lambda v, c=c, ti=ti: v.tensor_mul(
                                pbuf[:], it_fm[:, c * 512:(c + 1) * 512],
                                xnT_f[ti % 2][:, c * 512:(c + 1) * 512]),
                              waits=[ta, tj["xnT_f"]], inc=("sV", 1))
                            E("vector", lambda v, c=c: v.tensor_mul(
                                b_fm[:, c * 512:(c + 1) * 512], tsc[:], pbuf[:]),
                              waits=[tb_], inc=("sV", 1))
                            st["b_done"] = sch.tok("sV")
                            if jseq == 0:
                                E("vector", lambda v, c=c, ti=ti: v.tensor_tensor_scan(
                                    xnT_f[ti % 2][:, c * 512:(c + 1) * 512],
                                    a_fm[:, c * 512:(c + 1) * 512],
                                    b_fm[:, c * 512:(c + 1) * 512],
                                    0.0, op0=ALU.mult, op1=ALU.add), inc=("sV", 1))
                            else:
                                E("vector", lambda v, c=c, ti=ti: v.tensor_tensor_scan(
                                    xnT_f[ti % 2][:, c * 512:(c + 1) * 512],
                                    a_fm[:, c * 512:(c + 1) * 512],
                                    b_fm[:, c * 512:(c + 1) * 512],
                                    carry[:, c:c + 1], op0=ALU.mult, op1=ALU.add),
                                  inc=("sV", 1))
                            E("vector", lambda v, c=c, ti=ti: v.tensor_copy(
                                carry[:, c:c + 1],
                                xnT_f[ti % 2][:, c * 512 + 511:c * 512 + 512]),
                              inc=("sV", 1))
                            tok = sch.tok("sV")
                            tj["scan%d" % c] = tok
                            scan_toks.append(tok)
                        tj["scans"] = scan_toks

                    def stY(ti):
                        if ti < 0:
                            return
                        tj = tiles[ti]
                        for c in range(8):
                            half = c % 2
                            for m in range(4):
                                cc = c * 512 + m * 128
                                E("tensor", lambda t, cc=cc, half=half, m=m, ti=ti: t.transpose(
                                    ptf4[:, half * 512 + m * 128: half * 512 + m * 128 + 128],
                                    xnT_f[ti % 2][:, cc:cc + 128], ident_f[:]),
                                  waits=([tj["scans"][c], st["ptf"].get(half)]
                                         if m == 0 else []),
                                  inc=("sT", 1))
                            ttr = sch.tok("sT")
                            psrc = bass.AP(ptf4, half * 512, [[1024, 128], [128, 4], [1, 128]])
                            dstx = bass.AP(xb[ti % 3], c * 128, [[4096, 128], [1024, 4], [1, 128]])
                            tok = E("vector", lambda v, psrc=psrc, dstx=dstx: v.scalar_tensor_tensor(
                                dstx, psrc, 1.0, dstx, op0=ALU.mult, op1=ALU.add),
                              waits=[ttr], inc=("sV", 1))
                            st["ptf"][half] = tok
                        tdone = sch.tok("sV")
                        for m in range(4):
                            b, t = gtile(4 * ti + m)
                            E("sync", lambda q, ti=ti, m=m, b=b, t=t: q.dma_start(
                                dst_ap[b, t:t + 128, :], xb[ti % 3][:, m * D:(m + 1) * D]),
                              waits=[tdone] if m == 0 else [], inc=("sO", 16))
                        tj["stored"] = sch.tok("sO")
                        store_toks_out.append(tj["stored"])
                        tj["bnfree"] = tdone

                    emit_load(0)
                    emit_load(1)
                    stLN(0)
                    for ti in range(NT2):
                        stMM(ti)
                        stY(ti - 1)
                        emit_load(ti + 2)
                        stLN(ti + 1)
                    stY(NT2 - 1)
                    sch.flush(blk)

        def entry_toks():
            return [sch.tok("sO"), sch.tok("sV"), sch.tok("sS"), sch.tok("sT"),
                    sch.tok("sLA"), sch.tok("sLB")]

        phases = os.environ.get("ATH_PHASES", "123")
        p1_stores = []
        if "1" in phases:
            ffn_phase("f1", x_in, xs1, lambda st: None, p1_stores, entry_toks())
        p2_stores = []
        if "2" in phases:
            griffin_phase(xs1 if "1" in phases else x_in, xs2,
                          p1_stores, p2_stores, entry_toks())
        if "3" in phases:
            src3 = xs2 if "2" in phases else (xs1 if "1" in phases else x_in)
            p3_stores = []
            ffn_phase("f3", src3, y_out,
                      (lambda st: p2_stores[-1]) if p2_stores else
                      ((lambda st: p1_stores[-1]) if p1_stores else (lambda st: None)),
                      p3_stores, entry_toks())

        with nc.Block("fin") as blk:
            E("sync", lambda q: q.nop(),
              waits=[sch.tok("sO"), sch.tok("sOA"), sch.tok("sOB"), sch.tok("sW"),
                     sch.tok("sLA"), sch.tok("sLB"), sch.tok("sC")])
            sch.flush(blk)

    return nc


_NC_CACHE = {}


def _get_nc(B_core, S_seq):
    key = (B_core, S_seq)
    if key not in _NC_CACHE:
        _NC_CACHE[key] = build_program(B_core, S_seq)
    return _NC_CACHE[key]


def kernel(L0_attn_x, L0_attn_Wa, L0_attn_Wx, L0_attn_ba, L0_attn_bx, L0_attn_Lam,
           L0_ffn_w1, L0_ffn_b1, L0_ffn_w2, L0_ffn_b2):
    x = np.ascontiguousarray(np.asarray(L0_attn_x, dtype=np.float32))
    B, S, _ = x.shape
    B_core = B // NCORES
    nc = _get_nc(B_core, S)
    common = dict(
        L0_attn_Wa=np.ascontiguousarray(np.asarray(L0_attn_Wa, dtype=np.float32)),
        L0_attn_Wx=np.ascontiguousarray(np.asarray(L0_attn_Wx, dtype=np.float32)),
        L0_attn_ba=np.ascontiguousarray(np.asarray(L0_attn_ba, dtype=np.float32)),
        L0_attn_bx=np.ascontiguousarray(np.asarray(L0_attn_bx, dtype=np.float32)),
        L0_attn_Lam=np.ascontiguousarray(np.asarray(L0_attn_Lam, dtype=np.float32)),
        L0_ffn_w1=np.ascontiguousarray(np.asarray(L0_ffn_w1, dtype=np.float32)),
        L0_ffn_b1=np.ascontiguousarray(np.asarray(L0_ffn_b1, dtype=np.float32)),
        L0_ffn_w2=np.ascontiguousarray(np.asarray(L0_ffn_w2, dtype=np.float32)),
        L0_ffn_b2=np.ascontiguousarray(np.asarray(L0_ffn_b2, dtype=np.float32)),
        ident_f32=np.eye(128, dtype=np.float32),
        ident_bf16=np.eye(128).astype(ml_dtypes.bfloat16),
    )
    in_maps = []
    for c in range(NCORES):
        m = dict(common)
        m["L0_attn_x"] = np.ascontiguousarray(x[c * B_core:(c + 1) * B_core])
        in_maps.append(m)
    res = run_bass_kernel_spmd(nc, in_maps, core_ids=list(range(NCORES)))
    return np.concatenate([np.asarray(r["y"], dtype=np.float32) for r in res.results],
                          axis=0)


# version 14
# speedup vs baseline: 2.1545x; 1.0601x over previous
import os
import numpy as np
import ml_dtypes
from contextlib import ExitStack

import concourse.bass as bass
import concourse.mybir as mybir
from concourse.bass_utils import run_bass_kernel_spmd

F32 = mybir.dt.float32
BF16 = mybir.dt.bfloat16
I32 = mybir.dt.int32
AF = mybir.ActivationFunctionType
ALU = mybir.AluOpType
AX = mybir.AxisListType

D = 1024
F = 4096
NCORES = 8
EPS = 1e-5
C_GRIF = 8.0

ENGINES = ("sync", "tensor", "vector", "scalar", "gpsimd")


def sview(t, off, stride, count, rowlen, parts=128):
    return bass.AP(t, off, [[rowlen, parts], [stride, count]])


class Sched:

    def __init__(self, nc, semnames):
        self.nc = nc
        self.sems = {}
        self.count = {n: 0 for n in semnames}
        self.semnames = semnames
        self.q = {e: [] for e in ENGINES}

    def make_sems(self, stack):
        for n in self.semnames:
            self.sems[n] = stack.enter_context(self.nc.semaphore(n))

    OWN = {"vector": "sV", "scalar": "sS", "gpsimd": "sG"}

    def emit(self, eng, fn, inc=None, waits=()):
        waits = [w for w in waits if w is not None]
        for s, v in waits:
            if v > self.count[s]:
                raise AssertionError(f"wait on future {s}: {v} > {self.count[s]}")
        own = self.OWN.get(eng)
        if own is not None:
            waits = [(own, self.count[own])] + waits
        self.q[eng].append((tuple(waits), fn, inc))
        if inc is not None:
            self.count[inc[0]] += inc[1]
        return (inc[0], self.count[inc[0]]) if inc is not None else None

    def tok(self, sem):
        return (sem, self.count[sem])

    def copy(self, eng, dst, src, waits=()):
        if eng == "scalar":
            return self.emit("scalar", lambda s, dst=dst, src=src: s.copy(dst, src),
                             inc=("sS", 1), waits=waits)
        return self.emit("vector", lambda v, dst=dst, src=src: v.tensor_copy(dst, src),
                         inc=("sV", 1), waits=waits)

    def flush(self, block):
        for eng in ENGINES:
            ops = self.q[eng]
            if not ops:
                continue

            def body(engine, ops=ops):
                lastw = {}
                for waits, fn, inc in ops:
                    for s, v in waits:
                        if lastw.get(s, -1) >= v or v <= 0:
                            continue
                        engine.wait_ge(self.sems[s], v)
                        lastw[s] = v
                    if fn is not None:
                        ins = fn(engine)
                        if inc is not None:
                            ins.then_inc(self.sems[inc[0]], inc[1])

            getattr(block, eng)(body)
        self.q = {e: [] for e in ENGINES}


def build_program(B_core=2, S_seq=4096):
    NTOK = B_core * S_seq
    T1 = 256
    NST1 = NTOK // T1
    NST2_SEQ = S_seq // 512
    GPERSEQ = S_seq // 128

    nc = bass.Bass("TRN2", target_bir_lowering=False)

    x_in = nc.dram_tensor("L0_attn_x", [B_core, S_seq, D], F32, kind="ExternalInput").ap()
    wa_in = nc.dram_tensor("L0_attn_Wa", [D, D], F32, kind="ExternalInput").ap()
    wx_in = nc.dram_tensor("L0_attn_Wx", [D, D], F32, kind="ExternalInput").ap()
    ba_in = nc.dram_tensor("L0_attn_ba", [D], F32, kind="ExternalInput").ap()
    bx_in = nc.dram_tensor("L0_attn_bx", [D], F32, kind="ExternalInput").ap()
    lam_in = nc.dram_tensor("L0_attn_Lam", [D], F32, kind="ExternalInput").ap()
    w1_in = nc.dram_tensor("L0_ffn_w1", [F, D], F32, kind="ExternalInput").ap()
    b1_in = nc.dram_tensor("L0_ffn_b1", [F], F32, kind="ExternalInput").ap()
    w2_in = nc.dram_tensor("L0_ffn_w2", [D, F], F32, kind="ExternalInput").ap()
    b2_in = nc.dram_tensor("L0_ffn_b2", [D], F32, kind="ExternalInput").ap()
    id_f_in = nc.dram_tensor("ident_f32", [128, 128], F32, kind="ExternalInput").ap()
    id_b_in = nc.dram_tensor("ident_bf16", [128, 128], BF16, kind="ExternalInput").ap()
    y_out = nc.dram_tensor("y", [B_core, S_seq, D], F32, kind="ExternalOutput").ap()

    wq1T = nc.dram_tensor("wq1T", [D, F], BF16, kind="Internal").ap()
    wq2T = nc.dram_tensor("wq2T", [F, D], BF16, kind="Internal").ap()
    wqaT = nc.dram_tensor("wqaT", [D, D], BF16, kind="Internal").ap()
    wqxT = nc.dram_tensor("wqxT", [D, D], BF16, kind="Internal").ap()
    xs1 = nc.dram_tensor("xs1", [B_core, S_seq, D], F32, kind="Internal").ap()
    xs2 = nc.dram_tensor("xs2", [B_core, S_seq, D], F32, kind="Internal").ap()

    sch = Sched(nc, ["sC", "sW", "sLA", "sLB", "sOA", "sOB", "sO", "sT", "sV", "sS", "sG"])
    E = sch.emit

    def gtile(g):
        return g // GPERSEQ, (g % GPERSEQ) * 128

    with ExitStack() as top:
        sch.make_sems(top)
        sb = lambda name, shape, dt=F32: top.enter_context(nc.sbuf_tensor(name, shape, dt))

        ident_f = sb("ident_f", [128, 128], F32)
        ident_b = sb("ident_b", [128, 128], BF16)
        ones_row = sb("ones_row", [1, 128], F32)
        onescol = sb("onescol", [128, 1], F32)
        c256 = sb("c256", [128, 1], F32)
        c16 = sb("c16", [128, 1], F32)
        coef_fm = sb("coef_fm", [128, 8], F32)
        ba_fm = sb("ba_fm", [128, 8], F32)
        bx_fm = sb("bx_fm", [128, 8], F32)
        carry = sb("carry", [128, 8], F32)
        b1c = sb("b1c", [128, F], BF16)
        b2h = sb("b2h", [128, D], BF16)
        wM1 = sb("wM1", [128, 1], F32)
        wM2h = sb("wM2h", [128, 1], F32)
        sw1b = sb("sw1b", [128, 1], F32)
        sw2b = sb("sw2b", [128, 1], F32)
        NC_ = 21
        colp = sb("colp", [128, 3 * NC_], F32)

        with ExitStack() as ph:
            psb = lambda name, shape, dt=F32: ph.enter_context(nc.sbuf_tensor(name, shape, dt))
            wld = [psb("p0_wld0", [128, F]), psb("p0_wld1", [128, F])]
            zw = psb("p0_zw", [128, F])
            qw = psb("p0_qw", [128, F], I32)
            tern = psb("p0_tern", [128, F], BF16)
            twbf = psb("p0_twbf", [128, F], BF16)
            acc32 = psb("p0_acc", [128, 64])
            acc8 = psb("p0_acc8", [128, 32])
            corrw = psb("p0_corrw", [128, F], BF16)
            corrw2 = psb("p0_corrw2", [128, F], BF16)
            stg = [psb("p0_stg0", [128, 512], BF16), psb("p0_stg1", [128, 512], BF16)]
            rowbuf = psb("p0_row", [1, F])
            vec8 = psb("p0_vec8", [8, 128])
            cell = psb("p0_cell", [1, 4])
            pst4 = ph.enter_context(nc.psum_tensor("p0_pst4", [128, 4096], BF16))
            ps_bc = ph.enter_context(nc.psum_tensor("p0_bc", [128, 512], F32))

            with nc.Block("phase0") as blk:
                E("vector", lambda v: v.memset(ones_row[:], 1.0), inc=("sV", 1))
                E("vector", lambda v: v.memset(onescol[:], 1.0), inc=("sV", 1))
                E("vector", lambda v: v.memset(c256[:], 256.0), inc=("sV", 1))
                E("vector", lambda v: v.memset(c16[:], 16.0), inc=("sV", 1))
                E("vector", lambda v: v.memset(carry[:], 0.0), inc=("sV", 1))
                tconst = sch.tok("sV")
                E("sync", lambda q: q.dma_start(ident_f[:], id_f_in), inc=("sC", 16))
                E("sync", lambda q: q.dma_start(ident_b[:], id_b_in), inc=("sC", 16))
                tid = sch.tok("sC")
                for vin, vdst in ((lam_in, coef_fm), (ba_in, ba_fm), (bx_in, bx_fm)):
                    E("sync", lambda q, vin=vin: q.dma_start(
                        vec8[:], vin.rearrange("(c p) -> c p", p=128)),
                      waits=[sch.tok("sT")], inc=("sC", 16))
                    E("tensor", lambda t: t.transpose(ps_bc[:, 0:8], vec8[:],
                                                      ident_f[0:8, 0:8]),
                      waits=[sch.tok("sC"), sch.tok("sV")], inc=("sT", 1))
                    E("vector", lambda v, vdst=vdst: v.tensor_copy(vdst[:], ps_bc[:, 0:8]),
                      waits=[sch.tok("sT")], inc=("sV", 1))
                E("sync", lambda q: q.dma_start(rowbuf[0:1, 0:F], b1_in[None, :]),
                  waits=[sch.tok("sT")], inc=("sC", 16))
                tq6 = sch.tok("sC")
                E("scalar", lambda s: s.activation(coef_fm[:], coef_fm[:], AF.Exp, scale=-1.0),
                  waits=[tq6, sch.tok("sV")], inc=("sS", 1))
                E("scalar", lambda s: s.activation(coef_fm[:], coef_fm[:], AF.Ln, bias=1.0),
                  inc=("sS", 1))
                E("vector", lambda v: v.tensor_single_scalar(coef_fm[:], coef_fm[:], -C_GRIF,
                                                             op=ALU.mult),
                  waits=[sch.tok("sS")], inc=("sV", 1))

                for s5 in range(F // 512):
                    sl = slice(s5 * 512, (s5 + 1) * 512)
                    E("tensor", lambda t, sl=sl: t.matmul(ps_bc[:], ones_row[:],
                                                          rowbuf[0:1, sl], start=True, stop=True),
                      waits=[tq6, tconst, sch.tok("sV")], inc=("sT", 1))
                    E("vector", lambda v, sl=sl: v.tensor_copy(b1c[:, sl], ps_bc[:]),
                      waits=[sch.tok("sT")], inc=("sV", 1))
                E("sync", lambda q: q.dma_start(rowbuf[0:1, 0:D], b2_in[None, :]),
                  waits=[sch.tok("sV")], inc=("sC", 16))
                tq7 = sch.tok("sC")
                for s5 in range(D // 512):
                    sl = slice(s5 * 512, (s5 + 1) * 512)
                    E("tensor", lambda t, sl=sl: t.matmul(ps_bc[:], ones_row[:],
                                                          rowbuf[0:1, sl], start=True, stop=True),
                      waits=[tq7, sch.tok("sS")], inc=("sT", 1))
                    E("scalar", lambda s, sl=sl: s.activation(b2h[:, sl], ps_bc[:],
                                                              AF.Copy, scale=0.5),
                      waits=[sch.tok("sT")], inc=("sS", 1))

                wfree = {}

                def mean_abs(w_ap, nrows, ncols, acc_base, out_sw, out_m, deq):
                    nch = nrows // 128
                    nsb = ncols // 128
                    zwfree = None
                    for ci in range(nch):
                        wld_t = wld[ci % 2][:, 0:ncols]
                        sl_ = "sLA" if ci % 2 == 0 else "sLB"
                        w = [wfree.get(ci % 2)]
                        E("sync", lambda q, ci=ci, wld_t=wld_t, w_ap=w_ap: q.dma_start(
                            wld_t, w_ap[ci * 128:(ci + 1) * 128, :]), waits=w, inc=(sl_, 16))
                        tq = sch.tok(sl_)
                        E("scalar", lambda s, wld_t=wld_t: s.activation(
                            zw[:, 0:ncols], wld_t, AF.Abs), waits=[tq, zwfree], inc=("sS", 1))
                        tz = sch.tok("sS")
                        wfree[ci % 2] = tz
                        for sb in range(nsb):
                            E("vector", lambda v, sb=sb: v.reduce_sum(
                                acc8[:, sb:sb + 1], zw[:, sb * 128:(sb + 1) * 128],
                                axis=AX.X), waits=[tz] if sb == 0 else [], inc=("sV", 1))
                        zwfree = sch.tok("sV")
                        E("vector", lambda v, ci=ci: v.reduce_sum(
                            acc32[:, acc_base + ci:acc_base + ci + 1], acc8[:, 0:nsb],
                            axis=AX.X), inc=("sV", 1))
                    E("vector", lambda v: v.reduce_sum(
                        colp[:, 0:1], acc32[:, acc_base:acc_base + nch], axis=AX.X),
                      waits=[sch.tok("sS")], inc=("sV", 1))
                    E("tensor", lambda t: t.matmul(ps_bc[0:1, 0:1], colp[:, 0:1], onescol[:],
                                                   start=True, stop=True),
                      waits=[sch.tok("sV")], inc=("sT", 1))
                    E("vector", lambda v: v.tensor_copy(cell[0:1, 0:1], ps_bc[0:1, 0:1]),
                      waits=[sch.tok("sT")], inc=("sV", 1))
                    E("vector", lambda v: v.tensor_single_scalar(
                        cell[0:1, 1:2], cell[0:1, 0:1], 1.0 / (nrows * ncols), op=ALU.mult),
                      inc=("sV", 1))
                    E("vector", lambda v: v.tensor_single_scalar(
                        cell[0:1, 1:2], cell[0:1, 1:2], EPS, op=ALU.max), inc=("sV", 1))
                    E("vector", lambda v: v.reciprocal(cell[0:1, 2:3], cell[0:1, 1:2]),
                      inc=("sV", 1))
                    tv2 = sch.tok("sV")
                    E("tensor", lambda t: t.matmul(ps_bc[:, 0:1], ones_row[:], cell[0:1, 2:3],
                                                   start=True, stop=True),
                      waits=[tv2], inc=("sT", 1))
                    E("vector", lambda v, out_sw=out_sw: v.tensor_copy(out_sw, ps_bc[:, 0:1]),
                      waits=[sch.tok("sT")], inc=("sV", 1))
                    E("tensor", lambda t: t.matmul(ps_bc[:, 0:1], ones_row[:], cell[0:1, 1:2],
                                                   start=True, stop=True),
                      waits=[sch.tok("sV")], inc=("sT", 1))
                    E("scalar", lambda s, out_m=out_m, deq=deq: s.activation(
                        out_m, ps_bc[:, 0:1], AF.Copy, scale=deq),
                      waits=[sch.tok("sT")], inc=("sS", 1))

                mean_abs(w1_in, F, D, 0, sw1b[:], wM1[:], 1.0 / 127.0)
                mean_abs(w2_in, D, F, 40, sw2b[:], wM2h[:], 0.5 / 127.0)

                def quant_transpose(w_ap, nrows, ncols, swcol, out_T, do_quant, pctr=[0]):
                    nch = nrows // 128
                    npc = ncols // 128
                    for ci in range(nch):
                        wld_t = wld[ci % 2][:, 0:ncols]
                        sl_ = "sLA" if ci % 2 == 0 else "sLB"
                        w = [wfree.get(ci % 2)]
                        E("sync", lambda q, ci=ci, wld_t=wld_t, w_ap=w_ap: q.dma_start(
                            wld_t, w_ap[ci * 128:(ci + 1) * 128, :]), waits=w, inc=(sl_, 16))
                        tq = sch.tok(sl_)
                        if do_quant:
                            E("scalar", lambda s, wld_t=wld_t, swcol=swcol: s.activation(
                                zw[:, 0:ncols], wld_t, AF.Identity, scale=swcol),
                              waits=[tq, sch.tok("sV")], inc=("sS", 1))
                            wfree[ci % 2] = sch.tok("sS")
                            E("scalar", lambda s: s.copy(qw[:, 0:ncols], zw[:, 0:ncols]),
                              inc=("sS", 1))
                            tqq = sch.tok("sS")
                            E("vector", lambda v: v.scalar_tensor_tensor(
                                corrw[:, 0:ncols], zw[:, 0:ncols], -0.5, qw[:, 0:ncols],
                                op0=ALU.add, op1=ALU.is_ge), waits=[tqq], inc=("sV", 1))
                            E("vector", lambda v: v.scalar_tensor_tensor(
                                corrw2[:, 0:ncols], zw[:, 0:ncols], 0.5, qw[:, 0:ncols],
                                op0=ALU.add, op1=ALU.is_le), inc=("sV", 1))
                            E("vector", lambda v: v.scalar_tensor_tensor(
                                zw[:, 0:ncols], corrw[:, 0:ncols], 1.0, qw[:, 0:ncols],
                                op0=ALU.mult, op1=ALU.add), inc=("sV", 1))
                            E("vector", lambda v: v.tensor_sub(
                                zw[:, 0:ncols], zw[:, 0:ncols], corrw2[:, 0:ncols]),
                              inc=("sV", 1))
                            E("vector", lambda v: v.tensor_scalar(
                                tern[:, 0:ncols], zw[:, 0:ncols], 1.0, -1.0,
                                op0=ALU.min, op1=ALU.max),
                              waits=[sch.tok("sT")], inc=("sV", 1))
                            src = tern
                        else:
                            E("vector", lambda v, wld_t=wld_t: v.tensor_copy(
                                twbf[:, 0:ncols], wld_t),
                              waits=[tq, sch.tok("sT")], inc=("sV", 1))
                            wfree[ci % 2] = sch.tok("sV")
                            src = twbf
                        tsrc = sch.tok("sV")
                        for q4 in range(npc // 4):
                            gp = pctr[0]; pctr[0] += 1
                            pr = gp % 2
                            so_ = "sOA" if pr == 0 else "sOB"
                            for ci4 in range(4):
                                pi = q4 * 4 + ci4
                                E("tensor", lambda t, pi=pi, pr=pr, ci4=ci4, src=src:
                                  t.transpose(
                                    pst4[:, pr * 2048 + ci4 * 512: pr * 2048 + ci4 * 512 + 128],
                                    src[:, pi * 128:(pi + 1) * 128], ident_b[:]),
                                  waits=([tsrc, sch.tok("sV")] if ci4 == 0 else []),
                                  inc=("sT", 1))
                            tt = sch.tok("sT")
                            wev = [tt]
                            if gp >= 2:
                                wev.append(sch.tok(so_))
                            psrc = bass.AP(pst4, pr * 2048, [[4096, 128], [512, 4], [1, 128]])
                            E("vector", lambda v, pr=pr, psrc=psrc: v.tensor_copy(
                                stg[pr][:], psrc), waits=wev, inc=("sV", 1))
                            tv = sch.tok("sV")
                            dst = bass.AP(out_T.tensor, q4 * 512 * out_T.shape[1] + ci * 128,
                                          [[out_T.shape[1], 128],
                                           [128 * out_T.shape[1], 4], [1, 128]])
                            E("sync", lambda q, dst=dst, pr=pr: q.dma_start(dst, stg[pr][:]),
                              waits=[tv], inc=(so_, 16))

                quant_transpose(w1_in, F, D, sw1b[:], wq1T, True)
                quant_transpose(w2_in, D, F, sw2b[:], wq2T, True)
                quant_transpose(wa_in, D, D, None, wqaT, False)
                quant_transpose(wx_in, D, D, None, wqxT, False)
                sch.flush(blk)

        tok_p0a = sch.tok("sOA")
        tok_p0b = sch.tok("sOB")

        def ffn_phase(name, src_ap, dst_ap, src_tok_fn, store_toks_out, entry):
            NT = NTOK // 128
            with ExitStack() as ph:
                psb = lambda n, shape, dt=F32: ph.enter_context(
                    nc.sbuf_tensor(name + n, shape, dt))
                w1T = psb("_w1T", [128, 8 * F], BF16)
                w2T = psb("_w2T", [128, 32 * D], BF16)
                xb = [psb(f"_xb{i}", [128, D]) for i in range(3)]
                rb1 = psb("_rb1", [128, D], BF16)
                r1T = psb("_r1T", [128, 2048], BF16)
                hbuf = psb("_h", [128, F])
                zq = [psb(f"_zq{i}", [128, 1024]) for i in range(2)]
                qi = [psb(f"_qi{i}", [128, 1024], mybir.dt.int16) for i in range(2)]
                gbuf = [psb(f"_g{i}", [128, F], BF16) for i in range(2)]
                gstg2_ = psb("_gs2", [128, 1024], BF16)
                gstg2 = [gstg2_[:, 0:512], gstg2_[:, 512:1024]]
                bnh = psb("_bnh", [128, 48])
                bnb = bnh
                sqm2 = psb("_sqm2", [128, 16])
                sqm = sqm2
                pm = [ph.enter_context(nc.psum_tensor(f"{name}_pm{i}", [128, 512], F32))
                      for i in range(2)]
                po = [ph.enter_context(nc.psum_tensor(f"{name}_po{i}", [128, 512], F32))
                      for i in range(2)]
                ptr2 = ph.enter_context(nc.psum_tensor(name + "_pt2", [128, 4096], BF16))

                with nc.Block(name) as blk:
                    for k in range(8):
                        E("sync", lambda q, k=k: q.dma_start(
                            w1T[:, k * F:(k + 1) * F], wq1T[k * 128:(k + 1) * 128, :]),
                          waits=[tok_p0a, tok_p0b] + entry, inc=("sW", 16))
                    for fc in range(32):
                        E("sync", lambda q, fc=fc: q.dma_start(
                            w2T[:, fc * D:(fc + 1) * D], wq2T[fc * 128:(fc + 1) * 128, :]),
                          waits=[tok_p0a, tok_p0b] + entry, inc=("sW", 16))
                    tok_w = sch.tok("sW")

                    tiles = {}
                    st = dict(pm={}, po_ev=None, ptr={}, gstg={})

                    def emit_load(j):
                        if j >= NT:
                            return
                        sl_ = "sLA" if j % 2 == 0 else "sLB"
                        waits = [tok_w, src_tok_fn(j)]
                        if j >= 3:
                            waits.append(tiles[j - 3]["stored"])
                        if j >= 2:
                            waits.append(tiles[j - 2]["load"])
                        b, t = gtile(j)
                        E("sync", lambda q, j=j, b=b, t=t: q.dma_start(
                            xb[j % 3][:], src_ap[b, t:t + 128, :]),
                          waits=waits, inc=(sl_, 16))
                        tiles.setdefault(j, {})["load"] = sch.tok(sl_)

                    def stA(j):
                        if j >= NT:
                            return
                        tj = tiles.setdefault(j, {})
                        cb = NC_ * (j % 3)
                        col = lambda i: colp[:, cb + i:cb + i + 1]
                        tj["col"] = col
                        t2 = tiles.get(j - 2, {})
                        t3 = tiles.get(j - 3, {})
                        xt = xb[j % 3][:]
                        E("vector", lambda v: v.reduce_sum(col(0), xt, axis=AX.X),
                          waits=[tj["load"], t3.get("colfree")], inc=("sV", 1))
                        E("vector", lambda v: v.tensor_single_scalar(
                            col(3), col(0), 1.0 / 1024.0, op=ALU.mult), inc=("sV", 1))
                        E("vector", lambda v: v.reduce_max(col(12), xt, axis=AX.X),
                          inc=("sV", 1))
                        E("vector", lambda v: v.tensor_reduce(col(13), xt, axis=AX.X,
                                                              op=ALU.min), inc=("sV", 1))
                        E("vector", lambda v: v.tensor_sub(col(14), col(12), col(3)),
                          inc=("sV", 1))
                        E("vector", lambda v: v.tensor_sub(col(15), col(3), col(13)),
                          inc=("sV", 1))
                        E("vector", lambda v: v.tensor_max(col(14), col(14), col(15)),
                          inc=("sV", 1))
                        E("vector", lambda v: v.tensor_single_scalar(
                            col(15), col(14), 1e-30, op=ALU.max), inc=("sV", 1))
                        E("vector", lambda v: v.reciprocal(col(15), col(15)), inc=("sV", 1))
                        E("vector", lambda v: v.tensor_single_scalar(
                            col(18), col(15), 127.0, op=ALU.mult), inc=("sV", 1))
                        E("vector", lambda v: v.scalar_tensor_tensor(
                            col(19), col(3), -1.0, col(18), op0=ALU.mult, op1=ALU.mult),
                          inc=("sV", 1))
                        E("vector", lambda v: v.tensor_single_scalar(col(19), col(19), 256.0,
                                                                     op=ALU.add), inc=("sV", 1))
                        tcols = sch.tok("sV")
                        tj["tcols"] = tcols
                        E("scalar", lambda s: s.activation(
                            zq[0][:], xt, AF.Identity, bias=col(19), scale=col(18)),
                          waits=[tcols, t2.get("zq0free"),
                                 tiles.get(j - 1, {}).get("r1")], inc=("sS", 1))
                        E("scalar", lambda s: s.copy(qi[0][:], zq[0][:]), inc=("sS", 1))
                        tq1 = sch.tok("sS")
                        E("vector", lambda v: v.scalar_tensor_tensor(
                            zq[0][:], zq[0][:], -0.5, qi[0][:], op0=ALU.add, op1=ALU.is_ge),
                          waits=[tq1], inc=("sV", 1))
                        E("vector", lambda v: v.scalar_tensor_tensor(
                            rb1[:], zq[0][:], -256.0, qi[0][:], op0=ALU.add, op1=ALU.add),
                          waits=[tiles.get(j - 1, {}).get("tr1_done")], inc=("sV", 1))
                        tj["r1"] = sch.tok("sV")
                        E("vector", lambda v: v.bn_stats(bnb[:, 0:6], xt[:, 0:512]),
                          waits=[t2.get("sqmS")], inc=("sV", 1))
                        E("vector", lambda v: v.bn_stats(bnb[:, 6:12], xt[:, 512:1024]),
                          inc=("sV", 1))
                        tbn = sch.tok("sV")
                        E("vector", lambda v: v.tensor_tensor(
                            sqm[:, 0:4], sview(bnb, 1, 3, 4, 48), sview(bnb, 1, 3, 4, 48),
                            op=ALU.mult), inc=("sV", 1))
                        E("vector", lambda v: v.reduce_sum(col(1), sview(bnb, 2, 3, 4, 48),
                                                           axis=AX.X), inc=("sV", 1))
                        E("vector", lambda v: v.reduce_sum(col(2), sqm[:, 0:4], axis=AX.X),
                          inc=("sV", 1))
                        tj["sqmS"] = sch.tok("sV")
                        E("vector", lambda v: v.tensor_single_scalar(col(4), col(2), 0.25,
                                                                     op=ALU.mult), inc=("sV", 1))
                        E("vector", lambda v: v.scalar_tensor_tensor(
                            col(5), col(1), 1.0 / 1024.0, col(4), op0=ALU.mult, op1=ALU.add),
                          inc=("sV", 1))
                        E("vector", lambda v: v.tensor_mul(col(6), col(3), col(3)), inc=("sV", 1))
                        E("vector", lambda v: v.tensor_sub(col(7), col(5), col(6)), inc=("sV", 1))
                        E("vector", lambda v: v.tensor_single_scalar(col(8), col(7), EPS,
                                                                     op=ALU.add), inc=("sV", 1))
                        E("vector", lambda v: v.scalar_tensor_tensor(
                            col(9), col(8), EPS, col(7), op0=ALU.mult, op1=ALU.add),
                          inc=("sV", 1))
                        tv_ab = sch.tok("sV")
                        E("scalar", lambda s: s.activation(col(10), col(9), AF.Sqrt),
                          waits=[tv_ab], inc=("sS", 1))
                        E("vector", lambda v: v.reciprocal(col(11), col(10)),
                          waits=[sch.tok("sS")], inc=("sV", 1))
                        E("vector", lambda v: v.tensor_mul(col(16), col(11), col(14)),
                          inc=("sV", 1))
                        E("vector", lambda v: v.tensor_single_scalar(col(16), col(16), EPS,
                                                                     op=ALU.max), inc=("sV", 1))
                        E("vector", lambda v: v.tensor_mul(col(20), col(16), wM1[:]),
                          inc=("sV", 1))
                        tj["c1"] = sch.tok("sV")

                    def stTr1(j):
                        if j >= NT:
                            return
                        tj = tiles[j]
                        for q4 in range(2):
                            half = q4 % 2
                            for ci in range(4):
                                c = q4 * 4 + ci
                                E("tensor", lambda t, c=c, half=half, ci=ci: t.transpose(
                                    ptr2[:, half * 2048 + ci * 512: half * 2048 + ci * 512 + 128],
                                    rb1[:, c * 128:(c + 1) * 128], ident_b[:]),
                                  waits=([tj["r1"]] if ci == 0 else []) +
                                        ([st["ptr"].get(half)] if ci == 0 else []),
                                  inc=("sT", 1))
                            ttr = sch.tok("sT")
                            psrc = bass.AP(ptr2, half * 2048, [[4096, 128], [512, 4], [1, 128]])
                            dst = bass.AP(r1T, (j % 2) * 128 + q4 * 1024,
                                          [[2048, 128], [256, 4], [1, 128]])
                            tok = E("scalar", lambda s, dst=dst, psrc=psrc: s.copy(dst, psrc),
                                    waits=[ttr], inc=("sS", 1))
                            st["ptr"][half] = tok
                        tj["r1T_evac"] = sch.tok("sS")
                        tj["tr1_done"] = sch.tok("sT")

                    def stB1(j):
                        tj = tiles[j]
                        tprev = tiles.get(j - 1, {})
                        col = tj["col"]
                        tj["ev1"] = {}
                        gelu_toks = []
                        for f in range(8):
                            bank = pm[f % 2]
                            wmm = [tj["r1T_evac"], st["pm"].get(f % 2)]
                            if f == 0:
                                wmm.append(tok_w)
                            for k in range(8):
                                E("tensor", lambda t, f=f, k=k, bank=bank, j=j: t.matmul(
                                    bank[:],
                                    r1T[:, k * 256 + (j % 2) * 128: k * 256 + (j % 2) * 128 + 128],
                                    w1T[:, k * F + f * 512: k * F + (f + 1) * 512],
                                    start=(k == 0), stop=(k == 7)),
                                  waits=wmm if k == 0 else [], inc=("sT", 1))
                            tmm = sch.tok("sT")
                            wev = [tmm, tj["c1"]]
                            if f == 0:
                                wev.append(tprev.get("hfree"))
                            tok = E("vector", lambda v, f=f, bank=bank, col=col:
                                    v.scalar_tensor_tensor(
                                hbuf[:, f * 512:(f + 1) * 512], bank[:], col(20),
                                b1c[:, f * 512:(f + 1) * 512], op0=ALU.mult, op1=ALU.add),
                              waits=wev, inc=("sV", 1))
                            st["pm"][f % 2] = tok
                            tj["ev1"][f] = tok
                            if f == 3 or f == 7:
                                h0 = 0 if f == 3 else 2048
                                E("scalar", lambda s, h0=h0: s.activation(
                                    hbuf[:, h0:h0 + 2048], hbuf[:, h0:h0 + 2048], AF.Gelu),
                                  waits=[tok], inc=("sS", 1))
                                gelu_toks.append(sch.tok("sS"))
                        tj["gelu"] = gelu_toks
                        tj["mm1_done"] = sch.tok("sT")

                    def stC(j):
                        tj = tiles[j]
                        tnext = tiles.get(j + 1, {})
                        t2 = tiles.get(j - 2, {})
                        col = tj["col"]
                        E("vector", lambda v: v.reduce_max(col(0), hbuf[:], axis=AX.X,
                                                           apply_absolute_value=True),
                          waits=[tj["gelu"][1]], inc=("sV", 1))
                        E("vector", lambda v: v.tensor_single_scalar(
                            col(1), col(0), 1e-30, op=ALU.max), inc=("sV", 1))
                        E("vector", lambda v: v.reciprocal(col(1), col(1)), inc=("sV", 1))
                        E("vector", lambda v: v.tensor_single_scalar(
                            col(12), col(1), 127.0, op=ALU.mult), inc=("sV", 1))
                        tsig2 = sch.tok("sV")
                        tj["r2"] = []
                        for qc in range(4):
                            p2 = qc % 2
                            wz = [tsig2]
                            if qc == 0:
                                wz += [tnext.get("r1"), t2.get("zq0free2")]
                            elif qc == 1:
                                wz.append(t2.get("zq1free"))
                            else:
                                wz.append(tj["r2"][qc - 2])
                            E("scalar", lambda s, qc=qc, p2=p2, col=col: s.activation(
                                zq[p2][:], hbuf[:, qc * 1024:(qc + 1) * 1024], AF.Identity,
                                bias=c256[:], scale=col(12)), waits=wz, inc=("sS", 1))
                            wq_ = [tnext.get("r1")] if qc == 0 else []
                            E("scalar", lambda s, p2=p2: s.copy(qi[p2][:], zq[p2][:]),
                              waits=wq_, inc=("sS", 1))
                            tq2 = sch.tok("sS")
                            E("vector", lambda v, p2=p2: v.scalar_tensor_tensor(
                                zq[p2][:], zq[p2][:], -0.5, qi[p2][:],
                                op0=ALU.add, op1=ALU.is_ge), waits=[tq2], inc=("sV", 1))
                            tok = E("vector", lambda v, qc=qc, p2=p2, j=j: v.scalar_tensor_tensor(
                                gbuf[j % 2][:, qc * 1024:(qc + 1) * 1024], zq[p2][:], -256.0,
                                qi[p2][:], op0=ALU.add, op1=ALU.add),
                              waits=[t2.get("trg_done")] if qc == 0 else [], inc=("sV", 1))
                            tj["r2"].append(tok)
                        tj["zq0free2"] = tj["r2"][2]
                        tj["zq0free"] = tj["r2"][2]
                        tj["zq1free"] = tj["r2"][3]
                        tj["hfree"] = sch.tok("sV")
                        for i8 in range(8):
                            w = []
                            if i8 == 0:
                                w = [tj["gelu"][1], t2.get("sqm2V")]
                            E("vector", lambda v, i8=i8: v.bn_stats(
                                bnh[:, i8 * 6:(i8 + 1) * 6],
                                hbuf[:, i8 * 512:(i8 + 1) * 512]), waits=w, inc=("sV", 1))
                        E("vector", lambda v: v.tensor_tensor(
                            sqm2[:, 0:16], sview(bnh, 1, 3, 16, 48), sview(bnh, 1, 3, 16, 48),
                            op=ALU.mult), inc=("sV", 1))
                        E("vector", lambda v: v.reduce_sum(col(2), sview(bnh, 2, 3, 16, 48),
                                                           axis=AX.X), inc=("sV", 1))
                        E("vector", lambda v: v.reduce_sum(col(4), sqm2[:, 0:16], axis=AX.X),
                          inc=("sV", 1))
                        tj["sqm2V"] = sch.tok("sV")
                        E("vector", lambda v: v.tensor_single_scalar(col(4), col(4), 1.0 / 16,
                                                                     op=ALU.mult), inc=("sV", 1))
                        E("vector", lambda v: v.scalar_tensor_tensor(
                            col(5), col(2), 1.0 / F, col(4), op0=ALU.mult, op1=ALU.add),
                          inc=("sV", 1))
                        E("vector", lambda v: v.tensor_single_scalar(col(5), col(5), EPS,
                                                                     op=ALU.add), inc=("sV", 1))
                        tv_eh = sch.tok("sV")
                        E("scalar", lambda s: s.activation(col(2), col(5), AF.Sqrt),
                          waits=[tv_eh], inc=("sS", 1))
                        E("vector", lambda v: v.reciprocal(col(2), col(2)),
                          waits=[sch.tok("sS")], inc=("sV", 1))
                        E("vector", lambda v: v.tensor_mul(col(13), col(2), col(0)),
                          inc=("sV", 1))
                        E("vector", lambda v: v.tensor_single_scalar(col(13), col(13), EPS,
                                                                     op=ALU.max), inc=("sV", 1))
                        E("vector", lambda v: v.tensor_mul(col(13), col(13), wM2h[:]),
                          inc=("sV", 1))
                        tj["c2"] = sch.tok("sV")
                        tj["colfree"] = sch.tok("sS")

                    def stB2(j):
                        if j < 0:
                            return
                        tj = tiles[j]
                        col = tj["col"]
                        for q4 in range(8):
                            half = q4 % 2
                            for ci in range(4):
                                fc = q4 * 4 + ci
                                E("tensor", lambda t, fc=fc, half=half, ci=ci, j=j: t.transpose(
                                    ptr2[:, half * 2048 + ci * 512: half * 2048 + ci * 512 + 128],
                                    gbuf[j % 2][:, fc * 128:(fc + 1) * 128], ident_b[:]),
                                  waits=([tj["r2"][fc // 8]] if ci == 0 else []) +
                                        ([st["ptr"].get(half)] if ci == 0 else []),
                                  inc=("sT", 1))
                            ttr = sch.tok("sT")
                            psrc = bass.AP(ptr2, half * 2048, [[4096, 128], [512, 4], [1, 128]])
                            tok = E("scalar", lambda s, psrc=psrc, half=half: s.copy(
                                gstg2[half][:], psrc),
                                waits=[ttr, st["gstg"].get(half)], inc=("sS", 1))
                            st["ptr"][half] = tok
                            wmm2 = [tok]
                            if q4 == 0:
                                wmm2 += [st["po_ev"], tok_w]
                            for pi in range(4):
                                fc = q4 * 4 + pi
                                for n in range(2):
                                    E("tensor", lambda t, fc=fc, n=n, pi=pi, half=half: t.matmul(
                                        po[n][:], gstg2[half][:, pi * 128:(pi + 1) * 128],
                                        w2T[:, fc * D + n * 512: fc * D + (n + 1) * 512],
                                        start=(fc == 0), stop=(fc == 31)),
                                      waits=wmm2 if (pi == 0 and n == 0) else [],
                                      inc=("sT", 1))
                            st["gstg"][half] = sch.tok("sT")
                        tj["trg_done"] = sch.tok("sT")
                        tmm2 = tj["trg_done"]
                        xt = xb[j % 3][:]
                        for n in range(2):
                            E("vector", lambda v, n=n, xt=xt, col=col: v.scalar_tensor_tensor(
                                xt[:, n * 512:(n + 1) * 512], po[n][:], col(13),
                                xt[:, n * 512:(n + 1) * 512], op0=ALU.mult, op1=ALU.add),
                              waits=[tmm2, tj["c2"]] if n == 0 else [], inc=("sV", 1))
                        st["po_ev"] = sch.tok("sV")
                        E("vector", lambda v, xt=xt: v.tensor_add(xt, xt, b2h[:]),
                          inc=("sV", 1))
                        tout = sch.tok("sV")
                        b, t = gtile(j)
                        E("sync", lambda q, j=j, b=b, t=t: q.dma_start(
                            dst_ap[b, t:t + 128, :], xb[j % 3][:]),
                          waits=[tout], inc=("sO", 16))
                        tj["stored"] = sch.tok("sO")
                        store_toks_out.append(tj["stored"])

                    emit_load(0)
                    emit_load(1)
                    stA(0)
                    stTr1(0)
                    for j in range(NT):
                        stA(j + 1)
                        stB1(j)
                        stB2(j - 1)
                        emit_load(j + 2)
                        stTr1(j + 1)
                        stC(j)
                    stB2(NT - 1)
                    sch.flush(blk)

        def griffin_phase(src_ap, dst_ap, src_store_toks, store_toks_out, entry):
            NT2 = B_core * NST2_SEQ
            with ExitStack() as ph:
                psb = lambda n, shape, dt=F32: ph.enter_context(
                    nc.sbuf_tensor("g2" + n, shape, dt))
                waT = psb("_waT", [128, 8 * D], BF16)
                wxT = psb("_wxT", [128, 8 * D], BF16)
                xb = [psb(f"_xb{i}", [128, 4 * D]) for i in range(3)]
                xn_tm = psb("_xn", [128, D])
                xnT_f = [psb(f"_xnTf{i}", [128, 4096]) for i in range(2)]
                xnT_b = [psb(f"_xnTb{i}", [128, 4096], BF16) for i in range(2)]
                a_fm = psb("_a", [128, 4096])
                b_fm = psb("_b", [128, 4096])
                it_fm = psb("_it", [128, 4096])
                rt_tmp = psb("_rt", [128, 512])
                tsc = psb("_tsc", [128, 512])
                pbuf = psb("_pb", [128, 512])
                bnb = psb("_bnb2", [128, 12])
                sqm = psb("_sqm2", [128, 4])
                pg = [ph.enter_context(nc.psum_tensor(f"g2_pg{i}", [128, 512], F32))
                      for i in range(4)]
                ptf4 = ph.enter_context(nc.psum_tensor("g2_pt4", [128, 1024], F32))

                with nc.Block("phase2") as blk:
                    for k in range(8):
                        E("sync", lambda q, k=k: q.dma_start(
                            waT[:, k * D:(k + 1) * D], wqaT[k * 128:(k + 1) * 128, :]),
                          waits=[tok_p0a, tok_p0b] + entry, inc=("sW", 16))
                        E("sync", lambda q, k=k: q.dma_start(
                            wxT[:, k * D:(k + 1) * D], wqxT[k * 128:(k + 1) * 128, :]),
                          waits=[tok_p0a, tok_p0b] + entry, inc=("sW", 16))
                    tok_w = sch.tok("sW")

                    tiles = {}
                    st = dict(ptf={}, pg={})

                    def emit_load(ti):
                        if ti >= NT2:
                            return
                        sl_ = "sLA" if ti % 2 == 0 else "sLB"
                        waits = [tok_w]
                        if src_store_toks:
                            waits.append(src_store_toks[-1])
                        if ti >= 3:
                            waits.append(tiles[ti - 3]["stored"])
                        if ti >= 2:
                            waits.append(tiles[ti - 2]["load"])
                        for m in range(4):
                            b, t = gtile(4 * ti + m)
                            E("sync", lambda q, ti=ti, m=m, b=b, t=t: q.dma_start(
                                xb[ti % 3][:, m * D:(m + 1) * D], src_ap[b, t:t + 128, :]),
                              waits=waits if m == 0 else [], inc=(sl_, 16))
                        tiles.setdefault(ti, {})["load"] = sch.tok(sl_)

                    def stLN(ti):
                        if ti >= NT2:
                            return
                        tj = tiles.setdefault(ti, {})
                        t2 = tiles.get(ti - 2, {})
                        for m in range(4):
                            cb = NC_ * (m % 2)
                            col = lambda i: colp[:, cb + i:cb + i + 1]
                            xt = xb[ti % 3][:, m * D:(m + 1) * D]
                            E("vector", lambda v, xt=xt: v.bn_stats(bnb[:, 0:6], xt[:, 0:512]),
                              waits=[tj["load"], tj.get("colfree%d" % (m % 2)),
                                     t2.get("bnfree")], inc=("sV", 1))
                            E("vector", lambda v, xt=xt: v.bn_stats(bnb[:, 6:12],
                                                                    xt[:, 512:1024]),
                              inc=("sV", 1))
                            tbn = sch.tok("sV")
                            E("vector", lambda v, col=col: v.tensor_tensor(
                                sqm[:, 0:4], sview(bnb, 1, 3, 4, 12), sview(bnb, 1, 3, 4, 12),
                                op=ALU.mult), inc=("sV", 1))
                            E("vector", lambda v, col=col: v.reduce_sum(
                                col(0), sview(bnb, 1, 3, 4, 12), axis=AX.X), inc=("sV", 1))
                            E("vector", lambda v, col=col: v.reduce_sum(
                                col(1), sview(bnb, 2, 3, 4, 12), axis=AX.X), inc=("sV", 1))
                            E("vector", lambda v, col=col: v.reduce_sum(
                                col(2), sqm[:, 0:4], axis=AX.X), inc=("sV", 1))
                            E("vector", lambda v, col=col: v.tensor_single_scalar(
                                col(3), col(0), 0.25, op=ALU.mult), inc=("sV", 1))
                            E("vector", lambda v, col=col: v.tensor_single_scalar(
                                col(4), col(2), 0.25, op=ALU.mult), inc=("sV", 1))
                            E("vector", lambda v, col=col: v.scalar_tensor_tensor(
                                col(5), col(1), 1.0 / 1024.0, col(4), op0=ALU.mult,
                                op1=ALU.add), inc=("sV", 1))
                            E("vector", lambda v, col=col: v.tensor_mul(col(6), col(3), col(3)),
                              inc=("sV", 1))
                            E("vector", lambda v, col=col: v.tensor_sub(col(7), col(5), col(6)),
                              inc=("sV", 1))
                            E("vector", lambda v, col=col: v.tensor_single_scalar(
                                col(8), col(7), EPS, op=ALU.add), inc=("sV", 1))
                            tv_a = sch.tok("sV")
                            E("scalar", lambda s, col=col: s.activation(col(9), col(8), AF.Sqrt),
                              waits=[tv_a], inc=("sS", 1))
                            E("vector", lambda v, col=col: v.reciprocal(col(10), col(9)),
                              waits=[sch.tok("sS")], inc=("sV", 1))
                            E("vector", lambda v, col=col: v.scalar_tensor_tensor(
                                col(11), col(3), -1.0, col(10), op0=ALU.mult, op1=ALU.mult),
                              inc=("sV", 1))
                            tv_c = sch.tok("sV")
                            E("scalar", lambda s, xt=xt, col=col: s.activation(
                                xn_tm[:], xt, AF.Identity, bias=col(11), scale=col(10)),
                              waits=[tv_c, tj.get("xn_tr")], inc=("sS", 1))
                            txn = sch.tok("sS")
                            tj["colfree%d" % (m % 2)] = txn
                            for q4 in range(2):
                                half = q4 % 2
                                for ci in range(4):
                                    c = q4 * 4 + ci
                                    E("tensor", lambda t, c=c, half=half, ci=ci: t.transpose(
                                        ptf4[:, half * 512 + ci * 128:
                                             half * 512 + ci * 128 + 128],
                                        xn_tm[:, c * 128:(c + 1) * 128], ident_f[:]),
                                      waits=([txn, st["ptf"].get(half)] if ci == 0 else []),
                                      inc=("sT", 1))
                                ttr = sch.tok("sT")
                                psrc = bass.AP(ptf4, half * 512,
                                               [[1024, 128], [128, 4], [1, 128]])
                                dstf = bass.AP(xnT_f[ti % 2],
                                               (q4 * 4) * 512 + m * 128,
                                               [[4096, 128], [512, 4], [1, 128]])
                                t1 = E("vector", lambda v, dstf=dstf, psrc=psrc: v.tensor_copy(
                                    dstf, psrc), waits=[ttr], inc=("sV", 1))
                                dstb = bass.AP(xnT_b[ti % 2],
                                               (q4 * 4) * 512 + m * 128,
                                               [[4096, 128], [512, 4], [1, 128]])
                                srcf = dstf
                                E("scalar", lambda s, dstb=dstb, srcf=srcf: s.copy(dstb, srcf),
                                  waits=[t1], inc=("sS", 1))
                                st["ptf"][half] = t1
                            tj["xn_tr"] = sch.tok("sT")
                        tj["xnT_b"] = sch.tok("sS")
                        tj["xnT_f"] = sch.tok("sV")

                    def stMM(ti):
                        tj = tiles[ti]
                        tprev = tiles.get(ti - 1, {})
                        jseq = ti % NST2_SEQ
                        scan_toks = []
                        for c in range(8):
                            bk_rt = pg[(2 * c) % 4]
                            bk_it = pg[(2 * c + 1) % 4]
                            wmm = [tj["xnT_b"], st["pg"].get((2 * c) % 4),
                                   st["pg"].get((2 * c + 1) % 4)]
                            if c == 0:
                                wmm.append(tok_w)
                            for k in range(8):
                                E("tensor", lambda t, c=c, k=k, bk=bk_rt, ti=ti: t.matmul(
                                    bk[:], waT[:, k * D + c * 128: k * D + (c + 1) * 128],
                                    xnT_b[ti % 2][:, k * 512:(k + 1) * 512],
                                    start=(k == 0), stop=(k == 7)),
                                  waits=wmm if k == 0 else [], inc=("sT", 1))
                            for k in range(8):
                                E("tensor", lambda t, c=c, k=k, bk=bk_it, ti=ti: t.matmul(
                                    bk[:], wxT[:, k * D + c * 128: k * D + (c + 1) * 128],
                                    xnT_b[ti % 2][:, k * 512:(k + 1) * 512],
                                    start=(k == 0), stop=(k == 7)), inc=("sT", 1))
                            tmm = sch.tok("sT")
                            E("scalar", lambda s, c=c, bk=bk_rt: s.activation(
                                rt_tmp[:], bk[:], AF.Sigmoid, bias=ba_fm[:, c:c + 1]),
                              waits=[tmm, tprev.get("scan%d" % c)], inc=("sS", 1))
                            E("scalar", lambda s, c=c: s.activation(
                                a_fm[:, c * 512:(c + 1) * 512], rt_tmp[:], AF.Exp,
                                scale=coef_fm[:, c:c + 1]), inc=("sS", 1))
                            st["pg"][(2 * c) % 4] = sch.tok("sS")
                            E("scalar", lambda s, c=c, bk=bk_it: s.activation(
                                it_fm[:, c * 512:(c + 1) * 512], bk[:], AF.Sigmoid,
                                bias=bx_fm[:, c:c + 1]), inc=("sS", 1))
                            st["pg"][(2 * c + 1) % 4] = sch.tok("sS")
                            ta = sch.tok("sS")
                            E("scalar", lambda s, c=c: s.activation(
                                tsc[:], a_fm[:, c * 512:(c + 1) * 512], AF.Square),
                              waits=[ta, st.get("b_done")], inc=("sS", 1))
                            E("scalar", lambda s: s.activation(
                                tsc[:], tsc[:], AF.Sqrt, bias=1.0, scale=-1.0), inc=("sS", 1))
                            tb_ = sch.tok("sS")
                            E("vector", lambda v, c=c, ti=ti: v.tensor_mul(
                                pbuf[:], it_fm[:, c * 512:(c + 1) * 512],
                                xnT_f[ti % 2][:, c * 512:(c + 1) * 512]),
                              waits=[ta, tj["xnT_f"]], inc=("sV", 1))
                            E("vector", lambda v, c=c: v.tensor_mul(
                                b_fm[:, c * 512:(c + 1) * 512], tsc[:], pbuf[:]),
                              waits=[tb_], inc=("sV", 1))
                            st["b_done"] = sch.tok("sV")
                            if jseq == 0:
                                E("vector", lambda v, c=c, ti=ti: v.tensor_tensor_scan(
                                    xnT_f[ti % 2][:, c * 512:(c + 1) * 512],
                                    a_fm[:, c * 512:(c + 1) * 512],
                                    b_fm[:, c * 512:(c + 1) * 512],
                                    0.0, op0=ALU.mult, op1=ALU.add), inc=("sV", 1))
                            else:
                                E("vector", lambda v, c=c, ti=ti: v.tensor_tensor_scan(
                                    xnT_f[ti % 2][:, c * 512:(c + 1) * 512],
                                    a_fm[:, c * 512:(c + 1) * 512],
                                    b_fm[:, c * 512:(c + 1) * 512],
                                    carry[:, c:c + 1], op0=ALU.mult, op1=ALU.add),
                                  inc=("sV", 1))
                            E("vector", lambda v, c=c, ti=ti: v.tensor_copy(
                                carry[:, c:c + 1],
                                xnT_f[ti % 2][:, c * 512 + 511:c * 512 + 512]),
                              inc=("sV", 1))
                            tok = sch.tok("sV")
                            tj["scan%d" % c] = tok
                            scan_toks.append(tok)
                        tj["scans"] = scan_toks

                    def stY(ti):
                        if ti < 0:
                            return
                        tj = tiles[ti]
                        for c in range(8):
                            half = c % 2
                            for m in range(4):
                                cc = c * 512 + m * 128
                                E("tensor", lambda t, cc=cc, half=half, m=m, ti=ti: t.transpose(
                                    ptf4[:, half * 512 + m * 128: half * 512 + m * 128 + 128],
                                    xnT_f[ti % 2][:, cc:cc + 128], ident_f[:]),
                                  waits=([tj["scans"][c], st["ptf"].get(half)]
                                         if m == 0 else []),
                                  inc=("sT", 1))
                            ttr = sch.tok("sT")
                            psrc = bass.AP(ptf4, half * 512, [[1024, 128], [128, 4], [1, 128]])
                            dstx = bass.AP(xb[ti % 3], c * 128, [[4096, 128], [1024, 4], [1, 128]])
                            tok = E("vector", lambda v, psrc=psrc, dstx=dstx: v.scalar_tensor_tensor(
                                dstx, psrc, 1.0, dstx, op0=ALU.mult, op1=ALU.add),
                              waits=[ttr], inc=("sV", 1))
                            st["ptf"][half] = tok
                        tdone = sch.tok("sV")
                        for m in range(4):
                            b, t = gtile(4 * ti + m)
                            E("sync", lambda q, ti=ti, m=m, b=b, t=t: q.dma_start(
                                dst_ap[b, t:t + 128, :], xb[ti % 3][:, m * D:(m + 1) * D]),
                              waits=[tdone] if m == 0 else [], inc=("sO", 16))
                        tj["stored"] = sch.tok("sO")
                        store_toks_out.append(tj["stored"])
                        tj["bnfree"] = tdone

                    emit_load(0)
                    emit_load(1)
                    stLN(0)
                    for ti in range(NT2):
                        stMM(ti)
                        stY(ti - 1)
                        emit_load(ti + 2)
                        stLN(ti + 1)
                    stY(NT2 - 1)
                    sch.flush(blk)

        def entry_toks():
            return [sch.tok("sO"), sch.tok("sV"), sch.tok("sS"), sch.tok("sT"),
                    sch.tok("sLA"), sch.tok("sLB")]

        phases = os.environ.get("ATH_PHASES", "123")
        p1_stores = []
        if "1" in phases:
            ffn_phase("f1", x_in, xs1, lambda st: None, p1_stores, entry_toks())
        p2_stores = []
        if "2" in phases:
            griffin_phase(xs1 if "1" in phases else x_in, xs2,
                          p1_stores, p2_stores, entry_toks())
        if "3" in phases:
            src3 = xs2 if "2" in phases else (xs1 if "1" in phases else x_in)
            p3_stores = []
            ffn_phase("f3", src3, y_out,
                      (lambda st: p2_stores[-1]) if p2_stores else
                      ((lambda st: p1_stores[-1]) if p1_stores else (lambda st: None)),
                      p3_stores, entry_toks())

        with nc.Block("fin") as blk:
            E("sync", lambda q: q.nop(),
              waits=[sch.tok("sO"), sch.tok("sOA"), sch.tok("sOB"), sch.tok("sW"),
                     sch.tok("sLA"), sch.tok("sLB"), sch.tok("sC")])
            sch.flush(blk)

    return nc


_NC_CACHE = {}


def _get_nc(B_core, S_seq):
    key = (B_core, S_seq)
    if key not in _NC_CACHE:
        _NC_CACHE[key] = build_program(B_core, S_seq)
    return _NC_CACHE[key]


def kernel(L0_attn_x, L0_attn_Wa, L0_attn_Wx, L0_attn_ba, L0_attn_bx, L0_attn_Lam,
           L0_ffn_w1, L0_ffn_b1, L0_ffn_w2, L0_ffn_b2):
    x = np.ascontiguousarray(np.asarray(L0_attn_x, dtype=np.float32))
    B, S, _ = x.shape
    B_core = B // NCORES
    nc = _get_nc(B_core, S)
    common = dict(
        L0_attn_Wa=np.ascontiguousarray(np.asarray(L0_attn_Wa, dtype=np.float32)),
        L0_attn_Wx=np.ascontiguousarray(np.asarray(L0_attn_Wx, dtype=np.float32)),
        L0_attn_ba=np.ascontiguousarray(np.asarray(L0_attn_ba, dtype=np.float32)),
        L0_attn_bx=np.ascontiguousarray(np.asarray(L0_attn_bx, dtype=np.float32)),
        L0_attn_Lam=np.ascontiguousarray(np.asarray(L0_attn_Lam, dtype=np.float32)),
        L0_ffn_w1=np.ascontiguousarray(np.asarray(L0_ffn_w1, dtype=np.float32)),
        L0_ffn_b1=np.ascontiguousarray(np.asarray(L0_ffn_b1, dtype=np.float32)),
        L0_ffn_w2=np.ascontiguousarray(np.asarray(L0_ffn_w2, dtype=np.float32)),
        L0_ffn_b2=np.ascontiguousarray(np.asarray(L0_ffn_b2, dtype=np.float32)),
        ident_f32=np.eye(128, dtype=np.float32),
        ident_bf16=np.eye(128).astype(ml_dtypes.bfloat16),
    )
    in_maps = []
    for c in range(NCORES):
        m = dict(common)
        m["L0_attn_x"] = np.ascontiguousarray(x[c * B_core:(c + 1) * B_core])
        in_maps.append(m)
    res = run_bass_kernel_spmd(nc, in_maps, core_ids=list(range(NCORES)))
    return np.concatenate([np.asarray(r["y"], dtype=np.float32) for r in res.results],
                          axis=0)
